# version 18
# baseline (speedup 1.0000x reference)
import contextlib
import numpy as np
import concourse.bass as bass
import concourse.mybir as mybir
from concourse.bass_utils import run_bass_kernel_spmd

F32 = mybir.dt.float32
AF = mybir.ActivationFunctionType
ALU = mybir.AluOpType
AX = mybir.AxisListType

DM = 1024
DI = 2048
NS = 16
DT_M = DM // 128
DT_I = DI // 128
EPS = 1e-5

import os as _os
HALF = False
T = int(_os.environ.get("MK_T", "0")) or (2048 if HALF else 4096)
TC = 512
NCH = T // TC
NTT = TC // 128
XS = TC + 3


class P:
    def __init__(self, nc, es):
        self.nc = nc
        self.s = {}
        self.c = {}
        for name in ("qs", "qg", "ct", "cv", "cs", "qw0", "qw1", "qt0", "qt1", "qb0", "qb1", "qbs", "qr", "qv", "cp", "ck", "cq", "qwi0", "qwi1", "qx", "qi"):
            self.s[name] = es.enter_context(nc.semaphore(name))
            self.c[name] = 0

    def inc(self, instr, name, k=1):
        instr.then_inc(self.s[name], k)
        self.c[name] += k
        return self.c[name]

    def wait(self, eng, name, val=None):
        if val is None:
            val = self.c[name]
        if val > 0:
            eng.wait_ge(self.s[name], val)

    def edge(self, eng, name):
        self.wait(eng, name, self.c[name])


def build_nc():
    nc = bass.Bass("TRN2", target_bir_lowering=False)

    x_d = nc.dram_tensor("x", [T, DM], F32, kind="ExternalInput").ap()
    xh_d = nc.dram_tensor("x_halo3", [3, DM], F32, kind="ExternalInput").ap()
    hm_d = nc.dram_tensor("halo_mask", [128, 1], F32, kind="ExternalInput").ap()
    winT_d = nc.dram_tensor("w_in_t", [DM, 2 * DI], F32, kind="ExternalInput").ap()
    wxT_d = nc.dram_tensor("w_x_t", [DI, DI + 2 * NS], F32, kind="ExternalInput").ap()
    woT_d = nc.dram_tensor("w_out_t", [DI, DM], F32, kind="ExternalInput").ap()
    wfT_d = nc.dram_tensor("w_ffn_t", [DM, DM], F32, kind="ExternalInput").ap()
    cbixi_d = nc.dram_tensor("cb_in_xi", [128, DT_I], F32, kind="ExternalInput").ap()
    cbires_d = nc.dram_tensor("cb_in_res", [128, DT_I], F32, kind="ExternalInput").ap()
    cwconv_d = nc.dram_tensor("cw_conv", [128, DT_I * 4], F32, kind="ExternalInput").ap()
    cbconv_d = nc.dram_tensor("cb_conv", [128, DT_I], F32, kind="ExternalInput").ap()
    cbxd_d = nc.dram_tensor("cb_x_d", [128, DT_I], F32, kind="ExternalInput").ap()
    cbxbc_d = nc.dram_tensor("cb_x_bc", [32, 1], F32, kind="ExternalInput").ap()
    cbout_d = nc.dram_tensor("cb_out", [128, DT_M], F32, kind="ExternalInput").ap()
    cd_d = nc.dram_tensor("c_d", [128, DT_I], F32, kind="ExternalInput").ap()
    calog_d = nc.dram_tensor("c_a_log", [128, DT_I * NS], F32, kind="ExternalInput").ap()
    alpha_d = nc.dram_tensor("c_alpha", [128, 1], F32, kind="ExternalInput").ap()
    cbffn_d = nc.dram_tensor("cb_ffn", [128, DT_M], F32, kind="ExternalInput").ap()
    iden_d = nc.dram_tensor("iden", [128, 128], F32, kind="ExternalInput").ap()
    out_d = nc.dram_tensor("out", [T, DM], F32, kind="ExternalOutput").ap()
    xi2_d = nc.dram_tensor("xi2_s", [128, DT_I * T], F32, kind="Internal").ap()
    sres_d = nc.dram_tensor("sres_s", [128, DT_I * T], F32, kind="Internal").ap()
    v_d = nc.dram_tensor("v_s", [128, DT_I * T], F32, kind="Internal").ap()
    bcd_d = nc.dram_tensor("bc_s", [32, T], F32, kind="Internal").ap()

    es = contextlib.ExitStack()
    with es:
        p = P(nc, es)
        sb = lambda name, shape: es.enter_context(nc.sbuf_tensor(name, shape, F32))

        cbixi = sb("cbixi", [128, DT_I])
        cbires = sb("cbires", [128, DT_I])
        cwconv = sb("cwconv", [128, DT_I * 4])
        cbconv = sb("cbconv", [128, DT_I])
        cbxd = sb("cbxd", [128, DT_I])
        cbxbc = sb("cbxbc", [32, 1])
        cbout = sb("cbout", [128, DT_M])
        cdv = sb("cdv", [128, DT_I])
        ca = sb("ca", [128, DT_I * NS])
        alph = sb("alph", [128, 1])
        cbffn = sb("cbffn", [128, DT_M])
        iden = sb("iden_sb", [128, 128])
        hmask = sb("hmask", [128, 1])
        epsb = sb("epsb", [128, 1])
        ones_row = sb("ones_row", [1, 128])
        ones_col = sb("ones_col", [128, 1])
        mv_all = sb("mv_all", [128, 2 * (T // 128)])
        rstd_all = sb("rstd_all", [128, T // 128])
        hstate = sb("hstate", [128, DT_I * NS])
        xihalo = sb("xihalo", [128, DT_I * 3])

        pb = [es.enter_context(nc.psum_tensor(f"pb{i}", [128, 512], F32)) for i in range(8)]

        blk = es.enter_context(nc.Block())

        @blk.gpsimd
        def _(g):
            for dst, src in ((cbixi, cbixi_d), (cbires, cbires_d), (cwconv, cwconv_d),
                             (cbconv, cbconv_d), (cbxd, cbxd_d), (cbxbc, cbxbc_d),
                             (cbout, cbout_d), (cdv, cd_d), (ca, calog_d),
                             (alph, alpha_d), (cbffn, cbffn_d), (iden, iden_d),
                             (hmask, hm_d)):
                p.inc(g.dma_start(out=dst[:], in_=src[:]), "qg", 16)

        @blk.vector
        def _(v):
            p.wait(v, "qg")
            v.memset(ones_row[:], 1.0)
            v.memset(epsb[:], EPS)
            v.memset(ones_col[:], 1.0)
            v.memset(hstate[:], 0.0)
            p.inc(v.memset(xihalo[:], 0.0), "cv")

        @blk.scalar
        def _(s):
            p.wait(s, "qg")
            p.wait(s, "cv")
            li = None
            for t in range(DT_I):
                li = s.activation(out=ca[:, t * NS:(t + 1) * NS],
                                  in_=ca[:, t * NS:(t + 1) * NS], func=AF.Exp)
            p.inc(li, "cs")

        @blk.vector
        def _(v):
            p.wait(v, "cs")
            li = None
            for t in range(DT_I):
                li = v.tensor_scalar(out=ca[:, t * NS:(t + 1) * NS],
                                     in0=ca[:, t * NS:(t + 1) * NS],
                                     scalar1=-1.0, scalar2=None, op0=ALU.mult)
            p.inc(li, "cv")

        with contextlib.ExitStack() as es0:
            x0 = es0.enter_context(nc.sbuf_tensor("p0_x", [128, NTT * DM], F32))
            bns0 = es0.enter_context(nc.sbuf_tensor("p0_bns", [128, NTT * 12], F32))
            for j in range(NCH):
                @blk.sync
                def _(sy, j=j):
                    if j >= 1:
                        p.wait(sy, "cv", p.c["cv"])
                    for tt in range(NTT):
                        p.inc(sy.dma_start(out=x0[:, tt * DM:(tt + 1) * DM],
                                           in_=x_d[j * TC + tt * 128: j * TC + (tt + 1) * 128, :]),
                              "qs", 16)

                @blk.vector
                def _(v, j=j):
                    p.wait(v, "qs")
                    li = None
                    for tt in range(NTT):
                        v.bn_stats(out=bns0[:, tt * 12:tt * 12 + 6],
                                   in_=x0[:, tt * DM:tt * DM + 512])
                        li = v.bn_stats(out=bns0[:, tt * 12 + 6:tt * 12 + 12],
                                        in_=x0[:, tt * DM + 512:tt * DM + 1024])
                    p.inc(li, "cv")
                    p.edge(v, "cv")
                    li = None
                    for tt in range(NTT):
                        g = j * NTT + tt
                        li = v.bn_aggr(out=mv_all[:, 2 * g:2 * g + 2],
                                       in_=bns0[:, tt * 12:tt * 12 + 12])
                    p.inc(li, "cv")

                @blk.scalar
                def _(s, j=j):
                    p.wait(s, "cv")
                    li = None
                    for tt in range(NTT):
                        g = j * NTT + tt
                        li = s.activation(out=rstd_all[:, g:g + 1],
                                          in_=mv_all[:, 2 * g + 1:2 * g + 2],
                                          func=AF.Ln, bias=epsb[:, 0:1])
                    p.inc(li, "cs")
                    p.edge(s, "cs")
                    li = None
                    for tt in range(NTT):
                        g = j * NTT + tt
                        li = s.activation(out=rstd_all[:, g:g + 1], in_=rstd_all[:, g:g + 1],
                                          func=AF.Exp, scale=-0.5)
                    p.inc(li, "cs")

        cs_p0 = p.c["cs"]

        TP = 256
        NC2 = T // TP
        TC1 = TP
        XS1 = TC1 + 3
        NTT1 = TC1 // 128
        NO = (DI + 2 * NS + 127) // 128
        GRPS = [list(range(0, 12)), list(range(12, 17))]
        with contextlib.ExitStack() as es2:
            sb2 = lambda name, shape: es2.enter_context(nc.sbuf_tensor(name, shape, F32))
            wxb = [sb2(f"wx{i}", [128, 12 * 128]) for i in range(2)]
            bcb = [sb2(f"p2_bcb{i}", [128, 1, 2 * TP]) for i in range(2)]
            xi2 = sb2("p2_xi2", [128, DT_I * TP])
            sdel = [sb2(f"p2_del{i}", [128, DT_I * TP]) for i in range(2)]
            sdtx = sb2("p2_dtx", [128, DT_I * TP])
            sy_ = sb2("p2_y", [128, DT_I * TP])
            srs = sb2("p2_srs", [128, DT_I * TP])
            h_all = sb2("p2_h", [128, 8 * TP])
            da = sb2("p2_da", [128, DT_I * TP])
            sp12 = sb2("p2_sp", [128, 16 * TP])
            dz = sb2("p2_dz", [128, 4 * TP])
            bc = sb2("p2_bc", [32, TP])
            x1 = sb2("p1_x", [128, NTT1 * DM])
            bns = sb2("p1_bns", [128, 16])
            xnf = sb2("p1_xnf", [128, DT_M, TC1])
            xi = sb2("p1_xi", [128, DT_I, XS1])
            cvs2 = [sb2(f"p1_cv{i}", [128, 2 * TC1]) for i in range(2)]
            stg2 = [sb2(f"p1_s2{i}", [128, 2 * TC1]) for i in range(2)]
            wino = [sb2(f"p1_wo{i}", [128, 512]) for i in range(2)]

            @blk.sync
            def _(sy):
                p.wait(sy, "cv", p.c["cv"])
                p.wait(sy, "cs", p.c["cs"])
                p.inc(sy.dma_start(out=x1[:3, 0:DM], in_=xh_d[:]), "qs", 16)
            qs_halo = p.c["qs"]

            @blk.vector
            def _(v):
                p.wait(v, "qs", qs_halo)
                p.wait(v, "cs")
                v.bn_stats(out=bns[:3, 0:6], in_=x1[:3, 0:512])
                li = v.bn_stats(out=bns[:3, 6:12], in_=x1[:3, 512:1024])
                p.inc(li, "cp")
                p.edge(v, "cp")
                li = v.bn_aggr(out=bns[:3, 12:14], in_=bns[:3, 0:12])
                p.inc(li, "cp")
            cp_hs = p.c["cp"]

            @blk.scalar
            def _(s):
                p.wait(s, "cp", cp_hs)
                li = s.activation(out=bns[:3, 14:15], in_=bns[:3, 13:14], func=AF.Ln,
                                  bias=epsb[:3, 0:1])
                p.inc(li, "cq")
                p.edge(s, "cq")
                li = s.activation(out=bns[:3, 14:15], in_=bns[:3, 14:15], func=AF.Exp,
                                  scale=-0.5)
                p.inc(li, "cq")
            cq_hr = p.c["cq"]

            @blk.vector
            def _(v):
                p.wait(v, "cq", cq_hr)
                li = v.tensor_scalar(out=x1[:3, 0:DM], in0=x1[:3, 0:DM],
                                     scalar1=bns[:3, 12:13], scalar2=bns[:3, 14:15],
                                     op0=ALU.subtract, op1=ALU.mult)
                p.inc(li, "cp")
            cp_hx = p.c["cp"]

            @blk.tensor
            def _(t):
                p.wait(t, "cp", cp_hx)
                li = None
                for c_ in range(DT_M):
                    li = t.transpose(pb[6][:, c_ * 3:c_ * 3 + 3],
                                     x1[:3, c_ * 128:(c_ + 1) * 128], iden[:3, 0:3])
                p.inc(li, "ck")
            ck_htr = p.c["ck"]

            @blk.scalar
            def _(s):
                p.wait(s, "ck", ck_htr)
                p.inc(s.activation(out=xnf[:, 0, 0:DT_M * 3], in_=pb[6][:, 0:DT_M * 3],
                                   func=AF.Identity), "cq")
            cq_hev = p.c["cq"]

            qwib_h = (p.c["qwi0"], p.c["qwi1"])

            @blk.gpsimd
            def _(g):
                p.wait(g, "cq", cq_hev)
                for og in range(4):
                    for k in range(DT_M):
                        L = og * 8 + k
                        if L >= 2:
                            p.wait(g, "ck", ck_htr + L - 1)
                        p.inc(g.dma_start(out=wino[k % 2][:],
                                          in_=winT_d[k * 128:(k + 1) * 128,
                                                     og * 512:(og + 1) * 512]),
                              f"qwi{k % 2}", 16)

            @blk.tensor
            def _(t, qwib_h=qwib_h):
                for og in range(4):
                    if og >= 1:
                        p.wait(t, "cq", cq_hev + og)
                    for k in range(DT_M):
                        p.wait(t, f"qwi{k % 2}", qwib_h[k % 2] + 16 * (og * 4 + k // 2 + 1))
                        li = None
                        for oi in range(4):
                            li = t.matmul(pb[4 + oi][:, 0:3],
                                          wino[k % 2][:, oi * 128:(oi + 1) * 128],
                                          xnf[:, 0, k * 3:(k + 1) * 3],
                                          start=(k == 0), stop=(k == DT_M - 1))
                        p.inc(li, "ck")
            ck_hmm = p.c["ck"]

            @blk.scalar
            def _(s):
                for og in range(4):
                    p.wait(s, "ck", ck_htr + og * 8 + 8)
                    li = None
                    for oi in range(4):
                        o = og * 4 + oi
                        li = s.activation(out=xihalo[:, o * 3:(o + 1) * 3],
                                          in_=pb[4 + oi][:, 0:3], func=AF.Identity,
                                          bias=cbixi[:, o:o + 1])
                    p.inc(li, "cq")
            cq_hx2 = p.c["cq"]

            @blk.vector
            def _(v):
                p.wait(v, "cq", cq_hx2)
                li = v.tensor_scalar(out=xihalo[:], in0=xihalo[:], scalar1=hmask[:, 0:1],
                                     scalar2=None, op0=ALU.mult)
                p.inc(li, "cp")
                p.edge(v, "cp")
                li = None
                for t_ in range(DT_I):
                    li = v.tensor_copy(out=xi[:, t_, 0:3],
                                       in_=xihalo[:, t_ * 3:(t_ + 1) * 3])
                p.inc(li, "cp")

            S = {"qs_x1": {}, "qwib": {}, "cpLN": {}, "cp_cvb": {}, "ck_trb": {},
                 "cq_xnfb": {}, "cq_ogend": {-1: p.c["cq"]}, "silu_base": {},
                 "res_ev": {}, "store_qtb": {}, "cp_convend": {-1: p.c["cp"]},
                 "ck_mmend": {-1: ck_hmm}, "ck_trend": {}}

            def emit_x1_load(c):
                @blk.gpsimd
                def _(sy, c=c):
                    if c >= 1:
                        p.wait(sy, "ck", S["ck_trend"][c - 1])
                    else:
                        p.wait(sy, "ck", ck_htr)
                    for tt in range(NTT1):
                        p.inc(sy.dma_start(
                            out=x1[:, tt * DM:(tt + 1) * DM],
                            in_=x_d[c * TC1 + tt * 128: c * TC1 + (tt + 1) * 128, :]),
                              "qx", 16)
                if c in S["qs_x1"]:
                    assert S["qs_x1"][c] == p.c["qx"], (c, S["qs_x1"][c], p.c["qx"])
                S["qs_x1"][c] = p.c["qx"]

            def emit_ln(c, qs_pred=None):
                if qs_pred is not None:
                    S["qs_x1"][c] = qs_pred
                @blk.vector
                def _(v, c=c):
                    p.wait(v, "qx", S["qs_x1"][c])
                    if c == 0:
                        p.wait(v, "cs", cs_p0)
                    li = None
                    for tt in range(NTT1):
                        g = c * NTT1 + tt
                        li = v.tensor_scalar(out=x1[:, tt * DM:(tt + 1) * DM],
                                             in0=x1[:, tt * DM:(tt + 1) * DM],
                                             scalar1=mv_all[:, 2 * g:2 * g + 1],
                                             scalar2=rstd_all[:, g:g + 1],
                                             op0=ALU.subtract, op1=ALU.mult)
                    p.inc(li, "cp")
                if c in S["cpLN"]:
                    assert S["cpLN"][c] == p.c["cp"], (c, S["cpLN"][c], p.c["cp"])
                S["cpLN"][c] = p.c["cp"]

            def emit_winloads(c, ckT):
                S["qwib"][c] = (p.c["qwi0"], p.c["qwi1"])
                mmb = ckT + 4
                @blk.gpsimd
                def _(g, c=c, mmb=mmb):
                    for og in range(8):
                        for k in range(DT_M):
                            L = og * 8 + k
                            if L < 2:
                                p.wait(g, "ck", S["ck_mmend"][c - 1])
                            else:
                                og_p, k_p = (og, k - 2) if k >= 2 else (og - 1, k + 6)
                                p.wait(g, "ck", mmb + 8 * og_p + k_p + 1)
                            p.inc(g.dma_start(out=wino[k % 2][:],
                                              in_=winT_d[k * 128:(k + 1) * 128,
                                                         og * 512:(og + 1) * 512]),
                                  f"qwi{k % 2}", 16)

            def emit_p1_tensor(c, ckT):
                assert p.c["ck"] == ckT, (p.c["ck"], ckT)
                S["ck_trb"][c] = ckT
                S["ck_trend"][c] = ckT + 4
                @blk.tensor
                def _(t, c=c, ckT=ckT):
                    p.wait(t, "cp", S["cpLN"][c])
                    for grp in range(4):
                        if grp < 2:
                            p.wait(t, "cq", S["cq_ogend"][c - 1])
                        else:
                            p.wait(t, "cq", S["cq_xnfb"][c] + grp - 1)
                        bank = pb[6 + grp % 2]
                        li = None
                        for i in range(4):
                            idx = grp * 4 + i
                            tt, c_ = idx // 8, idx % 8
                            li = t.transpose(bank[:, i * 128:(i + 1) * 128],
                                             x1[:, tt * DM + c_ * 128: tt * DM + (c_ + 1) * 128],
                                             iden[:])
                        p.inc(li, "ck")
                    for og in range(8):
                        for k in range(DT_M):
                            p.wait(t, f"qwi{k % 2}",
                                   S["qwib"][c][k % 2] + 16 * (og * 4 + k // 2 + 1))
                            if og == 0 and k == 0:
                                p.wait(t, "cq", S["cq_xnfb"][c] + 4)
                            if k == 0 and og >= 1:
                                p.wait(t, "cq", S["cq_xnfb"][c] + 4 + 4 * og)
                            li = None
                            for oi in range(4):
                                li = t.matmul(
                                    pb[6 + oi // 2][:, (oi % 2) * TC1:(oi % 2 + 1) * TC1],
                                    wino[k % 2][:, oi * 128:(oi + 1) * 128],
                                    xnf[:, k, :],
                                    start=(k == 0 and oi % 2 == 0),
                                    stop=(k == DT_M - 1 and oi % 2 == 1))
                            p.inc(li, "ck")
                S["ck_mmend"][c] = p.c["ck"]

            def emit_p1_evacs(c, ck_trb):
                cqb = p.c["cq"]
                if c in S["cq_xnfb"]:
                    assert S["cq_xnfb"][c] == cqb, (c, S["cq_xnfb"][c], cqb)
                S["cq_xnfb"][c] = cqb
                mmb = ck_trb + 4
                qtb = (p.c["qt0"], p.c["qt1"])
                S["store_qtb"][c] = qtb
                @blk.scalar
                def _(s, c=c, cqb=cqb, ck_trb=ck_trb, mmb=mmb, qtb=qtb):
                    for grp in range(4):
                        p.wait(s, "ck", ck_trb + grp + 1)
                        if grp == 0:
                            p.wait(s, "ck", S["ck_mmend"][c - 1])
                        li = s.activation(
                            out=xnf[:, (grp % 2) * 4:(grp % 2) * 4 + 4,
                                    (grp // 2) * 128:(grp // 2 + 1) * 128],
                            in_=pb[6 + grp % 2][:, 0:512], func=AF.Identity)
                        p.inc(li, "cq")
                    for og in range(8):
                        p.wait(s, "ck", mmb + 8 * og + 8)
                        if og == 0:
                            p.wait(s, "cp", S["cp_convend"][c - 1])
                        for oi in range(4):
                            o = og * 4 + oi
                            bank = pb[6 + oi // 2]
                            half = (oi % 2) * TC1
                            if o < DT_I:
                                li = s.activation(out=xi[:, o, 3:3 + TC1],
                                                  in_=bank[:, half:half + TC1],
                                                  func=AF.Identity,
                                                  bias=cbixi[:, o:o + 1])
                            else:
                                oo = o - DT_I
                                pr = oo // 2
                                par = pr % 2
                                if oo % 2 == 0:
                                    if pr >= 2:
                                        p.wait(s, f"qt{par}",
                                               qtb[par] + 16 * (2 * (pr // 2)))
                                    elif c >= 1:
                                        p.wait(s, f"qt{par}", qtb[par])
                                li = s.activation(out=stg2[par][:, (oo % 2) * TC1:(oo % 2 + 1) * TC1],
                                                  in_=bank[:, half:half + TC1],
                                                  func=AF.Silu,
                                                  bias=cbires[:, oo:oo + 1])
                            p.inc(li, "cq")
                            if o >= DT_I and oo % 2 == 1:
                                p.edge(s, "cq")
                                for h_ in range(2):
                                    o2 = oo - 1 + h_
                                    p.inc(s.dma_start(
                                        out=sres_d[:, o2 * T + c * TC1: o2 * T + c * TC1 + TC1],
                                        in_=stg2[par][:, h_ * TC1:(h_ + 1) * TC1]),
                                          f"qt{par}", 16)
                S["cq_ogend"][c] = p.c["cq"]

            def emit_silu(c):
                sb_ = S["silu_base"][c]
                assert p.c["cq"] == sb_, (p.c["cq"], sb_, c)
                cpb = S["cp_cvb"][c]
                @blk.scalar
                def _(s, c=c, cpb=cpb):
                    for t_ in range(DT_I):
                        cg = t_ // 2
                        par = cg % 2
                        p.wait(s, "cp", cpb + 5 * cg + 4)
                        if t_ % 2 == 0:
                            if cg >= 2:
                                p.wait(s, f"qt{par}",
                                       S["store_qtb"][c][par] + 16 * (8 + 2 * (cg // 2)))
                            else:
                                p.wait(s, f"qt{par}", S["store_qtb"][c][par] + 16 * 8)
                        li = s.activation(out=stg2[par][:, (t_ % 2) * TC1:(t_ % 2 + 1) * TC1],
                                          in_=cvs2[par][:, (t_ % 2) * TC1:(t_ % 2 + 1) * TC1],
                                          func=AF.Silu, bias=cbconv[:, t_:t_ + 1])
                        p.inc(li, "cq")
                        p.edge(s, "cq")
                        p.inc(s.dma_start(
                            out=xi2_d[:, t_ * T + c * TC1: t_ * T + c * TC1 + TC1],
                            in_=stg2[par][:, (t_ % 2) * TC1:(t_ % 2 + 1) * TC1]),
                              f"qt{par}", 16)

            def emit_conv(c):
                cpb = p.c["cp"]
                S["cp_cvb"][c] = cpb
                sb_c = p.c["cq"]
                S["silu_base"][c] = sb_c
                cqx = S["cq_xnfb"][c]
                sbp = S["silu_base"].get(c - 1)
                @blk.vector
                def _(v, c=c, cqx=cqx, sbp=sbp, sb_c=sb_c):
                    for cg in range(8):
                        p.wait(v, "cq", cqx + 4 + 4 * (cg // 2) + 2 * (cg % 2) + 2)
                        if cg < 2:
                            if c >= 1:
                                p.wait(v, "cq", sbp + 14 + 2 * (cg % 2))
                        else:
                            p.wait(v, "cq", sb_c + 2 * cg - 2)
                        li = None
                        for t_ in (2 * cg, 2 * cg + 1):
                            li = v.tensor_scalar(
                                out=cvs2[cg % 2][:, (t_ % 2) * TC1:(t_ % 2 + 1) * TC1],
                                in0=xi[:, t_, 0:TC1],
                                scalar1=cwconv[:, t_ * 4:t_ * 4 + 1],
                                scalar2=None, op0=ALU.mult)
                        p.inc(li, "cp")
                        for k in (1, 2, 3):
                            p.edge(v, "cp")
                            li = None
                            for t_ in (2 * cg, 2 * cg + 1):
                                li = v.scalar_tensor_tensor(
                                    out=cvs2[cg % 2][:, (t_ % 2) * TC1:(t_ % 2 + 1) * TC1],
                                    in0=xi[:, t_, k:k + TC1],
                                    scalar=cwconv[:, t_ * 4 + k:t_ * 4 + k + 1],
                                    in1=cvs2[cg % 2][:, (t_ % 2) * TC1:(t_ % 2 + 1) * TC1],
                                    op0=ALU.mult, op1=ALU.add)
                            p.inc(li, "cp")
                        p.edge(v, "cp")
                        li = v.tensor_copy(out=xi[:, 2 * cg:2 * cg + 2, 0:3],
                                           in_=xi[:, 2 * cg:2 * cg + 2, TC1:TC1 + 3])
                        p.inc(li, "cp")
                S["cp_convend"][c] = p.c["cp"]

            emit_x1_load(0)
            emit_winloads(0, p.c["ck"])
            emit_ln(0)
            S["cq_xnfb"][0] = p.c["cq"]
            emit_p1_tensor(0, p.c["ck"])
            emit_p1_evacs(0, S["ck_trb"][0])
            emit_conv(0)
            emit_silu(0)

            def da_off(jj, n):
                if n == 0:
                    return -16
                return 16 * (n - 1) + (0 if jj == NC2 - 1 or n < 12
                                       else 3 if n < 14 else 6)

            def emit_dA_part(jj, ns, csA, cvb_jj, qbs_jj, prevcv):
                @blk.scalar
                def _(s, jj=jj, ns=ns, csA=csA, cvb_jj=cvb_jj, qbs_jj=qbs_jj,
                      prevcv=prevcv):
                    cvs0 = cvb_jj + 5
                    if ns[0] == 0:
                        p.wait(s, "cv", cvb_jj + 4)
                        p.wait(s, "qbs", qbs_jj)
                        p.edge(s, "cs")
                    for n in ns:
                        if n >= 2:
                            p.wait(s, "cv", cvs0 + 16 * (n - 1))
                        elif prevcv is not None:
                            p.wait(s, "cv", prevcv + 5 + 16 * (15 + n))
                        p.inc(s.dma_start(
                            out=bcb[n % 2][:, 0, 0:TP],
                            in_=bcd_d[n, jj * TP:(jj + 1) * TP].partition_broadcast(128)),
                            f"qb{n % 2}", 16)
                        p.inc(s.dma_start(
                            out=bcb[n % 2][:, 0, TP:2 * TP],
                            in_=bcd_d[NS + n, jj * TP:(jj + 1) * TP].partition_broadcast(128)),
                            f"qb{n % 2}", 16)
                        for t_i in range(DT_I):
                            if n > 0:
                                p.wait(s, "cs", csA + da_off(jj, n - 1) + t_i + 1)
                                p.wait(s, "cv", cvs0 + 16 * (n - 1) + 4 * (t_i // 4) + 2)
                            elif prevcv is not None:
                                p.wait(s, "cv", prevcv + 5 + 16 * 15 + 4 * (t_i // 4) + 2)
                            li = s.activation(out=da[:, t_i * TP:(t_i + 1) * TP],
                                              in_=sdel[jj % 2][:, t_i * TP:(t_i + 1) * TP],
                                              func=AF.Exp,
                                              scale=ca[:, t_i * NS + n:t_i * NS + n + 1])
                            p.inc(li, "cs")

            prev = None
            prev_qbs = 0
            prevcv_prev = None
            for j in range(NC2):
                ctb = p.c["ct"]
                csb = p.c["cs"]
                cvb = p.c["cv"]
                ckT = p.c["ck"]
                cqT = p.c["cq"]

                if j + 1 < NC2:
                    emit_x1_load(j + 1)

                @blk.gpsimd
                def _(sy, j=j, prev=prev):
                    if j == 0:
                        p.wait(sy, "cv", p.c["cv"])
                        p.wait(sy, "cs", p.c["cs"])
                        p.wait(sy, "ct", p.c["ct"])
                    else:
                        (pq, pct, pcs, pcv) = prev
                        p.wait(sy, "cv", pcv + 5)
                        p.wait(sy, "ct", pct + 32)
                    for par in range(2):
                        p.wait(sy, f"qt{par}", S["store_qtb"][j][par] + 256)
                    for t_i in range(DT_I):
                        p.inc(sy.dma_start(out=xi2[:, t_i * TP:(t_i + 1) * TP],
                                           in_=xi2_d[:, t_i * T + j * TP: t_i * T + j * TP + TP]),
                              "qi", 16)
                qs_xi2 = p.c["qi"]

                qwb = (p.c["qw0"], p.c["qw1"])

                @blk.sync
                def _(sy, j=j, ctb=ctb):
                    for g in range(2):
                        c0 = 0 if g == 0 else 12 * 128
                        cw = 12 * 128 if g == 0 else (DI + 2 * NS - 12 * 128)
                        for k in range(DT_I):
                            p.wait(sy, "ct", ctb + 16 * g + k - 1)
                            p.inc(sy.dma_start(out=wxb[k % 2][:, 0:cw],
                                               in_=wxT_d[k * 128:(k + 1) * 128, c0:c0 + cw]),
                                  f"qw{k % 2}", 16)

                @blk.sync
                def _(sy, j=j, csb=csb):
                    p.wait(sy, "cs", csb + (214 if j >= 1 else 6))
                    p.inc(sy.dma_start(out=bcd_d[:, j * TP:(j + 1) * TP], in_=bc[:, 0:TP]),
                          "qbs", 16)
                qbs_j = p.c["qbs"]

                @blk.sync
                def _(sy, j=j, prev=prev):
                    if prev is not None:
                        p.wait(sy, "cv", prev[3] + 262)
                    for par in range(2):
                        p.wait(sy, f"qt{par}", S["store_qtb"][j][par] + 128)
                    for t_i in range(DT_I):
                        p.inc(sy.dma_start(out=srs[:, t_i * TP:(t_i + 1) * TP],
                                           in_=sres_d[:, t_i * T + j * TP: t_i * T + j * TP + TP]),
                              "qr", 16)
                    if prev is not None:
                        jp = j - 1
                        for t_i in range(DT_I):
                            p.inc(sy.dma_start(
                                out=v_d[:, t_i * T + jp * TP: t_i * T + jp * TP + TP],
                                in_=sy_[:, t_i * TP:(t_i + 1) * TP]), "qv", 16)
                qt0_sres = p.c["qr"]
                qt1_vst = p.c["qv"]

                if j + 1 < NC2:
                    emit_winloads(j + 1, ckT)



                daB = csb
                if prev is not None:
                    emit_dA_part(j - 1, list(range(1, 8)), daB, prev[3],
                                 prev_qbs, prevcv_prev)
                if j >= 1:
                    emit_silu(j)
                if prev is not None:
                    emit_dA_part(j - 1, [8, 9, 10, 11], daB, prev[3],
                                 prev_qbs, prevcv_prev)

                def emit_sp(g, j=j, prev=prev, ctb=ctb, csb=csb, cvb=cvb,
                            pqbs=prev_qbs):
                    @blk.scalar
                    def _(s, g=g):
                        grp = GRPS[g]
                        if g == 0:
                            p.wait(s, "cs", csb)
                            if prev is not None:
                                p.wait(s, "cv", prev[3] + 3)

                        p.wait(s, "ct", ctb + 16 * (g + 1))
                        dts = [(bi + 12 * g, o) for bi, o in enumerate(grp) if o < DI // 128]
                        li = None
                        for sl, o in dts:
                            lb = sl - 12 * g
                            bank = pb[lb // 2]
                            half = (lb % 2) * TP
                            li = s.activation(out=sp12[:, sl * TP:(sl + 1) * TP],
                                              in_=bank[:, half:half + TP],
                                              func=AF.Abs, bias=cbxd[:, o:o + 1])
                        for sl, o in dts:
                            lb = sl - 12 * g
                            bank = pb[lb // 2]
                            half = (lb % 2) * TP
                            li = s.activation(out=sdel[j % 2][:, o * TP:(o + 1) * TP],
                                              in_=bank[:, half:half + TP],
                                              func=AF.Relu, bias=cbxd[:, o:o + 1])
                        p.inc(li, "cs")
                        p.edge(s, "cs")
                        li = None
                        for sl, o in dts:
                            li = s.activation(out=sp12[:, sl * TP:(sl + 1) * TP],
                                              in_=sp12[:, sl * TP:(sl + 1) * TP],
                                              func=AF.Exp, scale=-1.0)
                        p.inc(li, "cs")
                        p.edge(s, "cs")
                        li = None
                        for sl, o in dts:
                            li = s.activation(out=sp12[:, sl * TP:(sl + 1) * TP],
                                              in_=sp12[:, sl * TP:(sl + 1) * TP],
                                              func=AF.Ln, bias=1.0)
                        if g == 1:
                            p.wait(s, "qbs", pqbs)
                            li = s.activation(out=bc[:, 0:TP], in_=pb[2][:32, 0:TP],
                                              func=AF.Identity, bias=cbxbc[:, 0:1])
                        p.inc(li, "cs")

                emit_sp(0)
                if prev is not None:
                    emit_dA_part(j - 1, [12, 13], daB, prev[3], prev_qbs,
                                 prevcv_prev)
                emit_sp(1)
                if prev is not None:
                    emit_dA_part(j - 1, [14, 15], daB, prev[3], prev_qbs,
                                 prevcv_prev)
                qb_j = (p.c["qb0"], p.c["qb1"])
                emit_dA_part(j, [0], csb + (22 if j == 0 else 262), cvb, qbs_j,
                             prev[3] if prev is not None else None)



                if j + 1 < NC2:
                    emit_p1_evacs(j + 1, ckT)

                @blk.tensor
                def _(t, j=j, prev=prev, ctb=ctb, csb=csb, cvb=cvb, qwb=qwb,
                      qs_xi2=qs_xi2):
                    for g, grp in enumerate(GRPS):
                        if g == 0:
                            p.wait(t, "qi", qs_xi2)
                            if j == 0:
                                p.wait(t, "cq", cq_hx2)
                            if prev is not None:
                                p.wait(t, "cs", prev[2] + (212 if j >= 2 else 4))
                        else:
                            p.wait(t, "cs", csb + (177 if j >= 1 else 1))
                        for k in range(DT_I):
                            p.wait(t, f"qw{k % 2}",
                                   qwb[k % 2] + 16 * (8 * g + k // 2 + 1))
                            li = None
                            for bi, o in enumerate(grp):
                                w = 32 if o == NO - 1 else 128
                                lc = bi * 128
                                bank = pb[bi // 2]
                                half = (bi % 2) * TP
                                dst = (bank[:, half:half + TP] if w == 128
                                       else bank[:32, half:half + TP])
                                st_ = (k == 0 and bi % 2 == 0)
                                sp_ = (k == DT_I - 1 and (bi % 2 == 1
                                                          or bi == len(grp) - 1))
                                li = t.matmul(dst, wxb[k % 2][:, lc:lc + w],
                                              xi2[:, k * TP:(k + 1) * TP],
                                              start=st_, stop=sp_)
                            p.inc(li, "ct")
                assert p.c["ct"] == ctb + 32

                if j + 1 < NC2:
                    if (j + 1) not in S["cpLN"]:
                        S["cpLN"][j + 1] = p.c["cp"] + 1
                    emit_p1_tensor(j + 1, ckT)

                daB2 = csb + (22 if j == 0 else 262)

                def emit_scan_n(v, j, n, qb_j, daB2):
                    p.wait(v, f"qb{n % 2}", qb_j[n % 2] + 32 * (n // 2 + 1))
                    bap_b = bcb[n % 2][:, 0:1, 0:TP].broadcast_to([128, 4, TP])
                    bap_c = bcb[n % 2][:, 0:1, TP:2 * TP].broadcast_to([128, 4, TP])
                    for tp_ in range(0, DT_I, 4):
                        p.wait(v, "cs", daB2 + da_off(j, n) + tp_ + 4)
                        li = v.tensor_tensor(out=dz[:],
                                             in0=sdtx[:, tp_ * TP:(tp_ + 4) * TP],
                                             in1=bap_b, op=ALU.mult)
                        p.inc(li, "cv")
                        p.edge(v, "cv")
                        li = None
                        for t_i in range(tp_, tp_ + 4):
                            li = v.tensor_tensor_scan(
                                out=h_all[:, (t_i % 8) * TP:(t_i % 8 + 1) * TP],
                                data0=da[:, t_i * TP:(t_i + 1) * TP],
                                data1=dz[:, (t_i - tp_) * TP:(t_i - tp_ + 1) * TP],
                                initial=hstate[:, t_i * NS + n:t_i * NS + n + 1],
                                op0=ALU.mult, op1=ALU.add)
                        p.inc(li, "cv")
                        p.edge(v, "cv")
                        li = v.tensor_tensor(out=dz[:],
                                             in0=h_all[:, (tp_ % 8) * TP:(tp_ % 8 + 4) * TP],
                                             in1=bap_c, op=ALU.mult)
                        p.inc(li, "cv")
                        p.edge(v, "cv")
                        li = v.tensor_tensor(out=sy_[:, tp_ * TP:(tp_ + 4) * TP],
                                             in0=sy_[:, tp_ * TP:(tp_ + 4) * TP],
                                             in1=dz[:], op=ALU.add)
                        if tp_ % 8 == 4:
                            base = tp_ - 4
                            li = v.tensor_copy(
                                out=hstate[:, base * NS + n:(tp_ + 4) * NS:NS],
                                in_=h_all[:, TP - 1::TP])
                        p.inc(li, "cv")
                        p.edge(v, "cv")

                if j == 0 and NC2 >= 2:
                    emit_ln(1)

                @blk.vector
                def _(v, j=j, prev=prev, ctb=ctb, csb=csb, cvb=cvb, qb_j=qb_j,
                      qt0_sres=qt0_sres, qt1_vst=qt1_vst, daB2=daB2):
                    p.edge(v, "cv")
                    for g, grp in enumerate(GRPS):
                        spb = csb + (176 if j >= 1 else 0)
                        p.wait(v, "cs", spb + (3 if g == 0 else
                                               (38 if j >= 1 else 6)))
                        dts = [(bi + 12 * g, o) for bi, o in enumerate(grp) if o < DI // 128]
                        li = None
                        for sl, o in dts:
                            li = v.tensor_tensor(out=sdel[j % 2][:, o * TP:(o + 1) * TP],
                                                 in0=sdel[j % 2][:, o * TP:(o + 1) * TP],
                                                 in1=sp12[:, sl * TP:(sl + 1) * TP],
                                                 op=ALU.add)
                        p.inc(li, "cv")
                        p.edge(v, "cv")
                        li = None
                        for sl, o in dts:
                            li = v.tensor_tensor(out=sdtx[:, o * TP:(o + 1) * TP],
                                                 in0=sdel[j % 2][:, o * TP:(o + 1) * TP],
                                                 in1=xi2[:, o * TP:(o + 1) * TP],
                                                 op=ALU.mult)
                        p.inc(li, "cv")
                    p.wait(v, "qv", qt1_vst)
                    p.edge(v, "cv")
                    li = None
                    for t_i in range(DT_I):
                        li = v.tensor_scalar(out=sy_[:, t_i * TP:(t_i + 1) * TP],
                                             in0=xi2[:, t_i * TP:(t_i + 1) * TP],
                                             scalar1=cdv[:, t_i:t_i + 1],
                                             scalar2=None, op0=ALU.mult)
                    p.inc(li, "cv")
                    for n in range(8):
                        emit_scan_n(v, j, n, qb_j, daB2)

                if j + 1 < NC2:
                    emit_conv(j + 1)
                if j + 2 < NC2:
                    emit_ln(j + 2, qs_pred=p.c["qx"] + 16 * NTT1)

                @blk.vector
                def _(v, j=j, qb_j=qb_j, daB2=daB2, qt0_sres=qt0_sres):
                    for n in range(8, NS):
                        emit_scan_n(v, j, n, qb_j, daB2)
                    p.wait(v, "qr", qt0_sres)
                    p.edge(v, "cv")
                    li = None
                    for t_i in range(DT_I):
                        li = v.tensor_tensor(out=sy_[:, t_i * TP:(t_i + 1) * TP],
                                             in0=sy_[:, t_i * TP:(t_i + 1) * TP],
                                             in1=srs[:, t_i * TP:(t_i + 1) * TP], op=ALU.mult)
                    p.inc(li, "cv")
                assert p.c["cv"] == cvb + 262, (p.c["cv"], cvb)

                prev_qbs = qbs_j
                prevcv_prev = prev[3] if prev is not None else None
                prev = (qs_xi2, ctb, csb, cvb)
                if j == NC2 - 1:
                    emit_dA_part(j, list(range(1, NS)), csb + (22 if j == 0 else 262),
                                 cvb, qbs_j, prevcv_prev)

                    @blk.sync
                    def _(sy, j=j, cvb=cvb):
                        p.wait(sy, "cv", cvb + 262)
                        for t_i in range(DT_I):
                            p.inc(sy.dma_start(
                                out=v_d[:, t_i * T + j * TP: t_i * T + j * TP + TP],
                                in_=sy_[:, t_i * TP:(t_i + 1) * TP]), "qv", 16)

        with contextlib.ExitStack() as es3:
            sb3 = lambda name, shape: es3.enter_context(nc.sbuf_tensor(name, shape, F32))
            wob = [sb3(f"wo{i}", [128, DM]) for i in range(2)]
            wt = [sb3(f"wt{k}", [128, DM]) for k in range(DT_M)]
            vv = sb3("p3_v", [128, DT_I * TC])
            x3 = sb3("p3_x", [128, NTT * DM])
            xnf3 = sb3("p3_xnf", [128, DT_M * TC])
            ss = sb3("p3_s", [128, DT_M * TC])
            sg = sb3("p3_g", [128, DT_M * TC])
            thr = sb3("p3_thr", [128, 2])
            red8 = sb3("p3_red", [128, DT_M + 2])

            @blk.sync
            def _(sy):
                p.wait(sy, "qs", p.c["qs"])
                p.wait(sy, "qt0", p.c["qt0"])
                p.wait(sy, "qt1", p.c["qt1"])
                p.wait(sy, "qv", p.c["qv"])
                p.wait(sy, "qr", p.c["qr"])
                p.wait(sy, "qb0", p.c["qb0"])
                p.wait(sy, "qb1", p.c["qb1"])
                p.wait(sy, "cv", p.c["cv"])
                p.wait(sy, "cs", p.c["cs"])
                p.wait(sy, "ct", p.c["ct"])
                p.wait(sy, "cp", p.c["cp"])
                p.wait(sy, "cq", p.c["cq"])
                p.wait(sy, "ck", p.c["ck"])
                p.wait(sy, "qwi0", p.c["qwi0"])
                p.wait(sy, "qwi1", p.c["qwi1"])
                p.wait(sy, "qx", p.c["qx"])
                p.wait(sy, "qi", p.c["qi"])
                for k in range(DT_M):
                    p.inc(sy.dma_start(out=wt[k][:], in_=wfT_d[k * 128:(k + 1) * 128, :]),
                          "qs", 16)
            qs_wf = p.c["qs"]

            @blk.vector
            def _(v):
                p.wait(v, "qs", qs_wf)
                li = None
                for k in range(DT_M):
                    li = v.tensor_reduce(out=red8[:, k:k + 1], in_=wt[k][:], axis=AX.X,
                                         op=ALU.add, apply_absolute_value=True)
                p.inc(li, "cv")
                p.edge(v, "cv")
                li = v.tensor_reduce(out=red8[:, DT_M:DT_M + 1], in_=red8[:, 0:DT_M], axis=AX.X,
                                     op=ALU.add)
                p.inc(li, "cv")
            cv_red = p.c["cv"]

            @blk.tensor
            def _(t):
                p.wait(t, "cv", cv_red)
                li = t.matmul(pb[0][:1, 0:1], red8[:, DT_M:DT_M + 1], ones_col[:],
                              start=True, stop=True)
                p.inc(li, "ct")
            ct_tot = p.c["ct"]

            @blk.scalar
            def _(s):
                p.wait(s, "ct", ct_tot)
                li = s.activation(out=thr[:1, 0:1], in_=pb[0][:1, 0:1], func=AF.Copy,
                                  scale=0.7 / float(DM * DM))
                p.inc(li, "cs")
            cs_thr = p.c["cs"]

            @blk.tensor
            def _(t):
                p.wait(t, "cs", cs_thr)
                li = t.matmul(pb[1][:, 0:1], ones_row[:1, :], thr[:1, 0:1], start=True, stop=True)
                p.inc(li, "ct")
            ct_thr = p.c["ct"]

            @blk.vector
            def _(v):
                p.wait(v, "ct", ct_thr)
                li = v.tensor_copy(out=thr[:, 0:1], in_=pb[1][:, 0:1])
                p.inc(li, "cv")
                p.edge(v, "cv")
                li = v.tensor_scalar(out=thr[:, 1:2], in0=thr[:, 0:1], scalar1=-1.0,
                                     scalar2=None, op0=ALU.mult)
                p.inc(li, "cv")
                p.edge(v, "cv")
                for kh in range(2):
                    ks = range(kh * 4, kh * 4 + 4)
                    li = None
                    for k in ks:
                        li = v.tensor_scalar(out=sg[:, (k % 4) * DM:(k % 4 + 1) * DM],
                                             in0=wt[k][:], scalar1=thr[:, 0:1],
                                             scalar2=None, op0=ALU.is_gt)
                    for k in ks:
                        li = v.tensor_scalar(out=ss[:, (k % 4) * DM:(k % 4 + 1) * DM],
                                             in0=wt[k][:], scalar1=thr[:, 1:2],
                                             scalar2=None, op0=ALU.is_lt)
                    p.inc(li, "cv")
                    p.edge(v, "cv")
                    li = None
                    for k in ks:
                        li = v.tensor_tensor(out=wt[k][:],
                                             in0=sg[:, (k % 4) * DM:(k % 4 + 1) * DM],
                                             in1=ss[:, (k % 4) * DM:(k % 4 + 1) * DM],
                                             op=ALU.subtract)
                    p.inc(li, "cv")
                    p.edge(v, "cv")

            for j in range(NCH):
                cvb = p.c["cv"]
                ctb = p.c["ct"]
                csb = p.c["cs"]

                @blk.sync
                def _(sy, j=j):
                    p.wait(sy, "qs", p.c["qs"])
                    if j >= 1:
                        p.wait(sy, "cv", p.c["cv"])
                        p.wait(sy, "ct", p.c["ct"])
                    for t_i in range(DT_I):
                        p.inc(sy.dma_start(out=vv[:, t_i * TC:(t_i + 1) * TC],
                                           in_=v_d[:, t_i * T + j * TC: t_i * T + j * TC + TC]),
                              "qs", 16)
                    for tt in range(NTT):
                        p.inc(sy.dma_start(out=x3[:, tt * DM:(tt + 1) * DM],
                                           in_=x_d[j * TC + tt * 128: j * TC + (tt + 1) * 128, :]),
                              "qs", 16)
                qs_v = p.c["qs"]

                @blk.vector
                def _(v, j=j, qs_v=qs_v):
                    p.wait(v, "qs", qs_v)
                    li = None
                    for tt in range(NTT):
                        g = j * NTT + tt
                        li = v.tensor_scalar(out=x3[:, tt * DM:(tt + 1) * DM],
                                             in0=x3[:, tt * DM:(tt + 1) * DM],
                                             scalar1=mv_all[:, 2 * g:2 * g + 1],
                                             scalar2=rstd_all[:, g:g + 1],
                                             op0=ALU.subtract, op1=ALU.mult)
                    p.inc(li, "cv")

                qw_base3 = (p.c["qw0"], p.c["qw1"])

                @blk.sync
                def _(sy, j=j, ctb=ctb):
                    for k in range(DT_I):
                        p.wait(sy, "ct", ctb + 8 + k - 1)
                        p.inc(sy.dma_start(out=wob[k % 2][:],
                                           in_=woT_d[k * 128:(k + 1) * 128, :]),
                              f"qw{k % 2}", 16)

                @blk.tensor
                def _(t, j=j, cvb=cvb, csb=csb, qw_base3=qw_base3):
                    p.wait(t, "cv", cvb + 1)
                    for grp in range(DT_M):
                        if grp >= 2:
                            p.wait(t, "cv", cvb + 1 + (grp - 1))
                        bank = pb[grp % 2]
                        li = None
                        for i in range(4):
                            idx = grp * 4 + i
                            tt, c = idx // DT_M, idx % DT_M
                            li = t.transpose(bank[:, i * 128:(i + 1) * 128],
                                             x3[:, tt * DM + c * 128: tt * DM + (c + 1) * 128],
                                             iden[:])
                        p.inc(li, "ct")
                    p.wait(t, "cv", cvb + 9)
                    for k in range(DT_I):
                        p.wait(t, f"qw{k % 2}", qw_base3[k % 2] + 16 * (k // 2 + 1))
                        li = None
                        for o in range(DT_M):
                            li = t.matmul(pb[o][:, 0:TC], wob[k % 2][:, o * 128:(o + 1) * 128],
                                          vv[:, k * TC:(k + 1) * TC],
                                          start=(k == 0), stop=(k == DT_I - 1))
                        p.inc(li, "ct")
                    p.wait(t, "cs", csb + 1)
                    li = None
                    for o in range(DT_M):
                        for k in range(DT_M):
                            li = t.matmul(pb[o][:, 0:TC], wt[k][:, o * 128:(o + 1) * 128],
                                          xnf3[:, k * TC:(k + 1) * TC],
                                          start=(k == 0), stop=(k == DT_M - 1))
                    p.inc(li, "ct")
                    p.wait(t, "cv", cvb + 10)
                    for grp in range(DT_M):
                        if grp >= 2:
                            p.wait(t, "cv", cvb + 10 + (grp - 1))
                        bank = pb[grp % 2]
                        li = None
                        for i in range(4):
                            idx = grp * 4 + i
                            c, tt = idx // NTT, idx % NTT
                            li = t.transpose(bank[:, i * 128:(i + 1) * 128],
                                             ss[:, c * TC + tt * 128: c * TC + (tt + 1) * 128],
                                             iden[:])
                        p.inc(li, "ct")

                @blk.vector
                def _(v, j=j, ctb=ctb):
                    for grp in range(DT_M):
                        p.wait(v, "ct", ctb + grp + 1)
                        bank = pb[grp % 2]
                        li = None
                        for i in range(4):
                            idx = grp * 4 + i
                            tt, c = idx // DT_M, idx % DT_M
                            li = v.tensor_copy(out=xnf3[:, c * TC + tt * 128: c * TC + (tt + 1) * 128],
                                               in_=bank[:, i * 128:(i + 1) * 128])
                        p.inc(li, "cv")

                @blk.scalar
                def _(s, j=j, ctb=ctb):
                    p.wait(s, "ct", ctb + 8 + 16)
                    li = None
                    for o in range(DT_M):
                        li = s.activation(out=ss[:, o * TC:(o + 1) * TC], in_=pb[o][:, 0:TC],
                                          func=AF.Identity, bias=cbout[:, o:o + 1])
                    p.inc(li, "cs")
                    p.wait(s, "ct", ctb + 8 + 16 + 1)
                    li = None
                    for o in range(DT_M):
                        li = s.activation(out=sg[:, o * TC:(o + 1) * TC], in_=pb[o][:, 0:TC],
                                          func=AF.Silu, bias=cbffn[:, o:o + 1], scale=alph[:, 0:1])
                    p.inc(li, "cs")

                @blk.vector
                def _(v, j=j, csb=csb):
                    p.wait(v, "cs", csb + 2)
                    li = None
                    for o in range(DT_M):
                        li = v.tensor_tensor(out=ss[:, o * TC:(o + 1) * TC],
                                             in0=ss[:, o * TC:(o + 1) * TC],
                                             in1=sg[:, o * TC:(o + 1) * TC], op=ALU.mult)
                    p.inc(li, "cv")

                @blk.sync
                def _(sy, j=j, ctb=ctb):
                    p.wait(sy, "ct", ctb + 8)
                    for tt in range(NTT):
                        p.inc(sy.dma_start(out=x3[:, tt * DM:(tt + 1) * DM],
                                           in_=x_d[j * TC + tt * 128: j * TC + (tt + 1) * 128, :]),
                              "qs", 16)
                qs_x3 = p.c["qs"]

                @blk.vector
                def _(v, j=j, ctb=ctb, qs_x3=qs_x3):
                    p.wait(v, "qs", qs_x3)
                    for grp in range(DT_M):
                        p.wait(v, "ct", ctb + 8 + 16 + 1 + grp + 1)
                        bank = pb[grp % 2]
                        li = None
                        for i in range(4):
                            idx = grp * 4 + i
                            c, tt = idx // NTT, idx % NTT
                            li = v.tensor_tensor(out=x3[:, tt * DM + c * 128: tt * DM + (c + 1) * 128],
                                                 in0=x3[:, tt * DM + c * 128: tt * DM + (c + 1) * 128],
                                                 in1=bank[:, i * 128:(i + 1) * 128], op=ALU.add)
                        p.inc(li, "cv")

                @blk.sync
                def _(sy, j=j, cvb=cvb):
                    p.wait(sy, "cv", cvb + 18)
                    for tt in range(NTT):
                        p.inc(sy.dma_start(out=out_d[j * TC + tt * 128: j * TC + (tt + 1) * 128, :],
                                           in_=x3[:, tt * DM:(tt + 1) * DM]), "qs", 16)

        @blk.gpsimd
        def _(g):
            p.wait(g, "qs", p.c["qs"])
            p.wait(g, "qg", p.c["qg"])
            p.wait(g, "qt0", p.c["qt0"])
            p.wait(g, "qt1", p.c["qt1"])
            p.wait(g, "qb0", p.c["qb0"])
            p.wait(g, "qb1", p.c["qb1"])
            p.wait(g, "qbs", p.c["qbs"])
            p.wait(g, "qr", p.c["qr"])
            p.wait(g, "qv", p.c["qv"])
            p.wait(g, "qwi0", p.c["qwi0"])
            p.wait(g, "qwi1", p.c["qwi1"])
            p.wait(g, "qx", p.c["qx"])
            p.wait(g, "qi", p.c["qi"])

    return nc


def kernel(L0_attn_x, L0_attn_W_in, L0_attn_b_in, L0_attn_W_conv, L0_attn_b_conv,
           L0_attn_W_x, L0_attn_b_x, L0_attn_W_out, L0_attn_b_out, L0_attn_A_log,
           L0_attn_D, L0_ffn_weight, L0_ffn_alpha, L0_ffn_bias):
    f32 = np.float32
    c_ = np.ascontiguousarray
    x = c_(np.asarray(L0_attn_x, f32))

    winT = c_(np.asarray(L0_attn_W_in, f32).T)
    wxT = c_(np.asarray(L0_attn_W_x, f32).T)
    woT = c_(np.asarray(L0_attn_W_out, f32).T)
    wfT = c_(np.asarray(L0_ffn_weight, f32).T)
    b_in = np.asarray(L0_attn_b_in, f32)
    b_x = np.asarray(L0_attn_b_x, f32)

    def col128(vec, nt):
        return c_(vec.reshape(nt, 128).T)

    cb_in_xi = col128(b_in[:DI], DT_I)
    cb_in_res = col128(b_in[DI:], DT_I)
    cw_conv = c_(np.asarray(L0_attn_W_conv, f32)[:, 0, :].reshape(DT_I, 128, 4)
                 .transpose(1, 0, 2).reshape(128, DT_I * 4))
    cb_conv = col128(np.asarray(L0_attn_b_conv, f32), DT_I)
    cb_x_d = col128(b_x[:DI], DT_I)
    cb_x_bc = c_(b_x[DI:].reshape(32, 1))
    cb_out = col128(np.asarray(L0_attn_b_out, f32), DT_M)
    c_d = col128(np.asarray(L0_attn_D, f32), DT_I)
    c_a_log = c_(np.asarray(L0_attn_A_log, f32).reshape(DT_I, 128, NS)
                 .transpose(1, 0, 2).reshape(128, DT_I * NS))
    c_alpha = c_(np.broadcast_to(np.asarray(L0_ffn_alpha, f32).reshape(1, 1), (128, 1)))
    cb_ffn = col128(np.asarray(L0_ffn_bias, f32), DT_M)
    iden = np.eye(128, dtype=f32)
    zeros3 = np.zeros((3, DM), f32)
    zmask = np.zeros((128, 1), f32)

    shared = dict(w_in_t=winT, w_x_t=wxT, w_out_t=woT, w_ffn_t=wfT,
                  cb_in_xi=cb_in_xi, cb_in_res=cb_in_res, cw_conv=cw_conv,
                  cb_conv=cb_conv, cb_x_d=cb_x_d, cb_x_bc=cb_x_bc, cb_out=cb_out,
                  c_d=c_d, c_a_log=c_a_log, c_alpha=c_alpha, cb_ffn=cb_ffn, iden=iden)

    in_maps = []
    for c in range(8):
        seq = c % 4
        m = dict(shared)
        m["x"] = c_(x[seq])
        m["x_halo3"] = zeros3
        m["halo_mask"] = zmask
        in_maps.append(m)

    nc = build_nc()
    res = run_bass_kernel_spmd(nc, in_maps, core_ids=list(range(8)))
    out = np.empty((4, 4096, DM), f32)
    for seq in range(4):
        out[seq] = res.results[seq]["out"]
    return out


# version 19
# speedup vs baseline: 1.1376x; 1.1376x over previous
import contextlib
import numpy as np
import concourse.bass as bass
import concourse.mybir as mybir
from concourse.bass_utils import run_bass_kernel_spmd

F32 = mybir.dt.float32
AF = mybir.ActivationFunctionType
ALU = mybir.AluOpType
AX = mybir.AxisListType

DM = 1024
DI = 2048
NS = 16
DT_M = DM // 128
DT_I = DI // 128
EPS = 1e-5

import os as _os
HALF = False
T = int(_os.environ.get("MK_T", "0")) or (2048 if HALF else 4096)
TC = 512
NCH = T // TC
NTT = TC // 128
XS = TC + 3


class P:
    def __init__(self, nc, es):
        self.nc = nc
        self.s = {}
        self.c = {}
        for name in ("qs", "qg", "ct", "cv", "cs", "qw0", "qw1", "qt0", "qt1", "qb0", "qb1", "qbs", "qr", "qv"):
            self.s[name] = es.enter_context(nc.semaphore(name))
            self.c[name] = 0

    def inc(self, instr, name, k=1):
        instr.then_inc(self.s[name], k)
        self.c[name] += k
        return self.c[name]

    def wait(self, eng, name, val=None):
        if val is None:
            val = self.c[name]
        if val > 0:
            eng.wait_ge(self.s[name], val)

    def edge(self, eng, name):
        self.wait(eng, name, self.c[name])


def build_nc():
    nc = bass.Bass("TRN2", target_bir_lowering=False)

    x_d = nc.dram_tensor("x", [T, DM], F32, kind="ExternalInput").ap()
    xh_d = nc.dram_tensor("x_halo3", [3, DM], F32, kind="ExternalInput").ap()
    hm_d = nc.dram_tensor("halo_mask", [128, 1], F32, kind="ExternalInput").ap()
    winT_d = nc.dram_tensor("w_in_t", [DM, 2 * DI], F32, kind="ExternalInput").ap()
    wxT_d = nc.dram_tensor("w_x_t", [DI, DI + 2 * NS], F32, kind="ExternalInput").ap()
    woT_d = nc.dram_tensor("w_out_t", [DI, DM], F32, kind="ExternalInput").ap()
    wfT_d = nc.dram_tensor("w_ffn_t", [DM, DM], F32, kind="ExternalInput").ap()
    cbixi_d = nc.dram_tensor("cb_in_xi", [128, DT_I], F32, kind="ExternalInput").ap()
    cbires_d = nc.dram_tensor("cb_in_res", [128, DT_I], F32, kind="ExternalInput").ap()
    cwconv_d = nc.dram_tensor("cw_conv", [128, DT_I * 4], F32, kind="ExternalInput").ap()
    cbconv_d = nc.dram_tensor("cb_conv", [128, DT_I], F32, kind="ExternalInput").ap()
    cbxd_d = nc.dram_tensor("cb_x_d", [128, DT_I], F32, kind="ExternalInput").ap()
    cbxbc_d = nc.dram_tensor("cb_x_bc", [32, 1], F32, kind="ExternalInput").ap()
    cbout_d = nc.dram_tensor("cb_out", [128, DT_M], F32, kind="ExternalInput").ap()
    cd_d = nc.dram_tensor("c_d", [128, DT_I], F32, kind="ExternalInput").ap()
    calog_d = nc.dram_tensor("c_a_log", [128, DT_I * NS], F32, kind="ExternalInput").ap()
    alpha_d = nc.dram_tensor("c_alpha", [128, 1], F32, kind="ExternalInput").ap()
    cbffn_d = nc.dram_tensor("cb_ffn", [128, DT_M], F32, kind="ExternalInput").ap()
    iden_d = nc.dram_tensor("iden", [128, 128], F32, kind="ExternalInput").ap()
    out_d = nc.dram_tensor("out", [T, DM], F32, kind="ExternalOutput").ap()
    xi2_d = nc.dram_tensor("xi2_s", [128, DT_I * T], F32, kind="Internal").ap()
    sres_d = nc.dram_tensor("sres_s", [128, DT_I * T], F32, kind="Internal").ap()
    v_d = nc.dram_tensor("v_s", [128, DT_I * T], F32, kind="Internal").ap()
    bcd_d = nc.dram_tensor("bc_s", [32, T], F32, kind="Internal").ap()

    es = contextlib.ExitStack()
    with es:
        p = P(nc, es)
        sb = lambda name, shape: es.enter_context(nc.sbuf_tensor(name, shape, F32))

        cbixi = sb("cbixi", [128, DT_I])
        cbires = sb("cbires", [128, DT_I])
        cwconv = sb("cwconv", [128, DT_I * 4])
        cbconv = sb("cbconv", [128, DT_I])
        cbxd = sb("cbxd", [128, DT_I])
        cbxbc = sb("cbxbc", [32, 1])
        cbout = sb("cbout", [128, DT_M])
        cdv = sb("cdv", [128, DT_I])
        ca = sb("ca", [128, DT_I * NS])
        alph = sb("alph", [128, 1])
        cbffn = sb("cbffn", [128, DT_M])
        iden = sb("iden_sb", [128, 128])
        hmask = sb("hmask", [128, 1])
        epsb = sb("epsb", [128, 1])
        ones_row = sb("ones_row", [1, 128])
        ones_col = sb("ones_col", [128, 1])
        mv_all = sb("mv_all", [128, 2 * (T // 128)])
        rstd_all = sb("rstd_all", [128, T // 128])
        hstate = sb("hstate", [128, DT_I * NS])
        xihalo = sb("xihalo", [128, DT_I * 3])

        pb = [es.enter_context(nc.psum_tensor(f"pb{i}", [128, 512], F32)) for i in range(8)]

        blk = es.enter_context(nc.Block())

        @blk.gpsimd
        def _(g):
            for dst, src in ((cbixi, cbixi_d), (cbires, cbires_d), (cwconv, cwconv_d),
                             (cbconv, cbconv_d), (cbxd, cbxd_d), (cbxbc, cbxbc_d),
                             (cbout, cbout_d), (cdv, cd_d), (ca, calog_d),
                             (alph, alpha_d), (cbffn, cbffn_d), (iden, iden_d),
                             (hmask, hm_d)):
                p.inc(g.dma_start(out=dst[:], in_=src[:]), "qg", 16)

        @blk.vector
        def _(v):
            p.wait(v, "qg")
            v.memset(ones_row[:], 1.0)
            v.memset(epsb[:], EPS)
            v.memset(ones_col[:], 1.0)
            v.memset(hstate[:], 0.0)
            p.inc(v.memset(xihalo[:], 0.0), "cv")

        @blk.scalar
        def _(s):
            p.wait(s, "qg")
            p.wait(s, "cv")
            li = None
            for t in range(DT_I):
                li = s.activation(out=ca[:, t * NS:(t + 1) * NS],
                                  in_=ca[:, t * NS:(t + 1) * NS], func=AF.Exp)
            p.inc(li, "cs")

        @blk.vector
        def _(v):
            p.wait(v, "cs")
            li = None
            for t in range(DT_I):
                li = v.tensor_scalar(out=ca[:, t * NS:(t + 1) * NS],
                                     in0=ca[:, t * NS:(t + 1) * NS],
                                     scalar1=-1.0, scalar2=None, op0=ALU.mult)
            p.inc(li, "cv")

        with contextlib.ExitStack() as es1:
            sb1 = lambda name, shape: es1.enter_context(nc.sbuf_tensor(name, shape, F32))
            win = [sb1(f"win{k}", [128, 2 * DI]) for k in range(DT_M)]
            x1 = sb1("p1_x", [128, NTT * DM])
            bns = sb1("p1_bns", [128, 64])
            xnf = sb1("p1_xnf", [128, DT_M * TC])
            xi = sb1("p1_xi", [128, DT_I * XS])
            cvs = [sb1(f"p1_cv{i}", [128, TC]) for i in range(2)]
            stg = [sb1(f"p1_stg{i}", [128, TC]) for i in range(2)]

            @blk.sync
            def _(sy):
                p.wait(sy, "cv", p.c["cv"])
                p.wait(sy, "cs", p.c["cs"])
                p.inc(sy.dma_start(out=x1[:3, 0:DM], in_=xh_d[:]), "qs", 16)
            qs_halo = p.c["qs"]

            @blk.sync
            def _(sy):
                p.wait(sy, "qs", p.c["qs"])
                for k in range(DT_M):
                    p.inc(sy.dma_start(out=win[k][:], in_=winT_d[k * 128:(k + 1) * 128, :]),
                          "qs", 16)
            qs_win = p.c["qs"]

            @blk.vector
            def _(v):
                p.wait(v, "qs", qs_halo)
                p.wait(v, "cs")
                v.bn_stats(out=bns[:3, 0:6], in_=x1[:3, 0:512])
                li = v.bn_stats(out=bns[:3, 6:12], in_=x1[:3, 512:1024])
                p.inc(li, "cv")
                p.edge(v, "cv")
                li = v.bn_aggr(out=bns[:3, 12:14], in_=bns[:3, 0:12])
                p.inc(li, "cv")
            cv_halo_stats = p.c["cv"]

            @blk.scalar
            def _(s):
                p.wait(s, "cv", cv_halo_stats)
                li = s.activation(out=bns[:3, 14:15], in_=bns[:3, 13:14], func=AF.Ln,
                                  bias=epsb[:3, 0:1])
                p.inc(li, "cs")
                p.edge(s, "cs")
                li = s.activation(out=bns[:3, 14:15], in_=bns[:3, 14:15], func=AF.Exp, scale=-0.5)
                p.inc(li, "cs")
            cs_halo_rstd = p.c["cs"]

            @blk.vector
            def _(v):
                p.wait(v, "cs", cs_halo_rstd)
                li = v.tensor_scalar(out=x1[:3, 0:DM], in0=x1[:3, 0:DM],
                                     scalar1=bns[:3, 12:13], scalar2=bns[:3, 14:15],
                                     op0=ALU.subtract, op1=ALU.mult)
                p.inc(li, "cv")
            cv_halo_xn = p.c["cv"]

            @blk.tensor
            def _(t):
                p.wait(t, "cv", cv_halo_xn)
                p.wait(t, "qs", qs_win)
                li = None
                for c in range(DT_M):
                    li = t.transpose(pb[0][:, c * 3:c * 3 + 3], x1[:3, c * 128:(c + 1) * 128],
                                     iden[:3, 0:3])
                p.inc(li, "ct")
            ct_halo_tr = p.c["ct"]

            @blk.vector
            def _(v):
                p.wait(v, "ct", ct_halo_tr)
                p.inc(v.tensor_copy(out=xnf[:, 0:DT_M * 3], in_=pb[0][:, 0:DT_M * 3]), "cv")
            cv_halo_evac = p.c["cv"]

            cs_hb = p.c["cs"]

            @blk.tensor
            def _(t):
                p.wait(t, "cv", cv_halo_evac)
                for o in range(DT_I):
                    if o >= 2:
                        p.wait(t, "cs", cs_hb + o - 1)
                    dst = pb[1] if o % 2 == 0 else pb[2]
                    li = None
                    for k in range(DT_M):
                        li = t.matmul(dst[:, 0:3], win[k][:, o * 128:(o + 1) * 128],
                                      xnf[:, k * 3:k * 3 + 3], start=(k == 0), stop=(k == DT_M - 1))
                    p.inc(li, "ct")
            ct_halo_mm = p.c["ct"]

            @blk.scalar
            def _(s):
                for o in range(DT_I):
                    src = pb[1] if o % 2 == 0 else pb[2]
                    p.wait(s, "ct", ct_halo_mm - DT_I + o + 1)
                    li = s.activation(out=xihalo[:, o * 3:(o + 1) * 3], in_=src[:, 0:3],
                                      func=AF.Identity, bias=cbixi[:, o:o + 1])
                    p.inc(li, "cs")
            cs_halo = p.c["cs"]

            @blk.vector
            def _(v):
                p.wait(v, "cs", cs_halo)
                li = v.tensor_scalar(out=xihalo[:], in0=xihalo[:], scalar1=hmask[:, 0:1],
                                     scalar2=None, op0=ALU.mult)
                p.inc(li, "cv")
                p.edge(v, "cv")
                li = None
                for t in range(DT_I):
                    li = v.tensor_copy(out=xi[:, t * XS:t * XS + 3],
                                       in_=xihalo[:, t * 3:(t + 1) * 3])
                p.inc(li, "cv")

            ct_prev_tr_end = p.c["ct"]
            for j in range(NCH):
                qs_x = p.c["qs"] + 16 * NTT
                cvb = p.c["cv"]
                ctb = p.c["ct"]
                csb = p.c["cs"]
                qt_b = {0: p.c["qt0"], 1: p.c["qt1"]}

                @blk.sync
                def _(sy, j=j):
                    p.wait(sy, "ct", ct_prev_tr_end)
                    for tt in range(NTT):
                        p.inc(sy.dma_start(out=x1[:, tt * DM:(tt + 1) * DM],
                                           in_=x_d[j * TC + tt * 128: j * TC + (tt + 1) * 128, :]),
                              "qs", 16)

                @blk.vector
                def _(v, j=j, csb=csb):
                    p.wait(v, "qs", qs_x)
                    li = None
                    for tt in range(NTT):
                        v.bn_stats(out=bns[:, tt * 12:tt * 12 + 6],
                                   in_=x1[:, tt * DM:tt * DM + 512])
                        li = v.bn_stats(out=bns[:, tt * 12 + 6:tt * 12 + 12],
                                        in_=x1[:, tt * DM + 512:tt * DM + 1024])
                    p.inc(li, "cv")
                    p.edge(v, "cv")
                    li = None
                    for tt in range(NTT):
                        g = j * NTT + tt
                        li = v.bn_aggr(out=mv_all[:, 2 * g:2 * g + 2],
                                       in_=bns[:, tt * 12:tt * 12 + 12])
                    p.inc(li, "cv")
                    p.wait(v, "cs", csb + 2)
                    li = None
                    for tt in range(NTT):
                        g = j * NTT + tt
                        li = v.tensor_scalar(out=x1[:, tt * DM:(tt + 1) * DM],
                                             in0=x1[:, tt * DM:(tt + 1) * DM],
                                             scalar1=mv_all[:, 2 * g:2 * g + 1],
                                             scalar2=rstd_all[:, g:g + 1],
                                             op0=ALU.subtract, op1=ALU.mult)
                    p.inc(li, "cv")

                @blk.scalar
                def _(s, j=j, cvb=cvb):
                    p.wait(s, "cv", cvb + 2)
                    li = None
                    for tt in range(NTT):
                        g = j * NTT + tt
                        li = s.activation(out=rstd_all[:, g:g + 1],
                                          in_=mv_all[:, 2 * g + 1:2 * g + 2],
                                          func=AF.Ln, bias=epsb[:, 0:1])
                    p.inc(li, "cs")
                    p.edge(s, "cs")
                    li = None
                    for tt in range(NTT):
                        g = j * NTT + tt
                        li = s.activation(out=rstd_all[:, g:g + 1],
                                          in_=rstd_all[:, g:g + 1],
                                          func=AF.Exp, scale=-0.5)
                    p.inc(li, "cs")

                @blk.tensor
                def _(t, j=j, cvb=cvb, csb=csb):
                    p.wait(t, "cv", cvb + 3)
                    for grp in range(DT_M):
                        if grp >= 2:
                            p.wait(t, "cv", cvb + 3 + (grp - 1))
                        bank = pb[grp % 2]
                        li = None
                        for i in range(4):
                            idx = grp * 4 + i
                            tt, c = idx // DT_M, idx % DT_M
                            li = t.transpose(bank[:, i * 128:(i + 1) * 128],
                                             x1[:, tt * DM + c * 128: tt * DM + (c + 1) * 128],
                                             iden[:])
                        p.inc(li, "ct")
                    p.wait(t, "cv", cvb + 11)
                    for o in range(2 * DI // 128):
                        if o >= 2:
                            p.wait(t, "cs", csb + 2 + o - 1)
                        bank = pb[2 + o % 2]
                        li = None
                        for k in range(DT_M):
                            li = t.matmul(bank[:], win[k][:, o * 128:(o + 1) * 128],
                                          xnf[:, k * TC:(k + 1) * TC],
                                          start=(k == 0), stop=(k == DT_M - 1))
                        p.inc(li, "ct")
                ct_prev_tr_end = ctb + 8

                @blk.vector
                def _(v, j=j, ctb=ctb):
                    for grp in range(DT_M):
                        p.wait(v, "ct", ctb + grp + 1)
                        bank = pb[grp % 2]
                        li = None
                        for i in range(4):
                            idx = grp * 4 + i
                            tt, c = idx // DT_M, idx % DT_M
                            li = v.tensor_copy(out=xnf[:, c * TC + tt * 128: c * TC + (tt + 1) * 128],
                                               in_=bank[:, i * 128:(i + 1) * 128])
                        p.inc(li, "cv")

                qt_store_count = {0: 0, 1: 0}
                silu_qt_wait = {}
                res_qt_wait = {}
                for oo in range(DT_I):
                    s_ = oo % 2
                    res_qt_wait[oo] = qt_b[s_] + 16 * qt_store_count[s_]
                    qt_store_count[s_] += 1
                for t_i in range(DT_I):
                    s_ = t_i % 2
                    silu_qt_wait[t_i] = qt_b[s_] + 16 * qt_store_count[s_]
                    qt_store_count[s_] += 1

                @blk.scalar
                def _(s, j=j, ctb=ctb, cvb=cvb):
                    for o in range(2 * DI // 128):
                        p.wait(s, "ct", ctb + 8 + o + 1)
                        bank = pb[2 + o % 2]
                        if o < DI // 128:
                            li = s.activation(out=xi[:, o * XS + 3: o * XS + 3 + TC], in_=bank[:],
                                              func=AF.Identity, bias=cbixi[:, o:o + 1])
                        else:
                            oo = o - DI // 128
                            p.wait(s, f"qt{oo % 2}", res_qt_wait[oo])
                            li = s.activation(out=stg[oo % 2][:], in_=bank[:],
                                              func=AF.Silu, bias=cbires[:, oo:oo + 1])
                        p.inc(li, "cs")
                    for t_i in range(DT_I):
                        cg = t_i // 2
                        p.wait(s, "cv", cvb + 11 + 5 * cg + 4)
                        p.wait(s, f"qt{t_i % 2}", silu_qt_wait[t_i])
                        li = s.activation(out=stg[t_i % 2][:], in_=cvs[t_i % 2][:],
                                          func=AF.Silu, bias=cbconv[:, t_i:t_i + 1])
                        p.inc(li, "cs")

                @blk.vector
                def _(v, j=j, csb=csb):
                    for cg in range(DT_I // 2):
                        ts_grp = (2 * cg, 2 * cg + 1)
                        p.wait(v, "cs", csb + 2 + 2 * cg + 2)
                        if cg >= 1:
                            p.wait(v, "cs", csb + 2 + 32 + 2 * cg)
                        li = None
                        for t_i in ts_grp:
                            base = t_i * XS
                            li = v.tensor_scalar(out=cvs[t_i % 2][:], in0=xi[:, base:base + TC],
                                                 scalar1=cwconv[:, t_i * 4:t_i * 4 + 1],
                                                 scalar2=None, op0=ALU.mult)
                        p.inc(li, "cv")
                        for k in (1, 2, 3):
                            p.edge(v, "cv")
                            li = None
                            for t_i in ts_grp:
                                base = t_i * XS
                                li = v.scalar_tensor_tensor(out=cvs[t_i % 2][:],
                                                            in0=xi[:, base + k:base + k + TC],
                                                            scalar=cwconv[:, t_i * 4 + k:t_i * 4 + k + 1],
                                                            in1=cvs[t_i % 2][:],
                                                            op0=ALU.mult, op1=ALU.add)
                            p.inc(li, "cv")
                        p.edge(v, "cv")
                        li = None
                        for t_i in ts_grp:
                            base = t_i * XS
                            li = v.tensor_copy(out=xi[:, base:base + 3],
                                               in_=xi[:, base + TC:base + TC + 3])
                        p.inc(li, "cv")

                @blk.sync
                def _(sy, j=j, csb=csb):
                    for oo in range(DT_I):
                        p.wait(sy, "cs", csb + 2 + 16 + oo + 1)
                        p.inc(sy.dma_start(out=sres_d[:, oo * T + j * TC: oo * T + j * TC + TC],
                                           in_=stg[oo % 2][:]), f"qt{oo % 2}", 16)
                    for t_i in range(DT_I):
                        p.wait(sy, "cs", csb + 2 + 32 + t_i + 1)
                        p.inc(sy.dma_start(out=xi2_d[:, t_i * T + j * TC: t_i * T + j * TC + TC],
                                           in_=stg[t_i % 2][:]), f"qt{t_i % 2}", 16)

        TP = 256
        NC2 = T // TP
        with contextlib.ExitStack() as es2:
            sb2 = lambda name, shape: es2.enter_context(nc.sbuf_tensor(name, shape, F32))
            wxb = [sb2(f"wx{i}", [128, 12 * 128]) for i in range(2)]
            bcb = [sb2(f"p2_bcb{i}", [128, 1, 2 * TP]) for i in range(2)]
            xi2 = sb2("p2_xi2", [128, DT_I * TP])
            sdel = [sb2(f"p2_del{i}", [128, DT_I * TP]) for i in range(2)]
            sdtx = [sb2(f"p2_dtx{i}", [128, DT_I * TP]) for i in range(2)]
            sy_ = sb2("p2_y", [128, DT_I * TP])
            srs = sb2("p2_srs", [128, DT_I * TP])
            h_all = sb2("p2_h", [128, 8 * TP])
            da = sb2("p2_da", [128, DT_I * TP])
            sp12 = sb2("p2_sp", [128, 17 * TP])
            dbx = [sb2(f"p2_dbx{i}", [128, 4 * TP]) for i in range(2)]
            zt = [sb2(f"p2_z{i}", [128, 4 * TP]) for i in range(2)]
            bc = sb2("p2_bc", [32, TP])

            NO = (DI + 2 * NS + 127) // 128
            GRPS = [list(range(0, 12)), list(range(12, 17))]

            def da_off(jj, n):
                if jj == NC2 - 1:
                    return 16 * n
                return 16 * n + (0 if n < 12 else 3 if n < 14 else 6)

            def emit_dA_part(jj, ns, csA, cvb_jj, qbs_jj, prevcv):
                @blk.scalar
                def _(s, jj=jj, ns=ns, csA=csA, cvb_jj=cvb_jj, qbs_jj=qbs_jj,
                      prevcv=prevcv):
                    cvs0 = cvb_jj + 5
                    if ns[0] == 0:
                        p.wait(s, "cv", cvb_jj + 4)
                        p.wait(s, "qbs", qbs_jj)
                    for n in ns:
                        if n >= 2:
                            p.wait(s, "cv", cvs0 + 16 * (n - 1))
                        elif prevcv is not None:
                            p.wait(s, "cv", prevcv + 5 + 16 * (15 + n))
                        p.inc(s.dma_start(
                            out=bcb[n % 2][:, 0, 0:TP],
                            in_=bcd_d[n, jj * TP:(jj + 1) * TP].partition_broadcast(128)),
                            f"qb{n % 2}", 16)
                        p.inc(s.dma_start(
                            out=bcb[n % 2][:, 0, TP:2 * TP],
                            in_=bcd_d[NS + n, jj * TP:(jj + 1) * TP].partition_broadcast(128)),
                            f"qb{n % 2}", 16)
                        for t_i in range(DT_I):
                            if n > 0:
                                p.wait(s, "cs", csA + da_off(jj, n - 1) + t_i + 1)
                                p.wait(s, "cv", cvs0 + 16 * (n - 1) + 4 * (t_i // 4) + 2)
                            li = s.activation(out=da[:, t_i * TP:(t_i + 1) * TP],
                                              in_=sdel[jj % 2][:, t_i * TP:(t_i + 1) * TP],
                                              func=AF.Exp,
                                              scale=ca[:, t_i * NS + n:t_i * NS + n + 1])
                            p.inc(li, "cs")

            prev = None
            prev_qbs = 0
            prevcv_prev = None
            for j in range(NC2):
                ctb = p.c["ct"]
                csb = p.c["cs"]
                cvb = p.c["cv"]

                @blk.sync
                def _(sy, j=j, prev=prev):
                    if j == 0:
                        p.wait(sy, "qs", p.c["qs"])
                        p.wait(sy, "qt0", p.c["qt0"])
                        p.wait(sy, "qt1", p.c["qt1"])
                        p.wait(sy, "cv", p.c["cv"])
                        p.wait(sy, "cs", p.c["cs"])
                        p.wait(sy, "ct", p.c["ct"])
                    else:
                        (pq, pct, pcs, pcv) = prev
                        p.wait(sy, "cv", pcv + 5)
                        p.wait(sy, "ct", pct + 32)
                    for t_i in range(DT_I):
                        p.inc(sy.dma_start(out=xi2[:, t_i * TP:(t_i + 1) * TP],
                                           in_=xi2_d[:, t_i * T + j * TP: t_i * T + j * TP + TP]),
                              "qs", 16)
                qs_xi2 = p.c["qs"]


                qwb = (p.c["qw0"], p.c["qw1"])

                @blk.sync
                def _(sy, j=j, ctb=ctb):
                    for g in range(2):
                        c0 = 0 if g == 0 else 12 * 128
                        cw = 12 * 128 if g == 0 else (DI + 2 * NS - 12 * 128)
                        for k in range(DT_I):
                            p.wait(sy, "ct", ctb + 16 * g + k - 1)
                            p.inc(sy.dma_start(out=wxb[k % 2][:, 0:cw],
                                               in_=wxT_d[k * 128:(k + 1) * 128, c0:c0 + cw]),
                                  f"qw{k % 2}", 16)

                @blk.tensor
                def _(t, j=j, prev=prev, ctb=ctb, csb=csb, cvb=cvb, qwb=qwb,
                      qs_xi2=qs_xi2):
                    for g, grp in enumerate(GRPS):
                        if g == 0:
                            p.wait(t, "qs", qs_xi2)
                            if prev is not None:
                                p.wait(t, "cs", prev[2] + (228 if j >= 2 else 4))
                        else:
                            p.wait(t, "cs", csb + (193 if j >= 1 else 1))
                        for k in range(DT_I):
                            p.wait(t, f"qw{k % 2}",
                                   qwb[k % 2] + 16 * (8 * g + k // 2 + 1))
                            li = None
                            for bi, o in enumerate(grp):
                                w = 32 if o == NO - 1 else 128
                                lc = bi * 128
                                bank = pb[bi // 2]
                                half = (bi % 2) * TP
                                dst = (bank[:, half:half + TP] if w == 128
                                       else bank[:32, half:half + TP])
                                st_ = (k == 0 and bi % 2 == 0)
                                sp_ = (k == DT_I - 1 and (bi % 2 == 1
                                                          or bi == len(grp) - 1))
                                li = t.matmul(dst, wxb[k % 2][:, lc:lc + w],
                                              xi2[:, k * TP:(k + 1) * TP],
                                              start=st_, stop=sp_)
                            p.inc(li, "ct")

                @blk.sync
                def _(sy, j=j, csb=csb):
                    p.wait(sy, "cs", csb + (230 if j >= 1 else 6))
                    p.inc(sy.dma_start(out=bcd_d[:, j * TP:(j + 1) * TP], in_=bc[:, 0:TP]),
                          "qbs", 16)
                qbs_j = p.c["qbs"]
                @blk.sync
                def _(sy, j=j, prev=prev):
                    if j >= 1:
                        p.wait(sy, "cv", prev[3] + 262)
                    else:
                        p.wait(sy, "qt0", p.c["qt0"])
                        p.wait(sy, "qt1", p.c["qt1"])
                    for t_i in range(DT_I):
                        p.inc(sy.dma_start(out=srs[:, t_i * TP:(t_i + 1) * TP],
                                           in_=sres_d[:, t_i * T + j * TP: t_i * T + j * TP + TP]),
                              "qr", 16)
                qt0_sres = p.c["qr"]

                if prev is not None:
                    @blk.sync
                    def _(sy, j=j, prev=prev):
                        p.wait(sy, "cv", prev[3] + 262)
                        jp = j - 1
                        for t_i in range(DT_I):
                            p.inc(sy.dma_start(
                                out=v_d[:, t_i * T + jp * TP: t_i * T + jp * TP + TP],
                                in_=sy_[:, t_i * TP:(t_i + 1) * TP]), "qv", 16)

                daB = csb
                if prev is not None:
                    emit_dA_part(j - 1, list(range(0, 12)), daB, prev[3],
                                 prev_qbs, prevcv_prev)

                def emit_sp(g, j=j, prev=prev, ctb=ctb, csb=csb, cvb=cvb,
                            pqbs=prev_qbs):
                    @blk.scalar
                    def _(s, g=g):
                        grp = GRPS[g]
                        if g == 0:
                            p.wait(s, "cs", csb)
                            if prev is not None:
                                p.wait(s, "cv", prev[3] + 3)

                        p.wait(s, "ct", ctb + 16 * (g + 1))
                        dts = [(bi + 12 * g, o) for bi, o in enumerate(grp) if o < DI // 128]
                        li = None
                        for sl, o in dts:
                            lb = sl - 12 * g
                            bank = pb[lb // 2]
                            half = (lb % 2) * TP
                            li = s.activation(out=sp12[:, sl * TP:(sl + 1) * TP],
                                              in_=bank[:, half:half + TP],
                                              func=AF.Abs, bias=cbxd[:, o:o + 1])
                        for sl, o in dts:
                            lb = sl - 12 * g
                            bank = pb[lb // 2]
                            half = (lb % 2) * TP
                            li = s.activation(out=sdel[j % 2][:, o * TP:(o + 1) * TP],
                                              in_=bank[:, half:half + TP],
                                              func=AF.Relu, bias=cbxd[:, o:o + 1])
                        p.inc(li, "cs")
                        p.edge(s, "cs")
                        li = None
                        for sl, o in dts:
                            li = s.activation(out=sp12[:, sl * TP:(sl + 1) * TP],
                                              in_=sp12[:, sl * TP:(sl + 1) * TP],
                                              func=AF.Exp, scale=-1.0)
                        p.inc(li, "cs")
                        p.edge(s, "cs")
                        li = None
                        for sl, o in dts:
                            li = s.activation(out=sp12[:, sl * TP:(sl + 1) * TP],
                                              in_=sp12[:, sl * TP:(sl + 1) * TP],
                                              func=AF.Ln, bias=1.0)
                        if g == 1:
                            p.wait(s, "qbs", pqbs)
                            li = s.activation(out=bc[:, 0:TP], in_=pb[2][:32, 0:TP],
                                              func=AF.Identity, bias=cbxbc[:, 0:1])
                        p.inc(li, "cs")

                emit_sp(0)
                if prev is not None:
                    emit_dA_part(j - 1, [12, 13], daB, prev[3], prev_qbs,
                                 prevcv_prev)
                emit_sp(1)
                if prev is not None:
                    emit_dA_part(j - 1, [14, 15], daB, prev[3], prev_qbs,
                                 prevcv_prev)
                qb_j = (p.c["qb0"], p.c["qb1"])

                @blk.vector
                def _(v, j=j, prev=prev, ctb=ctb, csb=csb, cvb=cvb, qb_j=qb_j,
                      qt0_sres=qt0_sres, qt1_vst=p.c["qv"]):
                    for g, grp in enumerate(GRPS):
                        spb = csb + (192 if j >= 1 else 0)
                        p.wait(v, "cs", spb + (3 if g == 0 else
                                               (38 if j >= 1 else 6)))
                        dts = [(bi + 12 * g, o) for bi, o in enumerate(grp) if o < DI // 128]
                        li = None
                        for sl, o in dts:
                            li = v.tensor_tensor(out=sdel[j % 2][:, o * TP:(o + 1) * TP],
                                                 in0=sdel[j % 2][:, o * TP:(o + 1) * TP],
                                                 in1=sp12[:, sl * TP:(sl + 1) * TP],
                                                 op=ALU.add)
                        p.inc(li, "cv")
                        p.edge(v, "cv")
                        li = None
                        for sl, o in dts:
                            li = v.tensor_tensor(out=sdtx[j % 2][:, o * TP:(o + 1) * TP],
                                                 in0=sdel[j % 2][:, o * TP:(o + 1) * TP],
                                                 in1=xi2[:, o * TP:(o + 1) * TP],
                                                 op=ALU.mult)
                        p.inc(li, "cv")
                    p.wait(v, "qv", qt1_vst)
                    p.edge(v, "cv")
                    li = None
                    for t_i in range(DT_I):
                        li = v.tensor_scalar(out=sy_[:, t_i * TP:(t_i + 1) * TP],
                                             in0=xi2[:, t_i * TP:(t_i + 1) * TP],
                                             scalar1=cdv[:, t_i:t_i + 1],
                                             scalar2=None, op0=ALU.mult)
                    p.inc(li, "cv")
                    daB2 = csb + (6 if j == 0 else 262)
                    for n in range(NS):
                        p.wait(v, f"qb{n % 2}", qb_j[n % 2] + 32 * (n // 2 + 1))
                        bap_b = bcb[n % 2][:, 0:1, 0:TP].broadcast_to([128, 4, TP])
                        bap_c = bcb[n % 2][:, 0:1, TP:2 * TP].broadcast_to([128, 4, TP])
                        for tp_ in range(0, DT_I, 4):
                            qi = tp_ // 4
                            p.wait(v, "cs", daB2 + da_off(j, n) + tp_ + 4)
                            li = v.tensor_tensor(out=dbx[qi % 2][:],
                                                 in0=sdtx[j % 2][:, tp_ * TP:(tp_ + 4) * TP],
                                                 in1=bap_b, op=ALU.mult)
                            p.inc(li, "cv")
                            p.edge(v, "cv")
                            li = None
                            for t_i in range(tp_, tp_ + 4):
                                li = v.tensor_tensor_scan(
                                    out=h_all[:, (t_i % 8) * TP:(t_i % 8 + 1) * TP],
                                    data0=da[:, t_i * TP:(t_i + 1) * TP],
                                    data1=dbx[qi % 2][:, (t_i - tp_) * TP:(t_i - tp_ + 1) * TP],
                                    initial=hstate[:, t_i * NS + n:t_i * NS + n + 1],
                                    op0=ALU.mult, op1=ALU.add)
                            p.inc(li, "cv")
                            p.edge(v, "cv")
                            li = v.tensor_tensor(out=zt[qi % 2][:],
                                                 in0=h_all[:, (tp_ % 8) * TP:(tp_ % 8 + 4) * TP],
                                                 in1=bap_c, op=ALU.mult)
                            p.inc(li, "cv")
                            p.edge(v, "cv")
                            li = v.tensor_tensor(out=sy_[:, tp_ * TP:(tp_ + 4) * TP],
                                                 in0=sy_[:, tp_ * TP:(tp_ + 4) * TP],
                                                 in1=zt[qi % 2][:], op=ALU.add)
                            if tp_ % 8 == 4:
                                base = tp_ - 4
                                li = v.tensor_copy(
                                    out=hstate[:, base * NS + n:(tp_ + 4) * NS:NS],
                                    in_=h_all[:, TP - 1::TP])
                            p.inc(li, "cv")
                            p.edge(v, "cv")
                    p.wait(v, "qr", qt0_sres)
                    p.edge(v, "cv")
                    li = None
                    for t_i in range(DT_I):
                        li = v.tensor_tensor(out=sy_[:, t_i * TP:(t_i + 1) * TP],
                                             in0=sy_[:, t_i * TP:(t_i + 1) * TP],
                                             in1=srs[:, t_i * TP:(t_i + 1) * TP], op=ALU.mult)
                    p.inc(li, "cv")
                assert p.c["cv"] == cvb + 262, (p.c["cv"], cvb)


                prev_qbs = qbs_j
                prevcv_prev = prev[3] if prev is not None else None
                prev = (qs_xi2, ctb, csb, cvb)
                if j == NC2 - 1:
                    @blk.sync
                    def _(sy, j=j, cvb=cvb):
                        p.wait(sy, "cv", cvb + 262)
                        for t_i in range(DT_I):
                            p.inc(sy.dma_start(
                                out=v_d[:, t_i * T + j * TP: t_i * T + j * TP + TP],
                                in_=sy_[:, t_i * TP:(t_i + 1) * TP]), "qv", 16)
                    emit_dA_part(j, list(range(NS)), csb + (6 if j == 0 else 262),
                                 cvb, qbs_j, prevcv_prev)


        with contextlib.ExitStack() as es3:
            sb3 = lambda name, shape: es3.enter_context(nc.sbuf_tensor(name, shape, F32))
            wob = [sb3(f"wo{i}", [128, DM]) for i in range(2)]
            wt = [sb3(f"wt{k}", [128, DM]) for k in range(DT_M)]
            vv = sb3("p3_v", [128, DT_I * TC])
            x3 = sb3("p3_x", [128, NTT * DM])
            xnf3 = sb3("p3_xnf", [128, DT_M * TC])
            ss = sb3("p3_s", [128, DT_M * TC])
            sg = sb3("p3_g", [128, DT_M * TC])
            thr = sb3("p3_thr", [128, 2])
            red8 = sb3("p3_red", [128, DT_M + 2])

            @blk.sync
            def _(sy):
                p.wait(sy, "qs", p.c["qs"])
                p.wait(sy, "qt0", p.c["qt0"])
                p.wait(sy, "qt1", p.c["qt1"])
                p.wait(sy, "qv", p.c["qv"])
                p.wait(sy, "qr", p.c["qr"])
                p.wait(sy, "qb0", p.c["qb0"])
                p.wait(sy, "qb1", p.c["qb1"])
                p.wait(sy, "cv", p.c["cv"])
                p.wait(sy, "cs", p.c["cs"])
                p.wait(sy, "ct", p.c["ct"])
                for k in range(DT_M):
                    p.inc(sy.dma_start(out=wt[k][:], in_=wfT_d[k * 128:(k + 1) * 128, :]),
                          "qs", 16)
            qs_wf = p.c["qs"]

            @blk.vector
            def _(v):
                p.wait(v, "qs", qs_wf)
                li = None
                for k in range(DT_M):
                    li = v.tensor_reduce(out=red8[:, k:k + 1], in_=wt[k][:], axis=AX.X,
                                         op=ALU.add, apply_absolute_value=True)
                p.inc(li, "cv")
                p.edge(v, "cv")
                li = v.tensor_reduce(out=red8[:, DT_M:DT_M + 1], in_=red8[:, 0:DT_M], axis=AX.X,
                                     op=ALU.add)
                p.inc(li, "cv")
            cv_red = p.c["cv"]

            @blk.tensor
            def _(t):
                p.wait(t, "cv", cv_red)
                li = t.matmul(pb[0][:1, 0:1], red8[:, DT_M:DT_M + 1], ones_col[:],
                              start=True, stop=True)
                p.inc(li, "ct")
            ct_tot = p.c["ct"]

            @blk.scalar
            def _(s):
                p.wait(s, "ct", ct_tot)
                li = s.activation(out=thr[:1, 0:1], in_=pb[0][:1, 0:1], func=AF.Copy,
                                  scale=0.7 / float(DM * DM))
                p.inc(li, "cs")
            cs_thr = p.c["cs"]

            @blk.tensor
            def _(t):
                p.wait(t, "cs", cs_thr)
                li = t.matmul(pb[1][:, 0:1], ones_row[:1, :], thr[:1, 0:1], start=True, stop=True)
                p.inc(li, "ct")
            ct_thr = p.c["ct"]

            @blk.vector
            def _(v):
                p.wait(v, "ct", ct_thr)
                li = v.tensor_copy(out=thr[:, 0:1], in_=pb[1][:, 0:1])
                p.inc(li, "cv")
                p.edge(v, "cv")
                li = v.tensor_scalar(out=thr[:, 1:2], in0=thr[:, 0:1], scalar1=-1.0,
                                     scalar2=None, op0=ALU.mult)
                p.inc(li, "cv")
                p.edge(v, "cv")
                for kh in range(2):
                    ks = range(kh * 4, kh * 4 + 4)
                    li = None
                    for k in ks:
                        li = v.tensor_scalar(out=sg[:, (k % 4) * DM:(k % 4 + 1) * DM],
                                             in0=wt[k][:], scalar1=thr[:, 0:1],
                                             scalar2=None, op0=ALU.is_gt)
                    for k in ks:
                        li = v.tensor_scalar(out=ss[:, (k % 4) * DM:(k % 4 + 1) * DM],
                                             in0=wt[k][:], scalar1=thr[:, 1:2],
                                             scalar2=None, op0=ALU.is_lt)
                    p.inc(li, "cv")
                    p.edge(v, "cv")
                    li = None
                    for k in ks:
                        li = v.tensor_tensor(out=wt[k][:],
                                             in0=sg[:, (k % 4) * DM:(k % 4 + 1) * DM],
                                             in1=ss[:, (k % 4) * DM:(k % 4 + 1) * DM],
                                             op=ALU.subtract)
                    p.inc(li, "cv")
                    p.edge(v, "cv")

            for j in range(NCH):
                cvb = p.c["cv"]
                ctb = p.c["ct"]
                csb = p.c["cs"]

                @blk.sync
                def _(sy, j=j):
                    p.wait(sy, "qs", p.c["qs"])
                    if j >= 1:
                        p.wait(sy, "cv", p.c["cv"])
                        p.wait(sy, "ct", p.c["ct"])
                    for t_i in range(DT_I):
                        p.inc(sy.dma_start(out=vv[:, t_i * TC:(t_i + 1) * TC],
                                           in_=v_d[:, t_i * T + j * TC: t_i * T + j * TC + TC]),
                              "qs", 16)
                    for tt in range(NTT):
                        p.inc(sy.dma_start(out=x3[:, tt * DM:(tt + 1) * DM],
                                           in_=x_d[j * TC + tt * 128: j * TC + (tt + 1) * 128, :]),
                              "qs", 16)
                qs_v = p.c["qs"]

                @blk.vector
                def _(v, j=j, qs_v=qs_v):
                    p.wait(v, "qs", qs_v)
                    li = None
                    for tt in range(NTT):
                        g = j * NTT + tt
                        li = v.tensor_scalar(out=x3[:, tt * DM:(tt + 1) * DM],
                                             in0=x3[:, tt * DM:(tt + 1) * DM],
                                             scalar1=mv_all[:, 2 * g:2 * g + 1],
                                             scalar2=rstd_all[:, g:g + 1],
                                             op0=ALU.subtract, op1=ALU.mult)
                    p.inc(li, "cv")

                qw_base3 = (p.c["qw0"], p.c["qw1"])

                @blk.sync
                def _(sy, j=j, ctb=ctb):
                    for k in range(DT_I):
                        p.wait(sy, "ct", ctb + 8 + k - 1)
                        p.inc(sy.dma_start(out=wob[k % 2][:],
                                           in_=woT_d[k * 128:(k + 1) * 128, :]),
                              f"qw{k % 2}", 16)

                @blk.tensor
                def _(t, j=j, cvb=cvb, csb=csb, qw_base3=qw_base3):
                    p.wait(t, "cv", cvb + 1)
                    for grp in range(DT_M):
                        if grp >= 2:
                            p.wait(t, "cv", cvb + 1 + (grp - 1))
                        bank = pb[grp % 2]
                        li = None
                        for i in range(4):
                            idx = grp * 4 + i
                            tt, c = idx // DT_M, idx % DT_M
                            li = t.transpose(bank[:, i * 128:(i + 1) * 128],
                                             x3[:, tt * DM + c * 128: tt * DM + (c + 1) * 128],
                                             iden[:])
                        p.inc(li, "ct")
                    p.wait(t, "cv", cvb + 9)
                    for k in range(DT_I):
                        p.wait(t, f"qw{k % 2}", qw_base3[k % 2] + 16 * (k // 2 + 1))
                        li = None
                        for o in range(DT_M):
                            li = t.matmul(pb[o][:, 0:TC], wob[k % 2][:, o * 128:(o + 1) * 128],
                                          vv[:, k * TC:(k + 1) * TC],
                                          start=(k == 0), stop=(k == DT_I - 1))
                        p.inc(li, "ct")
                    p.wait(t, "cs", csb + 1)
                    li = None
                    for o in range(DT_M):
                        for k in range(DT_M):
                            li = t.matmul(pb[o][:, 0:TC], wt[k][:, o * 128:(o + 1) * 128],
                                          xnf3[:, k * TC:(k + 1) * TC],
                                          start=(k == 0), stop=(k == DT_M - 1))
                    p.inc(li, "ct")
                    p.wait(t, "cv", cvb + 10)
                    for grp in range(DT_M):
                        if grp >= 2:
                            p.wait(t, "cv", cvb + 10 + (grp - 1))
                        bank = pb[grp % 2]
                        li = None
                        for i in range(4):
                            idx = grp * 4 + i
                            c, tt = idx // NTT, idx % NTT
                            li = t.transpose(bank[:, i * 128:(i + 1) * 128],
                                             ss[:, c * TC + tt * 128: c * TC + (tt + 1) * 128],
                                             iden[:])
                        p.inc(li, "ct")

                @blk.vector
                def _(v, j=j, ctb=ctb):
                    for grp in range(DT_M):
                        p.wait(v, "ct", ctb + grp + 1)
                        bank = pb[grp % 2]
                        li = None
                        for i in range(4):
                            idx = grp * 4 + i
                            tt, c = idx // DT_M, idx % DT_M
                            li = v.tensor_copy(out=xnf3[:, c * TC + tt * 128: c * TC + (tt + 1) * 128],
                                               in_=bank[:, i * 128:(i + 1) * 128])
                        p.inc(li, "cv")

                @blk.scalar
                def _(s, j=j, ctb=ctb):
                    p.wait(s, "ct", ctb + 8 + 16)
                    li = None
                    for o in range(DT_M):
                        li = s.activation(out=ss[:, o * TC:(o + 1) * TC], in_=pb[o][:, 0:TC],
                                          func=AF.Identity, bias=cbout[:, o:o + 1])
                    p.inc(li, "cs")
                    p.wait(s, "ct", ctb + 8 + 16 + 1)
                    li = None
                    for o in range(DT_M):
                        li = s.activation(out=sg[:, o * TC:(o + 1) * TC], in_=pb[o][:, 0:TC],
                                          func=AF.Silu, bias=cbffn[:, o:o + 1], scale=alph[:, 0:1])
                    p.inc(li, "cs")

                @blk.vector
                def _(v, j=j, csb=csb):
                    p.wait(v, "cs", csb + 2)
                    li = None
                    for o in range(DT_M):
                        li = v.tensor_tensor(out=ss[:, o * TC:(o + 1) * TC],
                                             in0=ss[:, o * TC:(o + 1) * TC],
                                             in1=sg[:, o * TC:(o + 1) * TC], op=ALU.mult)
                    p.inc(li, "cv")

                @blk.sync
                def _(sy, j=j, ctb=ctb):
                    p.wait(sy, "ct", ctb + 8)
                    for tt in range(NTT):
                        p.inc(sy.dma_start(out=x3[:, tt * DM:(tt + 1) * DM],
                                           in_=x_d[j * TC + tt * 128: j * TC + (tt + 1) * 128, :]),
                              "qs", 16)
                qs_x3 = p.c["qs"]

                @blk.vector
                def _(v, j=j, ctb=ctb, qs_x3=qs_x3):
                    p.wait(v, "qs", qs_x3)
                    for grp in range(DT_M):
                        p.wait(v, "ct", ctb + 8 + 16 + 1 + grp + 1)
                        bank = pb[grp % 2]
                        li = None
                        for i in range(4):
                            idx = grp * 4 + i
                            c, tt = idx // NTT, idx % NTT
                            li = v.tensor_tensor(out=x3[:, tt * DM + c * 128: tt * DM + (c + 1) * 128],
                                                 in0=x3[:, tt * DM + c * 128: tt * DM + (c + 1) * 128],
                                                 in1=bank[:, i * 128:(i + 1) * 128], op=ALU.add)
                        p.inc(li, "cv")

                @blk.sync
                def _(sy, j=j, cvb=cvb):
                    p.wait(sy, "cv", cvb + 18)
                    for tt in range(NTT):
                        p.inc(sy.dma_start(out=out_d[j * TC + tt * 128: j * TC + (tt + 1) * 128, :],
                                           in_=x3[:, tt * DM:(tt + 1) * DM]), "qs", 16)

        @blk.gpsimd
        def _(g):
            p.wait(g, "qs", p.c["qs"])
            p.wait(g, "qg", p.c["qg"])
            p.wait(g, "qt0", p.c["qt0"])
            p.wait(g, "qt1", p.c["qt1"])
            p.wait(g, "qb0", p.c["qb0"])
            p.wait(g, "qb1", p.c["qb1"])
            p.wait(g, "qbs", p.c["qbs"])
            p.wait(g, "qr", p.c["qr"])
            p.wait(g, "qv", p.c["qv"])

    return nc


def kernel(L0_attn_x, L0_attn_W_in, L0_attn_b_in, L0_attn_W_conv, L0_attn_b_conv,
           L0_attn_W_x, L0_attn_b_x, L0_attn_W_out, L0_attn_b_out, L0_attn_A_log,
           L0_attn_D, L0_ffn_weight, L0_ffn_alpha, L0_ffn_bias):
    f32 = np.float32
    c_ = np.ascontiguousarray
    x = c_(np.asarray(L0_attn_x, f32))

    winT = c_(np.asarray(L0_attn_W_in, f32).T)
    wxT = c_(np.asarray(L0_attn_W_x, f32).T)
    woT = c_(np.asarray(L0_attn_W_out, f32).T)
    wfT = c_(np.asarray(L0_ffn_weight, f32).T)
    b_in = np.asarray(L0_attn_b_in, f32)
    b_x = np.asarray(L0_attn_b_x, f32)

    def col128(vec, nt):
        return c_(vec.reshape(nt, 128).T)

    cb_in_xi = col128(b_in[:DI], DT_I)
    cb_in_res = col128(b_in[DI:], DT_I)
    cw_conv = c_(np.asarray(L0_attn_W_conv, f32)[:, 0, :].reshape(DT_I, 128, 4)
                 .transpose(1, 0, 2).reshape(128, DT_I * 4))
    cb_conv = col128(np.asarray(L0_attn_b_conv, f32), DT_I)
    cb_x_d = col128(b_x[:DI], DT_I)
    cb_x_bc = c_(b_x[DI:].reshape(32, 1))
    cb_out = col128(np.asarray(L0_attn_b_out, f32), DT_M)
    c_d = col128(np.asarray(L0_attn_D, f32), DT_I)
    c_a_log = c_(np.asarray(L0_attn_A_log, f32).reshape(DT_I, 128, NS)
                 .transpose(1, 0, 2).reshape(128, DT_I * NS))
    c_alpha = c_(np.broadcast_to(np.asarray(L0_ffn_alpha, f32).reshape(1, 1), (128, 1)))
    cb_ffn = col128(np.asarray(L0_ffn_bias, f32), DT_M)
    iden = np.eye(128, dtype=f32)
    zeros3 = np.zeros((3, DM), f32)
    zmask = np.zeros((128, 1), f32)

    shared = dict(w_in_t=winT, w_x_t=wxT, w_out_t=woT, w_ffn_t=wfT,
                  cb_in_xi=cb_in_xi, cb_in_res=cb_in_res, cw_conv=cw_conv,
                  cb_conv=cb_conv, cb_x_d=cb_x_d, cb_x_bc=cb_x_bc, cb_out=cb_out,
                  c_d=c_d, c_a_log=c_a_log, c_alpha=c_alpha, cb_ffn=cb_ffn, iden=iden)

    in_maps = []
    for c in range(8):
        seq = c % 4
        m = dict(shared)
        m["x"] = c_(x[seq])
        m["x_halo3"] = zeros3
        m["halo_mask"] = zmask
        in_maps.append(m)

    nc = build_nc()
    res = run_bass_kernel_spmd(nc, in_maps, core_ids=list(range(8)))
    out = np.empty((4, 4096, DM), f32)
    for seq in range(4):
        out[seq] = res.results[seq]["out"]
    return out


# version 20
# speedup vs baseline: 1.1514x; 1.0121x over previous
import contextlib
import numpy as np
import concourse.bass as bass
import concourse.mybir as mybir
from concourse.bass_utils import run_bass_kernel_spmd

F32 = mybir.dt.float32
BF16 = mybir.dt.bfloat16
AF = mybir.ActivationFunctionType
ALU = mybir.AluOpType
AX = mybir.AxisListType

DM = 1024
DI = 2048
NS = 16
DT_M = DM // 128
DT_I = DI // 128
EPS = 1e-5

import os as _os
HALF = False
T = int(_os.environ.get("MK_T", "0")) or (2048 if HALF else 4096)
TC = 512
NCH = T // TC
NTT = TC // 128
XS = TC + 3


class P:
    def __init__(self, nc, es):
        self.nc = nc
        self.s = {}
        self.c = {}
        for name in ("qs", "qg", "ct", "cv", "cs", "qw0", "qw1", "qt0", "qt1", "qb0", "qb1", "qbs", "qr", "qv", "qh0", "qh1"):
            self.s[name] = es.enter_context(nc.semaphore(name))
            self.c[name] = 0

    def inc(self, instr, name, k=1):
        instr.then_inc(self.s[name], k)
        self.c[name] += k
        return self.c[name]

    def wait(self, eng, name, val=None):
        if val is None:
            val = self.c[name]
        if val > 0:
            eng.wait_ge(self.s[name], val)

    def edge(self, eng, name):
        self.wait(eng, name, self.c[name])


def build_nc():
    nc = bass.Bass("TRN2", target_bir_lowering=False)

    x_d = nc.dram_tensor("x", [T, DM], F32, kind="ExternalInput").ap()
    xh_d = nc.dram_tensor("x_halo3", [3, DM], F32, kind="ExternalInput").ap()
    hm_d = nc.dram_tensor("halo_mask", [128, 1], F32, kind="ExternalInput").ap()
    winT_d = nc.dram_tensor("w_in_t", [DM, 2 * DI], F32, kind="ExternalInput").ap()
    wxT_d = nc.dram_tensor("w_x_t", [DI, DI + 2 * NS], F32, kind="ExternalInput").ap()
    woT_d = nc.dram_tensor("w_out_t", [DI, DM], F32, kind="ExternalInput").ap()
    wfT_d = nc.dram_tensor("w_ffn_t", [DM, DM], F32, kind="ExternalInput").ap()
    cbixi_d = nc.dram_tensor("cb_in_xi", [128, DT_I], F32, kind="ExternalInput").ap()
    cbires_d = nc.dram_tensor("cb_in_res", [128, DT_I], F32, kind="ExternalInput").ap()
    cwconv_d = nc.dram_tensor("cw_conv", [128, DT_I * 4], F32, kind="ExternalInput").ap()
    cbconv_d = nc.dram_tensor("cb_conv", [128, DT_I], F32, kind="ExternalInput").ap()
    cbxd_d = nc.dram_tensor("cb_x_d", [128, DT_I], F32, kind="ExternalInput").ap()
    cbxbc_d = nc.dram_tensor("cb_x_bc", [32, 1], F32, kind="ExternalInput").ap()
    cbout_d = nc.dram_tensor("cb_out", [128, DT_M], F32, kind="ExternalInput").ap()
    cd_d = nc.dram_tensor("c_d", [128, DT_I], F32, kind="ExternalInput").ap()
    calog_d = nc.dram_tensor("c_a_log", [128, DT_I * NS], F32, kind="ExternalInput").ap()
    alpha_d = nc.dram_tensor("c_alpha", [128, 1], F32, kind="ExternalInput").ap()
    cbffn_d = nc.dram_tensor("cb_ffn", [128, DT_M], F32, kind="ExternalInput").ap()
    iden_d = nc.dram_tensor("iden", [128, 128], F32, kind="ExternalInput").ap()
    out_d = nc.dram_tensor("out", [T, DM], F32, kind="ExternalOutput").ap()
    xi2_d = nc.dram_tensor("xi2_s", [128, DT_I * T], F32, kind="Internal").ap()
    sres_d = nc.dram_tensor("sres_s", [128, DT_I * T], F32, kind="Internal").ap()
    v_d = nc.dram_tensor("v_s", [128, DT_I * T], F32, kind="Internal").ap()
    bcd_d = nc.dram_tensor("bc_s", [32, T], F32, kind="Internal").ap()

    es = contextlib.ExitStack()
    with es:
        p = P(nc, es)
        sb = lambda name, shape: es.enter_context(nc.sbuf_tensor(name, shape, F32))

        cbixi = sb("cbixi", [128, DT_I])
        cbires = sb("cbires", [128, DT_I])
        cwconv = sb("cwconv", [128, DT_I * 4])
        cbconv = sb("cbconv", [128, DT_I])
        cbxd = sb("cbxd", [128, DT_I])
        cbxbc = sb("cbxbc", [32, 1])
        cbout = sb("cbout", [128, DT_M])
        cdv = sb("cdv", [128, DT_I])
        ca = sb("ca", [128, DT_I * NS])
        alph = sb("alph", [128, 1])
        cbffn = sb("cbffn", [128, DT_M])
        iden = sb("iden_sb", [128, 128])
        hmask = sb("hmask", [128, 1])
        epsb = sb("epsb", [128, 1])
        ones_row = sb("ones_row", [1, 128])
        ones_col = sb("ones_col", [128, 1])
        mv_all = sb("mv_all", [128, 2 * (T // 128)])
        rstd_all = sb("rstd_all", [128, T // 128])
        hstate = sb("hstate", [128, DT_I * NS])
        xihalo = sb("xihalo", [128, DT_I * 3])

        pb = [es.enter_context(nc.psum_tensor(f"pb{i}", [128, 512], F32)) for i in range(8)]

        blk = es.enter_context(nc.Block())

        @blk.gpsimd
        def _(g):
            for dst, src in ((cbixi, cbixi_d), (cbires, cbires_d), (cwconv, cwconv_d),
                             (cbconv, cbconv_d), (cbxd, cbxd_d), (cbxbc, cbxbc_d),
                             (cbout, cbout_d), (cdv, cd_d), (ca, calog_d),
                             (alph, alpha_d), (cbffn, cbffn_d), (iden, iden_d),
                             (hmask, hm_d)):
                p.inc(g.dma_start(out=dst[:], in_=src[:]), "qg", 16)

        @blk.vector
        def _(v):
            p.wait(v, "qg")
            v.memset(ones_row[:], 1.0)
            v.memset(epsb[:], EPS)
            v.memset(ones_col[:], 1.0)
            v.memset(hstate[:], 0.0)
            p.inc(v.memset(xihalo[:], 0.0), "cv")

        @blk.scalar
        def _(s):
            p.wait(s, "qg")
            p.wait(s, "cv")
            li = None
            for t in range(DT_I):
                li = s.activation(out=ca[:, t * NS:(t + 1) * NS],
                                  in_=ca[:, t * NS:(t + 1) * NS], func=AF.Exp)
            p.inc(li, "cs")

        @blk.vector
        def _(v):
            p.wait(v, "cs")
            li = None
            for t in range(DT_I):
                li = v.tensor_scalar(out=ca[:, t * NS:(t + 1) * NS],
                                     in0=ca[:, t * NS:(t + 1) * NS],
                                     scalar1=-1.0, scalar2=None, op0=ALU.mult)
            p.inc(li, "cv")

        with contextlib.ExitStack() as es1:
            sb1 = lambda name, shape: es1.enter_context(nc.sbuf_tensor(name, shape, F32))
            winh = [es1.enter_context(nc.sbuf_tensor(f"winh{k}", [128, 2 * DI], BF16))
                    for k in range(DT_M)]
            winl = [es1.enter_context(nc.sbuf_tensor(f"winl{k}", [128, 2 * DI], BF16))
                    for k in range(DT_M)]
            x1 = sb1("p1_x", [128, NTT * DM])
            bns = sb1("p1_bns", [128, 64])
            xnfh = es1.enter_context(nc.sbuf_tensor("p1_xnfh", [128, DT_M * TC], BF16))
            xnfl = es1.enter_context(nc.sbuf_tensor("p1_xnfl", [128, DT_M * TC], BF16))
            xi = sb1("p1_xi", [128, DT_I * XS])
            cvs = [sb1(f"p1_cv{i}", [128, TC]) for i in range(2)]
            stg = [sb1(f"p1_stg{i}", [128, TC]) for i in range(2)]

            @blk.sync
            def _(sy):
                p.wait(sy, "cv", p.c["cv"])
                p.wait(sy, "cs", p.c["cs"])
                p.inc(sy.dma_start(out=x1[:3, 0:DM], in_=xh_d[:]), "qs", 16)
            qs_halo = p.c["qs"]

            cv_wb = p.c["cv"]

            @blk.sync
            def _(sy):
                p.wait(sy, "qs", p.c["qs"])
                for k in range(DT_M):
                    if k >= 2:
                        p.wait(sy, "cv", cv_wb + 2 * (k - 1))
                    p.inc(sy.dma_start(out=xi[:, (k % 2) * 4096:(k % 2) * 4096 + 4096],
                                       in_=winT_d[k * 128:(k + 1) * 128, :]),
                          f"qh{k % 2}", 16)

            @blk.vector
            def _(v):
                for k in range(DT_M):
                    tmp = xi[:, (k % 2) * 4096:(k % 2) * 4096 + 4096]
                    p.wait(v, f"qh{k % 2}", 16 * (k // 2 + 1))
                    li = v.tensor_copy(out=winh[k][:], in_=tmp)
                    p.inc(li, "cv")
                    p.edge(v, "cv")
                    li = v.tensor_tensor(out=winl[k][:], in0=tmp, in1=winh[k][:],
                                         op=ALU.subtract)
                    p.inc(li, "cv")
            qs_win = p.c["qs"]

            @blk.vector
            def _(v):
                p.wait(v, "qs", qs_halo)
                p.wait(v, "cs")
                v.bn_stats(out=bns[:3, 0:6], in_=x1[:3, 0:512])
                li = v.bn_stats(out=bns[:3, 6:12], in_=x1[:3, 512:1024])
                p.inc(li, "cv")
                p.edge(v, "cv")
                li = v.bn_aggr(out=bns[:3, 12:14], in_=bns[:3, 0:12])
                p.inc(li, "cv")
            cv_halo_stats = p.c["cv"]

            @blk.scalar
            def _(s):
                p.wait(s, "cv", cv_halo_stats)
                li = s.activation(out=bns[:3, 14:15], in_=bns[:3, 13:14], func=AF.Ln,
                                  bias=epsb[:3, 0:1])
                p.inc(li, "cs")
                p.edge(s, "cs")
                li = s.activation(out=bns[:3, 14:15], in_=bns[:3, 14:15], func=AF.Exp, scale=-0.5)
                p.inc(li, "cs")
            cs_halo_rstd = p.c["cs"]

            @blk.vector
            def _(v):
                p.wait(v, "cs", cs_halo_rstd)
                li = v.tensor_scalar(out=x1[:3, 0:DM], in0=x1[:3, 0:DM],
                                     scalar1=bns[:3, 12:13], scalar2=bns[:3, 14:15],
                                     op0=ALU.subtract, op1=ALU.mult)
                p.inc(li, "cv")
            cv_halo_xn = p.c["cv"]

            @blk.tensor
            def _(t):
                p.wait(t, "cv", cv_halo_xn)
                p.wait(t, "qs", qs_win)
                li = None
                for c in range(DT_M):
                    li = t.transpose(pb[0][:, c * 3:c * 3 + 3], x1[:3, c * 128:(c + 1) * 128],
                                     iden[:3, 0:3])
                p.inc(li, "ct")
            ct_halo_tr = p.c["ct"]

            @blk.vector
            def _(v):
                p.wait(v, "ct", ct_halo_tr)
                p.inc(v.tensor_copy(out=xnfh[:, 0:DT_M * 3], in_=pb[0][:, 0:DT_M * 3]),
                      "cv")
                p.edge(v, "cv")
                p.inc(v.tensor_tensor(out=xnfl[:, 0:DT_M * 3], in0=pb[0][:, 0:DT_M * 3],
                                      in1=xnfh[:, 0:DT_M * 3], op=ALU.subtract), "cv")
            cv_halo_evac = p.c["cv"]

            cs_hb = p.c["cs"]

            @blk.tensor
            def _(t):
                p.wait(t, "cv", cv_halo_evac)
                for o in range(DT_I):
                    if o >= 2:
                        p.wait(t, "cs", cs_hb + o - 1)
                    dst = pb[1] if o % 2 == 0 else pb[2]
                    li = None
                    for k in range(DT_M):
                        for wm, xm, tm in ((winh, xnfh, 0), (winh, xnfl, 1),
                                           (winl, xnfh, 2)):
                            li = t.matmul(dst[:, 0:3], wm[k][:, o * 128:(o + 1) * 128],
                                          xm[:, k * 3:k * 3 + 3],
                                          start=(k == 0 and tm == 0),
                                          stop=(k == DT_M - 1 and tm == 2))
                    p.inc(li, "ct")
            ct_halo_mm = p.c["ct"]

            @blk.scalar
            def _(s):
                for o in range(DT_I):
                    src = pb[1] if o % 2 == 0 else pb[2]
                    p.wait(s, "ct", ct_halo_mm - DT_I + o + 1)
                    li = s.activation(out=xihalo[:, o * 3:(o + 1) * 3], in_=src[:, 0:3],
                                      func=AF.Identity, bias=cbixi[:, o:o + 1])
                    p.inc(li, "cs")
            cs_halo = p.c["cs"]

            @blk.vector
            def _(v):
                p.wait(v, "cs", cs_halo)
                li = v.tensor_scalar(out=xihalo[:], in0=xihalo[:], scalar1=hmask[:, 0:1],
                                     scalar2=None, op0=ALU.mult)
                p.inc(li, "cv")
                p.edge(v, "cv")
                li = None
                for t in range(DT_I):
                    li = v.tensor_copy(out=xi[:, t * XS:t * XS + 3],
                                       in_=xihalo[:, t * 3:(t + 1) * 3])
                p.inc(li, "cv")

            ct_prev_tr_end = p.c["ct"]
            for j in range(NCH):
                qs_x = p.c["qs"] + 16 * NTT
                cvb = p.c["cv"]
                ctb = p.c["ct"]
                csb = p.c["cs"]
                qt_b = {0: p.c["qt0"], 1: p.c["qt1"]}

                @blk.sync
                def _(sy, j=j):
                    p.wait(sy, "ct", ct_prev_tr_end)
                    for tt in range(NTT):
                        p.inc(sy.dma_start(out=x1[:, tt * DM:(tt + 1) * DM],
                                           in_=x_d[j * TC + tt * 128: j * TC + (tt + 1) * 128, :]),
                              "qs", 16)

                @blk.vector
                def _(v, j=j, csb=csb):
                    p.wait(v, "qs", qs_x)
                    li = None
                    for tt in range(NTT):
                        v.bn_stats(out=bns[:, tt * 12:tt * 12 + 6],
                                   in_=x1[:, tt * DM:tt * DM + 512])
                        li = v.bn_stats(out=bns[:, tt * 12 + 6:tt * 12 + 12],
                                        in_=x1[:, tt * DM + 512:tt * DM + 1024])
                    p.inc(li, "cv")
                    p.edge(v, "cv")
                    li = None
                    for tt in range(NTT):
                        g = j * NTT + tt
                        li = v.bn_aggr(out=mv_all[:, 2 * g:2 * g + 2],
                                       in_=bns[:, tt * 12:tt * 12 + 12])
                    p.inc(li, "cv")
                    p.wait(v, "cs", csb + 2)
                    li = None
                    for tt in range(NTT):
                        g = j * NTT + tt
                        li = v.tensor_scalar(out=x1[:, tt * DM:(tt + 1) * DM],
                                             in0=x1[:, tt * DM:(tt + 1) * DM],
                                             scalar1=mv_all[:, 2 * g:2 * g + 1],
                                             scalar2=rstd_all[:, g:g + 1],
                                             op0=ALU.subtract, op1=ALU.mult)
                    p.inc(li, "cv")

                @blk.scalar
                def _(s, j=j, cvb=cvb):
                    p.wait(s, "cv", cvb + 2)
                    li = None
                    for tt in range(NTT):
                        g = j * NTT + tt
                        li = s.activation(out=rstd_all[:, g:g + 1],
                                          in_=mv_all[:, 2 * g + 1:2 * g + 2],
                                          func=AF.Ln, bias=epsb[:, 0:1])
                    p.inc(li, "cs")
                    p.edge(s, "cs")
                    li = None
                    for tt in range(NTT):
                        g = j * NTT + tt
                        li = s.activation(out=rstd_all[:, g:g + 1],
                                          in_=rstd_all[:, g:g + 1],
                                          func=AF.Exp, scale=-0.5)
                    p.inc(li, "cs")

                @blk.tensor
                def _(t, j=j, cvb=cvb, csb=csb):
                    p.wait(t, "cv", cvb + 3)
                    for grp in range(DT_M):
                        if grp >= 2:
                            p.wait(t, "cv", cvb + 5 + 2 * (grp - 2))
                        bank = pb[grp % 2]
                        li = None
                        for i in range(4):
                            idx = grp * 4 + i
                            tt, c = idx // DT_M, idx % DT_M
                            li = t.transpose(bank[:, i * 128:(i + 1) * 128],
                                             x1[:, tt * DM + c * 128: tt * DM + (c + 1) * 128],
                                             iden[:])
                        p.inc(li, "ct")
                    p.wait(t, "cv", cvb + 19)
                    for o in range(2 * DI // 128):
                        if o >= 2:
                            p.wait(t, "cs", csb + 2 + o - 1)
                        bank = pb[2 + o % 2]
                        li = None
                        for k in range(DT_M):
                            for wm, xm, tm in ((winh, xnfh, 0), (winh, xnfl, 1),
                                               (winl, xnfh, 2)):
                                li = t.matmul(bank[:], wm[k][:, o * 128:(o + 1) * 128],
                                              xm[:, k * TC:(k + 1) * TC],
                                              start=(k == 0 and tm == 0),
                                              stop=(k == DT_M - 1 and tm == 2))
                        p.inc(li, "ct")
                ct_prev_tr_end = ctb + 8

                @blk.vector
                def _(v, j=j, ctb=ctb):
                    for grp in range(DT_M):
                        p.wait(v, "ct", ctb + grp + 1)
                        bank = pb[grp % 2]
                        li = None
                        sls = []
                        for i in range(4):
                            idx = grp * 4 + i
                            tt, c = idx // DT_M, idx % DT_M
                            sls.append((slice(c * TC + tt * 128, c * TC + (tt + 1) * 128), i))
                        for sl, i in sls:
                            li = v.tensor_copy(out=xnfh[:, sl],
                                               in_=bank[:, i * 128:(i + 1) * 128])
                        p.inc(li, "cv")
                        p.edge(v, "cv")
                        li = None
                        for sl, i in sls:
                            li = v.tensor_tensor(out=xnfl[:, sl],
                                                 in0=bank[:, i * 128:(i + 1) * 128],
                                                 in1=xnfh[:, sl], op=ALU.subtract)
                        p.inc(li, "cv")

                qt_store_count = {0: 0, 1: 0}
                silu_qt_wait = {}
                res_qt_wait = {}
                for oo in range(DT_I):
                    s_ = oo % 2
                    res_qt_wait[oo] = qt_b[s_] + 16 * qt_store_count[s_]
                    qt_store_count[s_] += 1
                for t_i in range(DT_I):
                    s_ = t_i % 2
                    silu_qt_wait[t_i] = qt_b[s_] + 16 * qt_store_count[s_]
                    qt_store_count[s_] += 1

                @blk.scalar
                def _(s, j=j, ctb=ctb, cvb=cvb):
                    for o in range(2 * DI // 128):
                        p.wait(s, "ct", ctb + 8 + o + 1)
                        bank = pb[2 + o % 2]
                        if o < DI // 128:
                            li = s.activation(out=xi[:, o * XS + 3: o * XS + 3 + TC], in_=bank[:],
                                              func=AF.Identity, bias=cbixi[:, o:o + 1])
                        else:
                            oo = o - DI // 128
                            p.wait(s, f"qt{oo % 2}", res_qt_wait[oo])
                            li = s.activation(out=stg[oo % 2][:], in_=bank[:],
                                              func=AF.Silu, bias=cbires[:, oo:oo + 1])
                        p.inc(li, "cs")
                    for t_i in range(DT_I):
                        cg = t_i // 2
                        p.wait(s, "cv", cvb + 19 + 5 * cg + 4)
                        p.wait(s, f"qt{t_i % 2}", silu_qt_wait[t_i])
                        li = s.activation(out=stg[t_i % 2][:], in_=cvs[t_i % 2][:],
                                          func=AF.Silu, bias=cbconv[:, t_i:t_i + 1])
                        p.inc(li, "cs")

                @blk.vector
                def _(v, j=j, csb=csb):
                    for cg in range(DT_I // 2):
                        ts_grp = (2 * cg, 2 * cg + 1)
                        p.wait(v, "cs", csb + 2 + 2 * cg + 2)
                        if cg >= 1:
                            p.wait(v, "cs", csb + 2 + 32 + 2 * cg)
                        li = None
                        for t_i in ts_grp:
                            base = t_i * XS
                            li = v.tensor_scalar(out=cvs[t_i % 2][:], in0=xi[:, base:base + TC],
                                                 scalar1=cwconv[:, t_i * 4:t_i * 4 + 1],
                                                 scalar2=None, op0=ALU.mult)
                        p.inc(li, "cv")
                        for k in (1, 2, 3):
                            p.edge(v, "cv")
                            li = None
                            for t_i in ts_grp:
                                base = t_i * XS
                                li = v.scalar_tensor_tensor(out=cvs[t_i % 2][:],
                                                            in0=xi[:, base + k:base + k + TC],
                                                            scalar=cwconv[:, t_i * 4 + k:t_i * 4 + k + 1],
                                                            in1=cvs[t_i % 2][:],
                                                            op0=ALU.mult, op1=ALU.add)
                            p.inc(li, "cv")
                        p.edge(v, "cv")
                        li = None
                        for t_i in ts_grp:
                            base = t_i * XS
                            li = v.tensor_copy(out=xi[:, base:base + 3],
                                               in_=xi[:, base + TC:base + TC + 3])
                        p.inc(li, "cv")

                @blk.sync
                def _(sy, j=j, csb=csb):
                    for oo in range(DT_I):
                        p.wait(sy, "cs", csb + 2 + 16 + oo + 1)
                        p.inc(sy.dma_start(out=sres_d[:, oo * T + j * TC: oo * T + j * TC + TC],
                                           in_=stg[oo % 2][:]), f"qt{oo % 2}", 16)
                    for t_i in range(DT_I):
                        p.wait(sy, "cs", csb + 2 + 32 + t_i + 1)
                        p.inc(sy.dma_start(out=xi2_d[:, t_i * T + j * TC: t_i * T + j * TC + TC],
                                           in_=stg[t_i % 2][:]), f"qt{t_i % 2}", 16)

        TP = 256
        NC2 = T // TP
        with contextlib.ExitStack() as es2:
            sb2 = lambda name, shape: es2.enter_context(nc.sbuf_tensor(name, shape, F32))
            wxb = [sb2(f"wx{i}", [128, 12 * 128]) for i in range(2)]
            bcb = [sb2(f"p2_bcb{i}", [128, 1, 2 * TP]) for i in range(2)]
            xi2 = sb2("p2_xi2", [128, DT_I * TP])
            sdel = [sb2(f"p2_del{i}", [128, DT_I * TP]) for i in range(2)]
            sdtx = [sb2(f"p2_dtx{i}", [128, DT_I * TP]) for i in range(2)]
            sy_ = sb2("p2_y", [128, DT_I * TP])
            srs = sb2("p2_srs", [128, DT_I * TP])
            h_all = sb2("p2_h", [128, 8 * TP])
            da = sb2("p2_da", [128, DT_I * TP])
            sp12 = sb2("p2_sp", [128, 17 * TP])
            dbx = [sb2(f"p2_dbx{i}", [128, 4 * TP]) for i in range(2)]
            zt = [sb2(f"p2_z{i}", [128, 4 * TP]) for i in range(2)]
            bc = sb2("p2_bc", [32, TP])

            NO = (DI + 2 * NS + 127) // 128
            GRPS = [list(range(0, 12)), list(range(12, 17))]

            def da_off(jj, n):
                if jj == NC2 - 1:
                    return 16 * n
                return 16 * n + (0 if n < 12 else 3 if n < 14 else 6)

            def emit_dA_part(jj, ns, csA, cvb_jj, qbs_jj, prevcv):
                @blk.scalar
                def _(s, jj=jj, ns=ns, csA=csA, cvb_jj=cvb_jj, qbs_jj=qbs_jj,
                      prevcv=prevcv):
                    cvs0 = cvb_jj + 5
                    if ns[0] == 0:
                        p.wait(s, "cv", cvb_jj + 4)
                        p.wait(s, "qbs", qbs_jj)
                    for n in ns:
                        if n >= 2:
                            p.wait(s, "cv", cvs0 + 16 * (n - 1))
                        elif prevcv is not None:
                            p.wait(s, "cv", prevcv + 5 + 16 * (15 + n))
                        p.inc(s.dma_start(
                            out=bcb[n % 2][:, 0, 0:TP],
                            in_=bcd_d[n, jj * TP:(jj + 1) * TP].partition_broadcast(128)),
                            f"qb{n % 2}", 16)
                        p.inc(s.dma_start(
                            out=bcb[n % 2][:, 0, TP:2 * TP],
                            in_=bcd_d[NS + n, jj * TP:(jj + 1) * TP].partition_broadcast(128)),
                            f"qb{n % 2}", 16)
                        for t_i in range(DT_I):
                            if n > 0:
                                p.wait(s, "cs", csA + da_off(jj, n - 1) + t_i + 1)
                                p.wait(s, "cv", cvs0 + 16 * (n - 1) + 4 * (t_i // 4) + 2)
                            li = s.activation(out=da[:, t_i * TP:(t_i + 1) * TP],
                                              in_=sdel[jj % 2][:, t_i * TP:(t_i + 1) * TP],
                                              func=AF.Exp,
                                              scale=ca[:, t_i * NS + n:t_i * NS + n + 1])
                            p.inc(li, "cs")

            prev = None
            prev_qbs = 0
            prevcv_prev = None
            for j in range(NC2):
                ctb = p.c["ct"]
                csb = p.c["cs"]
                cvb = p.c["cv"]

                @blk.sync
                def _(sy, j=j, prev=prev):
                    if j == 0:
                        p.wait(sy, "qs", p.c["qs"])
                        p.wait(sy, "qt0", p.c["qt0"])
                        p.wait(sy, "qt1", p.c["qt1"])
                        p.wait(sy, "cv", p.c["cv"])
                        p.wait(sy, "cs", p.c["cs"])
                        p.wait(sy, "ct", p.c["ct"])
                    else:
                        (pq, pct, pcs, pcv) = prev
                        p.wait(sy, "cv", pcv + 5)
                        p.wait(sy, "ct", pct + 32)
                    for t_i in range(DT_I):
                        p.inc(sy.dma_start(out=xi2[:, t_i * TP:(t_i + 1) * TP],
                                           in_=xi2_d[:, t_i * T + j * TP: t_i * T + j * TP + TP]),
                              "qs", 16)
                qs_xi2 = p.c["qs"]


                qwb = (p.c["qw0"], p.c["qw1"])

                @blk.sync
                def _(sy, j=j, ctb=ctb):
                    for g in range(2):
                        c0 = 0 if g == 0 else 12 * 128
                        cw = 12 * 128 if g == 0 else (DI + 2 * NS - 12 * 128)
                        for k in range(DT_I):
                            p.wait(sy, "ct", ctb + 16 * g + k - 1)
                            p.inc(sy.dma_start(out=wxb[k % 2][:, 0:cw],
                                               in_=wxT_d[k * 128:(k + 1) * 128, c0:c0 + cw]),
                                  f"qw{k % 2}", 16)

                @blk.tensor
                def _(t, j=j, prev=prev, ctb=ctb, csb=csb, cvb=cvb, qwb=qwb,
                      qs_xi2=qs_xi2):
                    for g, grp in enumerate(GRPS):
                        if g == 0:
                            p.wait(t, "qs", qs_xi2)
                            if prev is not None:
                                p.wait(t, "cs", prev[2] + (228 if j >= 2 else 4))
                        else:
                            p.wait(t, "cs", csb + (193 if j >= 1 else 1))
                        for k in range(DT_I):
                            p.wait(t, f"qw{k % 2}",
                                   qwb[k % 2] + 16 * (8 * g + k // 2 + 1))
                            li = None
                            for bi, o in enumerate(grp):
                                w = 32 if o == NO - 1 else 128
                                lc = bi * 128
                                bank = pb[bi // 2]
                                half = (bi % 2) * TP
                                dst = (bank[:, half:half + TP] if w == 128
                                       else bank[:32, half:half + TP])
                                st_ = (k == 0 and bi % 2 == 0)
                                sp_ = (k == DT_I - 1 and (bi % 2 == 1
                                                          or bi == len(grp) - 1))
                                li = t.matmul(dst, wxb[k % 2][:, lc:lc + w],
                                              xi2[:, k * TP:(k + 1) * TP],
                                              start=st_, stop=sp_)
                            p.inc(li, "ct")

                @blk.sync
                def _(sy, j=j, csb=csb):
                    p.wait(sy, "cs", csb + (230 if j >= 1 else 6))
                    p.inc(sy.dma_start(out=bcd_d[:, j * TP:(j + 1) * TP], in_=bc[:, 0:TP]),
                          "qbs", 16)
                qbs_j = p.c["qbs"]
                @blk.sync
                def _(sy, j=j, prev=prev):
                    if j >= 1:
                        p.wait(sy, "cv", prev[3] + 262)
                    else:
                        p.wait(sy, "qt0", p.c["qt0"])
                        p.wait(sy, "qt1", p.c["qt1"])
                    for t_i in range(DT_I):
                        p.inc(sy.dma_start(out=srs[:, t_i * TP:(t_i + 1) * TP],
                                           in_=sres_d[:, t_i * T + j * TP: t_i * T + j * TP + TP]),
                              "qr", 16)
                qt0_sres = p.c["qr"]

                if prev is not None:
                    @blk.sync
                    def _(sy, j=j, prev=prev):
                        p.wait(sy, "cv", prev[3] + 262)
                        jp = j - 1
                        for t_i in range(DT_I):
                            p.inc(sy.dma_start(
                                out=v_d[:, t_i * T + jp * TP: t_i * T + jp * TP + TP],
                                in_=sy_[:, t_i * TP:(t_i + 1) * TP]), "qv", 16)

                daB = csb
                if prev is not None:
                    emit_dA_part(j - 1, list(range(0, 12)), daB, prev[3],
                                 prev_qbs, prevcv_prev)

                def emit_sp(g, j=j, prev=prev, ctb=ctb, csb=csb, cvb=cvb,
                            pqbs=prev_qbs):
                    @blk.scalar
                    def _(s, g=g):
                        grp = GRPS[g]
                        if g == 0:
                            p.wait(s, "cs", csb)
                            if prev is not None:
                                p.wait(s, "cv", prev[3] + 3)

                        p.wait(s, "ct", ctb + 16 * (g + 1))
                        dts = [(bi + 12 * g, o) for bi, o in enumerate(grp) if o < DI // 128]
                        li = None
                        for sl, o in dts:
                            lb = sl - 12 * g
                            bank = pb[lb // 2]
                            half = (lb % 2) * TP
                            li = s.activation(out=sp12[:, sl * TP:(sl + 1) * TP],
                                              in_=bank[:, half:half + TP],
                                              func=AF.Abs, bias=cbxd[:, o:o + 1])
                        for sl, o in dts:
                            lb = sl - 12 * g
                            bank = pb[lb // 2]
                            half = (lb % 2) * TP
                            li = s.activation(out=sdel[j % 2][:, o * TP:(o + 1) * TP],
                                              in_=bank[:, half:half + TP],
                                              func=AF.Relu, bias=cbxd[:, o:o + 1])
                        p.inc(li, "cs")
                        p.edge(s, "cs")
                        li = None
                        for sl, o in dts:
                            li = s.activation(out=sp12[:, sl * TP:(sl + 1) * TP],
                                              in_=sp12[:, sl * TP:(sl + 1) * TP],
                                              func=AF.Exp, scale=-1.0)
                        p.inc(li, "cs")
                        p.edge(s, "cs")
                        li = None
                        for sl, o in dts:
                            li = s.activation(out=sp12[:, sl * TP:(sl + 1) * TP],
                                              in_=sp12[:, sl * TP:(sl + 1) * TP],
                                              func=AF.Ln, bias=1.0)
                        if g == 1:
                            p.wait(s, "qbs", pqbs)
                            li = s.activation(out=bc[:, 0:TP], in_=pb[2][:32, 0:TP],
                                              func=AF.Identity, bias=cbxbc[:, 0:1])
                        p.inc(li, "cs")

                emit_sp(0)
                if prev is not None:
                    emit_dA_part(j - 1, [12, 13], daB, prev[3], prev_qbs,
                                 prevcv_prev)
                emit_sp(1)
                if prev is not None:
                    emit_dA_part(j - 1, [14, 15], daB, prev[3], prev_qbs,
                                 prevcv_prev)
                qb_j = (p.c["qb0"], p.c["qb1"])

                @blk.vector
                def _(v, j=j, prev=prev, ctb=ctb, csb=csb, cvb=cvb, qb_j=qb_j,
                      qt0_sres=qt0_sres, qt1_vst=p.c["qv"]):
                    for g, grp in enumerate(GRPS):
                        spb = csb + (192 if j >= 1 else 0)
                        p.wait(v, "cs", spb + (3 if g == 0 else
                                               (38 if j >= 1 else 6)))
                        dts = [(bi + 12 * g, o) for bi, o in enumerate(grp) if o < DI // 128]
                        li = None
                        for sl, o in dts:
                            li = v.tensor_tensor(out=sdel[j % 2][:, o * TP:(o + 1) * TP],
                                                 in0=sdel[j % 2][:, o * TP:(o + 1) * TP],
                                                 in1=sp12[:, sl * TP:(sl + 1) * TP],
                                                 op=ALU.add)
                        p.inc(li, "cv")
                        p.edge(v, "cv")
                        li = None
                        for sl, o in dts:
                            li = v.tensor_tensor(out=sdtx[j % 2][:, o * TP:(o + 1) * TP],
                                                 in0=sdel[j % 2][:, o * TP:(o + 1) * TP],
                                                 in1=xi2[:, o * TP:(o + 1) * TP],
                                                 op=ALU.mult)
                        p.inc(li, "cv")
                    p.wait(v, "qv", qt1_vst)
                    p.edge(v, "cv")
                    li = None
                    for t_i in range(DT_I):
                        li = v.tensor_scalar(out=sy_[:, t_i * TP:(t_i + 1) * TP],
                                             in0=xi2[:, t_i * TP:(t_i + 1) * TP],
                                             scalar1=cdv[:, t_i:t_i + 1],
                                             scalar2=None, op0=ALU.mult)
                    p.inc(li, "cv")
                    daB2 = csb + (6 if j == 0 else 262)
                    for n in range(NS):
                        p.wait(v, f"qb{n % 2}", qb_j[n % 2] + 32 * (n // 2 + 1))
                        bap_b = bcb[n % 2][:, 0:1, 0:TP].broadcast_to([128, 4, TP])
                        bap_c = bcb[n % 2][:, 0:1, TP:2 * TP].broadcast_to([128, 4, TP])
                        for tp_ in range(0, DT_I, 4):
                            qi = tp_ // 4
                            p.wait(v, "cs", daB2 + da_off(j, n) + tp_ + 4)
                            li = v.tensor_tensor(out=dbx[qi % 2][:],
                                                 in0=sdtx[j % 2][:, tp_ * TP:(tp_ + 4) * TP],
                                                 in1=bap_b, op=ALU.mult)
                            p.inc(li, "cv")
                            p.edge(v, "cv")
                            li = None
                            for t_i in range(tp_, tp_ + 4):
                                li = v.tensor_tensor_scan(
                                    out=h_all[:, (t_i % 8) * TP:(t_i % 8 + 1) * TP],
                                    data0=da[:, t_i * TP:(t_i + 1) * TP],
                                    data1=dbx[qi % 2][:, (t_i - tp_) * TP:(t_i - tp_ + 1) * TP],
                                    initial=hstate[:, t_i * NS + n:t_i * NS + n + 1],
                                    op0=ALU.mult, op1=ALU.add)
                            p.inc(li, "cv")
                            p.edge(v, "cv")
                            li = v.tensor_tensor(out=zt[qi % 2][:],
                                                 in0=h_all[:, (tp_ % 8) * TP:(tp_ % 8 + 4) * TP],
                                                 in1=bap_c, op=ALU.mult)
                            p.inc(li, "cv")
                            p.edge(v, "cv")
                            li = v.tensor_tensor(out=sy_[:, tp_ * TP:(tp_ + 4) * TP],
                                                 in0=sy_[:, tp_ * TP:(tp_ + 4) * TP],
                                                 in1=zt[qi % 2][:], op=ALU.add)
                            if tp_ % 8 == 4:
                                base = tp_ - 4
                                li = v.tensor_copy(
                                    out=hstate[:, base * NS + n:(tp_ + 4) * NS:NS],
                                    in_=h_all[:, TP - 1::TP])
                            p.inc(li, "cv")
                            p.edge(v, "cv")
                    p.wait(v, "qr", qt0_sres)
                    p.edge(v, "cv")
                    li = None
                    for t_i in range(DT_I):
                        li = v.tensor_tensor(out=sy_[:, t_i * TP:(t_i + 1) * TP],
                                             in0=sy_[:, t_i * TP:(t_i + 1) * TP],
                                             in1=srs[:, t_i * TP:(t_i + 1) * TP], op=ALU.mult)
                    p.inc(li, "cv")
                assert p.c["cv"] == cvb + 262, (p.c["cv"], cvb)


                prev_qbs = qbs_j
                prevcv_prev = prev[3] if prev is not None else None
                prev = (qs_xi2, ctb, csb, cvb)
                if j == NC2 - 1:
                    @blk.sync
                    def _(sy, j=j, cvb=cvb):
                        p.wait(sy, "cv", cvb + 262)
                        for t_i in range(DT_I):
                            p.inc(sy.dma_start(
                                out=v_d[:, t_i * T + j * TP: t_i * T + j * TP + TP],
                                in_=sy_[:, t_i * TP:(t_i + 1) * TP]), "qv", 16)
                    emit_dA_part(j, list(range(NS)), csb + (6 if j == 0 else 262),
                                 cvb, qbs_j, prevcv_prev)


        with contextlib.ExitStack() as es3:
            sb3 = lambda name, shape: es3.enter_context(nc.sbuf_tensor(name, shape, F32))
            wob = [sb3(f"wo{i}", [128, DM]) for i in range(2)]
            wt = [sb3(f"wt{k}", [128, DM]) for k in range(DT_M)]
            vv = sb3("p3_v", [128, DT_I * TC])
            x3 = sb3("p3_x", [128, NTT * DM])
            xnf3 = sb3("p3_xnf", [128, DT_M * TC])
            ss = sb3("p3_s", [128, DT_M * TC])
            sg = sb3("p3_g", [128, DT_M * TC])
            thr = sb3("p3_thr", [128, 2])
            red8 = sb3("p3_red", [128, DT_M + 2])

            @blk.sync
            def _(sy):
                p.wait(sy, "qs", p.c["qs"])
                p.wait(sy, "qt0", p.c["qt0"])
                p.wait(sy, "qt1", p.c["qt1"])
                p.wait(sy, "qv", p.c["qv"])
                p.wait(sy, "qr", p.c["qr"])
                p.wait(sy, "qb0", p.c["qb0"])
                p.wait(sy, "qb1", p.c["qb1"])
                p.wait(sy, "cv", p.c["cv"])
                p.wait(sy, "cs", p.c["cs"])
                p.wait(sy, "ct", p.c["ct"])
                for k in range(DT_M):
                    p.inc(sy.dma_start(out=wt[k][:], in_=wfT_d[k * 128:(k + 1) * 128, :]),
                          "qs", 16)
            qs_wf = p.c["qs"]

            @blk.vector
            def _(v):
                p.wait(v, "qs", qs_wf)
                li = None
                for k in range(DT_M):
                    li = v.tensor_reduce(out=red8[:, k:k + 1], in_=wt[k][:], axis=AX.X,
                                         op=ALU.add, apply_absolute_value=True)
                p.inc(li, "cv")
                p.edge(v, "cv")
                li = v.tensor_reduce(out=red8[:, DT_M:DT_M + 1], in_=red8[:, 0:DT_M], axis=AX.X,
                                     op=ALU.add)
                p.inc(li, "cv")
            cv_red = p.c["cv"]

            @blk.tensor
            def _(t):
                p.wait(t, "cv", cv_red)
                li = t.matmul(pb[0][:1, 0:1], red8[:, DT_M:DT_M + 1], ones_col[:],
                              start=True, stop=True)
                p.inc(li, "ct")
            ct_tot = p.c["ct"]

            @blk.scalar
            def _(s):
                p.wait(s, "ct", ct_tot)
                li = s.activation(out=thr[:1, 0:1], in_=pb[0][:1, 0:1], func=AF.Copy,
                                  scale=0.7 / float(DM * DM))
                p.inc(li, "cs")
            cs_thr = p.c["cs"]

            @blk.tensor
            def _(t):
                p.wait(t, "cs", cs_thr)
                li = t.matmul(pb[1][:, 0:1], ones_row[:1, :], thr[:1, 0:1], start=True, stop=True)
                p.inc(li, "ct")
            ct_thr = p.c["ct"]

            @blk.vector
            def _(v):
                p.wait(v, "ct", ct_thr)
                li = v.tensor_copy(out=thr[:, 0:1], in_=pb[1][:, 0:1])
                p.inc(li, "cv")
                p.edge(v, "cv")
                li = v.tensor_scalar(out=thr[:, 1:2], in0=thr[:, 0:1], scalar1=-1.0,
                                     scalar2=None, op0=ALU.mult)
                p.inc(li, "cv")
                p.edge(v, "cv")
                for kh in range(2):
                    ks = range(kh * 4, kh * 4 + 4)
                    li = None
                    for k in ks:
                        li = v.tensor_scalar(out=sg[:, (k % 4) * DM:(k % 4 + 1) * DM],
                                             in0=wt[k][:], scalar1=thr[:, 0:1],
                                             scalar2=None, op0=ALU.is_gt)
                    for k in ks:
                        li = v.tensor_scalar(out=ss[:, (k % 4) * DM:(k % 4 + 1) * DM],
                                             in0=wt[k][:], scalar1=thr[:, 1:2],
                                             scalar2=None, op0=ALU.is_lt)
                    p.inc(li, "cv")
                    p.edge(v, "cv")
                    li = None
                    for k in ks:
                        li = v.tensor_tensor(out=wt[k][:],
                                             in0=sg[:, (k % 4) * DM:(k % 4 + 1) * DM],
                                             in1=ss[:, (k % 4) * DM:(k % 4 + 1) * DM],
                                             op=ALU.subtract)
                    p.inc(li, "cv")
                    p.edge(v, "cv")

            for j in range(NCH):
                cvb = p.c["cv"]
                ctb = p.c["ct"]
                csb = p.c["cs"]

                @blk.sync
                def _(sy, j=j):
                    p.wait(sy, "qs", p.c["qs"])
                    if j >= 1:
                        p.wait(sy, "cv", p.c["cv"])
                        p.wait(sy, "ct", p.c["ct"])
                    for t_i in range(DT_I):
                        p.inc(sy.dma_start(out=vv[:, t_i * TC:(t_i + 1) * TC],
                                           in_=v_d[:, t_i * T + j * TC: t_i * T + j * TC + TC]),
                              "qs", 16)
                    for tt in range(NTT):
                        p.inc(sy.dma_start(out=x3[:, tt * DM:(tt + 1) * DM],
                                           in_=x_d[j * TC + tt * 128: j * TC + (tt + 1) * 128, :]),
                              "qs", 16)
                qs_v = p.c["qs"]

                @blk.vector
                def _(v, j=j, qs_v=qs_v):
                    p.wait(v, "qs", qs_v)
                    li = None
                    for tt in range(NTT):
                        g = j * NTT + tt
                        li = v.tensor_scalar(out=x3[:, tt * DM:(tt + 1) * DM],
                                             in0=x3[:, tt * DM:(tt + 1) * DM],
                                             scalar1=mv_all[:, 2 * g:2 * g + 1],
                                             scalar2=rstd_all[:, g:g + 1],
                                             op0=ALU.subtract, op1=ALU.mult)
                    p.inc(li, "cv")

                qw_base3 = (p.c["qw0"], p.c["qw1"])

                @blk.sync
                def _(sy, j=j, ctb=ctb):
                    for k in range(DT_I):
                        p.wait(sy, "ct", ctb + 8 + k - 1)
                        p.inc(sy.dma_start(out=wob[k % 2][:],
                                           in_=woT_d[k * 128:(k + 1) * 128, :]),
                              f"qw{k % 2}", 16)

                @blk.tensor
                def _(t, j=j, cvb=cvb, csb=csb, qw_base3=qw_base3):
                    p.wait(t, "cv", cvb + 1)
                    for grp in range(DT_M):
                        if grp >= 2:
                            p.wait(t, "cv", cvb + 1 + (grp - 1))
                        bank = pb[grp % 2]
                        li = None
                        for i in range(4):
                            idx = grp * 4 + i
                            tt, c = idx // DT_M, idx % DT_M
                            li = t.transpose(bank[:, i * 128:(i + 1) * 128],
                                             x3[:, tt * DM + c * 128: tt * DM + (c + 1) * 128],
                                             iden[:])
                        p.inc(li, "ct")
                    p.wait(t, "cv", cvb + 9)
                    for k in range(DT_I):
                        p.wait(t, f"qw{k % 2}", qw_base3[k % 2] + 16 * (k // 2 + 1))
                        li = None
                        for o in range(DT_M):
                            li = t.matmul(pb[o][:, 0:TC], wob[k % 2][:, o * 128:(o + 1) * 128],
                                          vv[:, k * TC:(k + 1) * TC],
                                          start=(k == 0), stop=(k == DT_I - 1))
                        p.inc(li, "ct")
                    p.wait(t, "cs", csb + 1)
                    li = None
                    for o in range(DT_M):
                        for k in range(DT_M):
                            li = t.matmul(pb[o][:, 0:TC], wt[k][:, o * 128:(o + 1) * 128],
                                          xnf3[:, k * TC:(k + 1) * TC],
                                          start=(k == 0), stop=(k == DT_M - 1))
                    p.inc(li, "ct")
                    p.wait(t, "cv", cvb + 10)
                    for grp in range(DT_M):
                        if grp >= 2:
                            p.wait(t, "cv", cvb + 10 + (grp - 1))
                        bank = pb[grp % 2]
                        li = None
                        for i in range(4):
                            idx = grp * 4 + i
                            c, tt = idx // NTT, idx % NTT
                            li = t.transpose(bank[:, i * 128:(i + 1) * 128],
                                             ss[:, c * TC + tt * 128: c * TC + (tt + 1) * 128],
                                             iden[:])
                        p.inc(li, "ct")

                @blk.vector
                def _(v, j=j, ctb=ctb):
                    for grp in range(DT_M):
                        p.wait(v, "ct", ctb + grp + 1)
                        bank = pb[grp % 2]
                        li = None
                        for i in range(4):
                            idx = grp * 4 + i
                            tt, c = idx // DT_M, idx % DT_M
                            li = v.tensor_copy(out=xnf3[:, c * TC + tt * 128: c * TC + (tt + 1) * 128],
                                               in_=bank[:, i * 128:(i + 1) * 128])
                        p.inc(li, "cv")

                @blk.scalar
                def _(s, j=j, ctb=ctb):
                    p.wait(s, "ct", ctb + 8 + 16)
                    li = None
                    for o in range(DT_M):
                        li = s.activation(out=ss[:, o * TC:(o + 1) * TC], in_=pb[o][:, 0:TC],
                                          func=AF.Identity, bias=cbout[:, o:o + 1])
                    p.inc(li, "cs")
                    p.wait(s, "ct", ctb + 8 + 16 + 1)
                    li = None
                    for o in range(DT_M):
                        li = s.activation(out=sg[:, o * TC:(o + 1) * TC], in_=pb[o][:, 0:TC],
                                          func=AF.Silu, bias=cbffn[:, o:o + 1], scale=alph[:, 0:1])
                    p.inc(li, "cs")

                @blk.vector
                def _(v, j=j, csb=csb):
                    p.wait(v, "cs", csb + 2)
                    li = None
                    for o in range(DT_M):
                        li = v.tensor_tensor(out=ss[:, o * TC:(o + 1) * TC],
                                             in0=ss[:, o * TC:(o + 1) * TC],
                                             in1=sg[:, o * TC:(o + 1) * TC], op=ALU.mult)
                    p.inc(li, "cv")

                @blk.sync
                def _(sy, j=j, ctb=ctb):
                    p.wait(sy, "ct", ctb + 8)
                    for tt in range(NTT):
                        p.inc(sy.dma_start(out=x3[:, tt * DM:(tt + 1) * DM],
                                           in_=x_d[j * TC + tt * 128: j * TC + (tt + 1) * 128, :]),
                              "qs", 16)
                qs_x3 = p.c["qs"]

                @blk.vector
                def _(v, j=j, ctb=ctb, qs_x3=qs_x3):
                    p.wait(v, "qs", qs_x3)
                    for grp in range(DT_M):
                        p.wait(v, "ct", ctb + 8 + 16 + 1 + grp + 1)
                        bank = pb[grp % 2]
                        li = None
                        for i in range(4):
                            idx = grp * 4 + i
                            c, tt = idx // NTT, idx % NTT
                            li = v.tensor_tensor(out=x3[:, tt * DM + c * 128: tt * DM + (c + 1) * 128],
                                                 in0=x3[:, tt * DM + c * 128: tt * DM + (c + 1) * 128],
                                                 in1=bank[:, i * 128:(i + 1) * 128], op=ALU.add)
                        p.inc(li, "cv")

                @blk.sync
                def _(sy, j=j, cvb=cvb):
                    p.wait(sy, "cv", cvb + 18)
                    for tt in range(NTT):
                        p.inc(sy.dma_start(out=out_d[j * TC + tt * 128: j * TC + (tt + 1) * 128, :],
                                           in_=x3[:, tt * DM:(tt + 1) * DM]), "qs", 16)

        @blk.gpsimd
        def _(g):
            p.wait(g, "qs", p.c["qs"])
            p.wait(g, "qg", p.c["qg"])
            p.wait(g, "qt0", p.c["qt0"])
            p.wait(g, "qt1", p.c["qt1"])
            p.wait(g, "qb0", p.c["qb0"])
            p.wait(g, "qb1", p.c["qb1"])
            p.wait(g, "qbs", p.c["qbs"])
            p.wait(g, "qr", p.c["qr"])
            p.wait(g, "qv", p.c["qv"])
            p.wait(g, "qh0", p.c["qh0"])
            p.wait(g, "qh1", p.c["qh1"])

    return nc


def kernel(L0_attn_x, L0_attn_W_in, L0_attn_b_in, L0_attn_W_conv, L0_attn_b_conv,
           L0_attn_W_x, L0_attn_b_x, L0_attn_W_out, L0_attn_b_out, L0_attn_A_log,
           L0_attn_D, L0_ffn_weight, L0_ffn_alpha, L0_ffn_bias):
    f32 = np.float32
    c_ = np.ascontiguousarray
    x = c_(np.asarray(L0_attn_x, f32))

    winT = c_(np.asarray(L0_attn_W_in, f32).T)
    wxT = c_(np.asarray(L0_attn_W_x, f32).T)
    woT = c_(np.asarray(L0_attn_W_out, f32).T)
    wfT = c_(np.asarray(L0_ffn_weight, f32).T)
    b_in = np.asarray(L0_attn_b_in, f32)
    b_x = np.asarray(L0_attn_b_x, f32)

    def col128(vec, nt):
        return c_(vec.reshape(nt, 128).T)

    cb_in_xi = col128(b_in[:DI], DT_I)
    cb_in_res = col128(b_in[DI:], DT_I)
    cw_conv = c_(np.asarray(L0_attn_W_conv, f32)[:, 0, :].reshape(DT_I, 128, 4)
                 .transpose(1, 0, 2).reshape(128, DT_I * 4))
    cb_conv = col128(np.asarray(L0_attn_b_conv, f32), DT_I)
    cb_x_d = col128(b_x[:DI], DT_I)
    cb_x_bc = c_(b_x[DI:].reshape(32, 1))
    cb_out = col128(np.asarray(L0_attn_b_out, f32), DT_M)
    c_d = col128(np.asarray(L0_attn_D, f32), DT_I)
    c_a_log = c_(np.asarray(L0_attn_A_log, f32).reshape(DT_I, 128, NS)
                 .transpose(1, 0, 2).reshape(128, DT_I * NS))
    c_alpha = c_(np.broadcast_to(np.asarray(L0_ffn_alpha, f32).reshape(1, 1), (128, 1)))
    cb_ffn = col128(np.asarray(L0_ffn_bias, f32), DT_M)
    iden = np.eye(128, dtype=f32)
    zeros3 = np.zeros((3, DM), f32)
    zmask = np.zeros((128, 1), f32)

    shared = dict(w_in_t=winT, w_x_t=wxT, w_out_t=woT, w_ffn_t=wfT,
                  cb_in_xi=cb_in_xi, cb_in_res=cb_in_res, cw_conv=cw_conv,
                  cb_conv=cb_conv, cb_x_d=cb_x_d, cb_x_bc=cb_x_bc, cb_out=cb_out,
                  c_d=c_d, c_a_log=c_a_log, c_alpha=c_alpha, cb_ffn=cb_ffn, iden=iden)

    in_maps = []
    for c in range(8):
        seq = c % 4
        m = dict(shared)
        m["x"] = c_(x[seq])
        m["x_halo3"] = zeros3
        m["halo_mask"] = zmask
        in_maps.append(m)

    nc = build_nc()
    res = run_bass_kernel_spmd(nc, in_maps, core_ids=list(range(8)))
    out = np.empty((4, 4096, DM), f32)
    for seq in range(4):
        out[seq] = res.results[seq]["out"]
    return out


# version 21
# speedup vs baseline: 1.1986x; 1.0410x over previous
import contextlib
import numpy as np
import concourse.bass as bass
import concourse.mybir as mybir
from concourse.bass_utils import run_bass_kernel_spmd

F32 = mybir.dt.float32
BF16 = mybir.dt.bfloat16
AF = mybir.ActivationFunctionType
ALU = mybir.AluOpType
AX = mybir.AxisListType

DM = 1024
DI = 2048
NS = 16
DT_M = DM // 128
DT_I = DI // 128
EPS = 1e-5

import os as _os
HALF = False
T = int(_os.environ.get("MK_T", "0")) or (2048 if HALF else 4096)
TC = 512
NCH = T // TC
NTT = TC // 128
XS = TC + 3


class P:
    def __init__(self, nc, es):
        self.nc = nc
        self.s = {}
        self.c = {}
        for name in ("qs", "qg", "ct", "cv", "cs", "qw0", "qw1", "qt0", "qt1", "qb0", "qb1", "qbs", "qr", "qv", "qh0", "qh1", "cz"):
            self.s[name] = es.enter_context(nc.semaphore(name))
            self.c[name] = 0

    def inc(self, instr, name, k=1):
        instr.then_inc(self.s[name], k)
        self.c[name] += k
        return self.c[name]

    def wait(self, eng, name, val=None):
        if val is None:
            val = self.c[name]
        if val > 0:
            eng.wait_ge(self.s[name], val)

    def edge(self, eng, name):
        self.wait(eng, name, self.c[name])


def build_nc():
    nc = bass.Bass("TRN2", target_bir_lowering=False)

    x_d = nc.dram_tensor("x", [T, DM], F32, kind="ExternalInput").ap()
    xh_d = nc.dram_tensor("x_halo3", [3, DM], F32, kind="ExternalInput").ap()
    hm_d = nc.dram_tensor("halo_mask", [128, 1], F32, kind="ExternalInput").ap()
    winT_d = nc.dram_tensor("w_in_t", [DM, 2 * DI], F32, kind="ExternalInput").ap()
    wxT_d = nc.dram_tensor("w_x_t", [DI, DI + 2 * NS], F32, kind="ExternalInput").ap()
    woT_d = nc.dram_tensor("w_out_t", [DI, DM], F32, kind="ExternalInput").ap()
    wfT_d = nc.dram_tensor("w_ffn_t", [DM, DM], F32, kind="ExternalInput").ap()
    cbixi_d = nc.dram_tensor("cb_in_xi", [128, DT_I], F32, kind="ExternalInput").ap()
    cbires_d = nc.dram_tensor("cb_in_res", [128, DT_I], F32, kind="ExternalInput").ap()
    cwconv_d = nc.dram_tensor("cw_conv", [128, DT_I * 4], F32, kind="ExternalInput").ap()
    cbconv_d = nc.dram_tensor("cb_conv", [128, DT_I], F32, kind="ExternalInput").ap()
    cbxd_d = nc.dram_tensor("cb_x_d", [128, DT_I], F32, kind="ExternalInput").ap()
    cbxbc_d = nc.dram_tensor("cb_x_bc", [32, 1], F32, kind="ExternalInput").ap()
    cbout_d = nc.dram_tensor("cb_out", [128, DT_M], F32, kind="ExternalInput").ap()
    cd_d = nc.dram_tensor("c_d", [128, DT_I], F32, kind="ExternalInput").ap()
    calog_d = nc.dram_tensor("c_a_log", [128, DT_I * NS], F32, kind="ExternalInput").ap()
    alpha_d = nc.dram_tensor("c_alpha", [128, 1], F32, kind="ExternalInput").ap()
    cbffn_d = nc.dram_tensor("cb_ffn", [128, DT_M], F32, kind="ExternalInput").ap()
    iden_d = nc.dram_tensor("iden", [128, 128], F32, kind="ExternalInput").ap()
    out_d = nc.dram_tensor("out", [T, DM], F32, kind="ExternalOutput").ap()
    xi2_d = nc.dram_tensor("xi2_s", [128, DT_I * T], F32, kind="Internal").ap()
    sres_d = nc.dram_tensor("sres_s", [128, DT_I * T], F32, kind="Internal").ap()
    v_d = nc.dram_tensor("v_s", [128, DT_I * T], F32, kind="Internal").ap()
    bcd_d = nc.dram_tensor("bc_s", [32, T], F32, kind="Internal").ap()

    es = contextlib.ExitStack()
    with es:
        p = P(nc, es)
        sb = lambda name, shape: es.enter_context(nc.sbuf_tensor(name, shape, F32))

        cbixi = sb("cbixi", [128, DT_I])
        cbires = sb("cbires", [128, DT_I])
        cwconv = sb("cwconv", [128, DT_I * 4])
        cbconv = sb("cbconv", [128, DT_I])
        cbxd = sb("cbxd", [128, DT_I])
        cbxbc = sb("cbxbc", [32, 1])
        cbout = sb("cbout", [128, DT_M])
        cdv = sb("cdv", [128, DT_I])
        ca = sb("ca", [128, DT_I * NS])
        alph = sb("alph", [128, 1])
        cbffn = sb("cbffn", [128, DT_M])
        iden = sb("iden_sb", [128, 128])
        hmask = sb("hmask", [128, 1])
        epsb = sb("epsb", [128, 1])
        ones_row = sb("ones_row", [1, 128])
        ones_col = sb("ones_col", [128, 1])
        mv_all = sb("mv_all", [128, 2 * (T // 128)])
        rstd_all = sb("rstd_all", [128, T // 128])
        hstate = sb("hstate", [128, DT_I * NS])
        xihalo = sb("xihalo", [128, DT_I * 3])

        pb = [es.enter_context(nc.psum_tensor(f"pb{i}", [128, 512], F32)) for i in range(8)]

        blk = es.enter_context(nc.Block())

        @blk.gpsimd
        def _(g):
            for dst, src in ((cbixi, cbixi_d), (cbires, cbires_d), (cwconv, cwconv_d),
                             (cbconv, cbconv_d), (cbxd, cbxd_d), (cbxbc, cbxbc_d),
                             (cbout, cbout_d), (cdv, cd_d), (ca, calog_d),
                             (alph, alpha_d), (cbffn, cbffn_d), (iden, iden_d),
                             (hmask, hm_d)):
                p.inc(g.dma_start(out=dst[:], in_=src[:]), "qg", 16)

        @blk.vector
        def _(v):
            p.wait(v, "qg")
            v.memset(ones_row[:], 1.0)
            v.memset(epsb[:], EPS)
            v.memset(ones_col[:], 1.0)
            v.memset(hstate[:], 0.0)
            p.inc(v.memset(xihalo[:], 0.0), "cv")

        @blk.scalar
        def _(s):
            p.wait(s, "qg")
            p.wait(s, "cv")
            li = None
            for t in range(DT_I):
                li = s.activation(out=ca[:, t * NS:(t + 1) * NS],
                                  in_=ca[:, t * NS:(t + 1) * NS], func=AF.Exp)
            p.inc(li, "cs")

        @blk.vector
        def _(v):
            p.wait(v, "cs")
            li = None
            for t in range(DT_I):
                li = v.tensor_scalar(out=ca[:, t * NS:(t + 1) * NS],
                                     in0=ca[:, t * NS:(t + 1) * NS],
                                     scalar1=-1.0, scalar2=None, op0=ALU.mult)
            p.inc(li, "cv")

        with contextlib.ExitStack() as es1:
            sb1 = lambda name, shape: es1.enter_context(nc.sbuf_tensor(name, shape, F32))
            winh = [es1.enter_context(nc.sbuf_tensor(f"winh{k}", [128, 2 * DI], BF16))
                    for k in range(DT_M)]
            winl = [es1.enter_context(nc.sbuf_tensor(f"winl{k}", [128, 2 * DI], BF16))
                    for k in range(DT_M)]
            x1 = sb1("p1_x", [128, NTT * DM])
            bns = sb1("p1_bns", [128, 64])
            xnfh = es1.enter_context(nc.sbuf_tensor("p1_xnfh", [128, DT_M * TC], BF16))
            xnfl = es1.enter_context(nc.sbuf_tensor("p1_xnfl", [128, DT_M * TC], BF16))
            xi = sb1("p1_xi", [128, DT_I * XS])
            cvs = [sb1(f"p1_cv{i}", [128, TC]) for i in range(2)]
            stg = [sb1(f"p1_stg{i}", [128, TC]) for i in range(2)]

            @blk.sync
            def _(sy):
                p.wait(sy, "cv", p.c["cv"])
                p.wait(sy, "cs", p.c["cs"])
                p.inc(sy.dma_start(out=x1[:3, 0:DM], in_=xh_d[:]), "qs", 16)
            qs_halo = p.c["qs"]

            cv_wb = p.c["cv"]

            @blk.sync
            def _(sy):
                p.wait(sy, "qs", p.c["qs"])
                for k in range(DT_M):
                    if k >= 2:
                        p.wait(sy, "cv", cv_wb + 2 * (k - 1))
                    p.inc(sy.dma_start(out=xi[:, (k % 2) * 4096:(k % 2) * 4096 + 4096],
                                       in_=winT_d[k * 128:(k + 1) * 128, :]),
                          f"qh{k % 2}", 16)

            @blk.vector
            def _(v):
                for k in range(DT_M):
                    tmp = xi[:, (k % 2) * 4096:(k % 2) * 4096 + 4096]
                    p.wait(v, f"qh{k % 2}", 16 * (k // 2 + 1))
                    li = v.tensor_copy(out=winh[k][:], in_=tmp)
                    p.inc(li, "cv")
                    p.edge(v, "cv")
                    li = v.tensor_tensor(out=winl[k][:], in0=tmp, in1=winh[k][:],
                                         op=ALU.subtract)
                    p.inc(li, "cv")
            qs_win = p.c["qs"]

            @blk.vector
            def _(v):
                p.wait(v, "qs", qs_halo)
                p.wait(v, "cs")
                v.bn_stats(out=bns[:3, 0:6], in_=x1[:3, 0:512])
                li = v.bn_stats(out=bns[:3, 6:12], in_=x1[:3, 512:1024])
                p.inc(li, "cv")
                p.edge(v, "cv")
                li = v.bn_aggr(out=bns[:3, 12:14], in_=bns[:3, 0:12])
                p.inc(li, "cv")
            cv_halo_stats = p.c["cv"]

            @blk.scalar
            def _(s):
                p.wait(s, "cv", cv_halo_stats)
                li = s.activation(out=bns[:3, 14:15], in_=bns[:3, 13:14], func=AF.Ln,
                                  bias=epsb[:3, 0:1])
                p.inc(li, "cs")
                p.edge(s, "cs")
                li = s.activation(out=bns[:3, 14:15], in_=bns[:3, 14:15], func=AF.Exp, scale=-0.5)
                p.inc(li, "cs")
            cs_halo_rstd = p.c["cs"]

            @blk.vector
            def _(v):
                p.wait(v, "cs", cs_halo_rstd)
                li = v.tensor_scalar(out=x1[:3, 0:DM], in0=x1[:3, 0:DM],
                                     scalar1=bns[:3, 12:13], scalar2=bns[:3, 14:15],
                                     op0=ALU.subtract, op1=ALU.mult)
                p.inc(li, "cv")
            cv_halo_xn = p.c["cv"]

            @blk.tensor
            def _(t):
                p.wait(t, "cv", cv_halo_xn)
                p.wait(t, "qs", qs_win)
                li = None
                for c in range(DT_M):
                    li = t.transpose(pb[0][:, c * 3:c * 3 + 3], x1[:3, c * 128:(c + 1) * 128],
                                     iden[:3, 0:3])
                p.inc(li, "ct")
            ct_halo_tr = p.c["ct"]

            @blk.vector
            def _(v):
                p.wait(v, "ct", ct_halo_tr)
                p.inc(v.tensor_copy(out=xnfh[:, 0:DT_M * 3], in_=pb[0][:, 0:DT_M * 3]),
                      "cv")
                p.edge(v, "cv")
                p.inc(v.tensor_tensor(out=xnfl[:, 0:DT_M * 3], in0=pb[0][:, 0:DT_M * 3],
                                      in1=xnfh[:, 0:DT_M * 3], op=ALU.subtract), "cv")
            cv_halo_evac = p.c["cv"]

            cs_hb = p.c["cs"]

            @blk.tensor
            def _(t):
                p.wait(t, "cv", cv_halo_evac)
                for o in range(DT_I):
                    if o >= 2:
                        p.wait(t, "cs", cs_hb + o - 1)
                    dst = pb[1] if o % 2 == 0 else pb[2]
                    li = None
                    for k in range(DT_M):
                        for wm, xm, tm in ((winh, xnfh, 0), (winh, xnfl, 1),
                                           (winl, xnfh, 2)):
                            li = t.matmul(dst[:, 0:3], wm[k][:, o * 128:(o + 1) * 128],
                                          xm[:, k * 3:k * 3 + 3],
                                          start=(k == 0 and tm == 0),
                                          stop=(k == DT_M - 1 and tm == 2))
                    p.inc(li, "ct")
            ct_halo_mm = p.c["ct"]

            @blk.scalar
            def _(s):
                for o in range(DT_I):
                    src = pb[1] if o % 2 == 0 else pb[2]
                    p.wait(s, "ct", ct_halo_mm - DT_I + o + 1)
                    li = s.activation(out=xihalo[:, o * 3:(o + 1) * 3], in_=src[:, 0:3],
                                      func=AF.Identity, bias=cbixi[:, o:o + 1])
                    p.inc(li, "cs")
            cs_halo = p.c["cs"]

            @blk.vector
            def _(v):
                p.wait(v, "cs", cs_halo)
                li = v.tensor_scalar(out=xihalo[:], in0=xihalo[:], scalar1=hmask[:, 0:1],
                                     scalar2=None, op0=ALU.mult)
                p.inc(li, "cv")
                p.edge(v, "cv")
                li = None
                for t in range(DT_I):
                    li = v.tensor_copy(out=xi[:, t * XS:t * XS + 3],
                                       in_=xihalo[:, t * 3:(t + 1) * 3])
                p.inc(li, "cv")

            ct_prev_tr_end = p.c["ct"]
            for j in range(NCH):
                qs_x = p.c["qs"] + 16 * NTT
                cvb = p.c["cv"]
                ctb = p.c["ct"]
                csb = p.c["cs"]
                qt_b = {0: p.c["qt0"], 1: p.c["qt1"]}

                @blk.sync
                def _(sy, j=j):
                    p.wait(sy, "ct", ct_prev_tr_end)
                    for tt in range(NTT):
                        p.inc(sy.dma_start(out=x1[:, tt * DM:(tt + 1) * DM],
                                           in_=x_d[j * TC + tt * 128: j * TC + (tt + 1) * 128, :]),
                              "qs", 16)

                @blk.vector
                def _(v, j=j, csb=csb):
                    p.wait(v, "qs", qs_x)
                    li = None
                    for tt in range(NTT):
                        v.bn_stats(out=bns[:, tt * 12:tt * 12 + 6],
                                   in_=x1[:, tt * DM:tt * DM + 512])
                        li = v.bn_stats(out=bns[:, tt * 12 + 6:tt * 12 + 12],
                                        in_=x1[:, tt * DM + 512:tt * DM + 1024])
                    p.inc(li, "cv")
                    p.edge(v, "cv")
                    li = None
                    for tt in range(NTT):
                        g = j * NTT + tt
                        li = v.bn_aggr(out=mv_all[:, 2 * g:2 * g + 2],
                                       in_=bns[:, tt * 12:tt * 12 + 12])
                    p.inc(li, "cv")
                    p.wait(v, "cs", csb + 2)
                    li = None
                    for tt in range(NTT):
                        g = j * NTT + tt
                        li = v.tensor_scalar(out=x1[:, tt * DM:(tt + 1) * DM],
                                             in0=x1[:, tt * DM:(tt + 1) * DM],
                                             scalar1=mv_all[:, 2 * g:2 * g + 1],
                                             scalar2=rstd_all[:, g:g + 1],
                                             op0=ALU.subtract, op1=ALU.mult)
                    p.inc(li, "cv")

                @blk.scalar
                def _(s, j=j, cvb=cvb):
                    p.wait(s, "cv", cvb + 2)
                    li = None
                    for tt in range(NTT):
                        g = j * NTT + tt
                        li = s.activation(out=rstd_all[:, g:g + 1],
                                          in_=mv_all[:, 2 * g + 1:2 * g + 2],
                                          func=AF.Ln, bias=epsb[:, 0:1])
                    p.inc(li, "cs")
                    p.edge(s, "cs")
                    li = None
                    for tt in range(NTT):
                        g = j * NTT + tt
                        li = s.activation(out=rstd_all[:, g:g + 1],
                                          in_=rstd_all[:, g:g + 1],
                                          func=AF.Exp, scale=-0.5)
                    p.inc(li, "cs")

                @blk.tensor
                def _(t, j=j, cvb=cvb, csb=csb):
                    p.wait(t, "cv", cvb + 3)
                    for grp in range(DT_M):
                        if grp >= 2:
                            p.wait(t, "cv", cvb + 5 + 2 * (grp - 2))
                        bank = pb[grp % 2]
                        li = None
                        for i in range(4):
                            idx = grp * 4 + i
                            tt, c = idx // DT_M, idx % DT_M
                            li = t.transpose(bank[:, i * 128:(i + 1) * 128],
                                             x1[:, tt * DM + c * 128: tt * DM + (c + 1) * 128],
                                             iden[:])
                        p.inc(li, "ct")
                    p.wait(t, "cv", cvb + 19)
                    for o in range(2 * DI // 128):
                        if o >= 2:
                            p.wait(t, "cs", csb + 2 + o - 1)
                        bank = pb[2 + o % 2]
                        li = None
                        for k in range(DT_M):
                            for wm, xm, tm in ((winh, xnfh, 0), (winh, xnfl, 1),
                                               (winl, xnfh, 2)):
                                li = t.matmul(bank[:], wm[k][:, o * 128:(o + 1) * 128],
                                              xm[:, k * TC:(k + 1) * TC],
                                              start=(k == 0 and tm == 0),
                                              stop=(k == DT_M - 1 and tm == 2))
                        p.inc(li, "ct")
                ct_prev_tr_end = ctb + 8

                @blk.vector
                def _(v, j=j, ctb=ctb):
                    for grp in range(DT_M):
                        p.wait(v, "ct", ctb + grp + 1)
                        bank = pb[grp % 2]
                        li = None
                        sls = []
                        for i in range(4):
                            idx = grp * 4 + i
                            tt, c = idx // DT_M, idx % DT_M
                            sls.append((slice(c * TC + tt * 128, c * TC + (tt + 1) * 128), i))
                        for sl, i in sls:
                            li = v.tensor_copy(out=xnfh[:, sl],
                                               in_=bank[:, i * 128:(i + 1) * 128])
                        p.inc(li, "cv")
                        p.edge(v, "cv")
                        li = None
                        for sl, i in sls:
                            li = v.tensor_tensor(out=xnfl[:, sl],
                                                 in0=bank[:, i * 128:(i + 1) * 128],
                                                 in1=xnfh[:, sl], op=ALU.subtract)
                        p.inc(li, "cv")

                qt_store_count = {0: 0, 1: 0}
                silu_qt_wait = {}
                res_qt_wait = {}
                for oo in range(DT_I):
                    s_ = oo % 2
                    res_qt_wait[oo] = qt_b[s_] + 16 * qt_store_count[s_]
                    qt_store_count[s_] += 1
                for t_i in range(DT_I):
                    s_ = t_i % 2
                    silu_qt_wait[t_i] = qt_b[s_] + 16 * qt_store_count[s_]
                    qt_store_count[s_] += 1

                @blk.scalar
                def _(s, j=j, ctb=ctb, cvb=cvb):
                    for o in range(2 * DI // 128):
                        p.wait(s, "ct", ctb + 8 + o + 1)
                        bank = pb[2 + o % 2]
                        if o < DI // 128:
                            li = s.activation(out=xi[:, o * XS + 3: o * XS + 3 + TC], in_=bank[:],
                                              func=AF.Identity, bias=cbixi[:, o:o + 1])
                        else:
                            oo = o - DI // 128
                            p.wait(s, f"qt{oo % 2}", res_qt_wait[oo])
                            li = s.activation(out=stg[oo % 2][:], in_=bank[:],
                                              func=AF.Silu, bias=cbires[:, oo:oo + 1])
                        p.inc(li, "cs")
                    for t_i in range(DT_I):
                        cg = t_i // 2
                        p.wait(s, "cv", cvb + 19 + 5 * cg + 4)
                        p.wait(s, f"qt{t_i % 2}", silu_qt_wait[t_i])
                        li = s.activation(out=stg[t_i % 2][:], in_=cvs[t_i % 2][:],
                                          func=AF.Silu, bias=cbconv[:, t_i:t_i + 1])
                        p.inc(li, "cs")

                @blk.vector
                def _(v, j=j, csb=csb):
                    for cg in range(DT_I // 2):
                        ts_grp = (2 * cg, 2 * cg + 1)
                        p.wait(v, "cs", csb + 2 + 2 * cg + 2)
                        if cg >= 1:
                            p.wait(v, "cs", csb + 2 + 32 + 2 * cg)
                        li = None
                        for t_i in ts_grp:
                            base = t_i * XS
                            li = v.tensor_scalar(out=cvs[t_i % 2][:], in0=xi[:, base:base + TC],
                                                 scalar1=cwconv[:, t_i * 4:t_i * 4 + 1],
                                                 scalar2=None, op0=ALU.mult)
                        p.inc(li, "cv")
                        for k in (1, 2, 3):
                            p.edge(v, "cv")
                            li = None
                            for t_i in ts_grp:
                                base = t_i * XS
                                li = v.scalar_tensor_tensor(out=cvs[t_i % 2][:],
                                                            in0=xi[:, base + k:base + k + TC],
                                                            scalar=cwconv[:, t_i * 4 + k:t_i * 4 + k + 1],
                                                            in1=cvs[t_i % 2][:],
                                                            op0=ALU.mult, op1=ALU.add)
                            p.inc(li, "cv")
                        p.edge(v, "cv")
                        li = None
                        for t_i in ts_grp:
                            base = t_i * XS
                            li = v.tensor_copy(out=xi[:, base:base + 3],
                                               in_=xi[:, base + TC:base + TC + 3])
                        p.inc(li, "cv")

                @blk.sync
                def _(sy, j=j, csb=csb):
                    for oo in range(DT_I):
                        p.wait(sy, "cs", csb + 2 + 16 + oo + 1)
                        p.inc(sy.dma_start(out=sres_d[:, oo * T + j * TC: oo * T + j * TC + TC],
                                           in_=stg[oo % 2][:]), f"qt{oo % 2}", 16)
                    for t_i in range(DT_I):
                        p.wait(sy, "cs", csb + 2 + 32 + t_i + 1)
                        p.inc(sy.dma_start(out=xi2_d[:, t_i * T + j * TC: t_i * T + j * TC + TC],
                                           in_=stg[t_i % 2][:]), f"qt{t_i % 2}", 16)

        TP = 256
        NC2 = T // TP
        with contextlib.ExitStack() as es2:
            sb2 = lambda name, shape: es2.enter_context(nc.sbuf_tensor(name, shape, F32))
            wxb = [sb2(f"wx{i}", [128, 12 * 128]) for i in range(2)]
            bcb = [sb2(f"p2_bcb{i}", [128, 1, 2 * TP]) for i in range(2)]
            xi2 = sb2("p2_xi2", [128, DT_I * TP])
            sdel = [sb2(f"p2_del{i}", [128, DT_I * TP]) for i in range(2)]
            sdtx = [sb2(f"p2_dtx{i}", [128, DT_I * TP]) for i in range(2)]
            sy_ = sb2("p2_y", [128, DT_I * TP])
            srs = sb2("p2_srs", [128, DT_I * TP])
            h_all = sb2("p2_h", [128, 8 * TP])
            da = sb2("p2_da", [128, DT_I * TP])
            sp12 = sb2("p2_sp", [128, 17 * TP])
            dbx = [sb2(f"p2_dbx{i}", [128, 4 * TP]) for i in range(2)]
            zt = [sb2(f"p2_z{i}", [128, 4 * TP]) for i in range(2)]
            bc = sb2("p2_bc", [32, TP])

            NO = (DI + 2 * NS + 127) // 128
            GRPS = [list(range(0, 12)), list(range(12, 17))]

            def da_off(jj, n):
                if jj == NC2 - 1:
                    return 16 * n
                return 16 * n + (0 if n < 12 else 3 if n < 14 else 6)

            def emit_dA_part(jj, ns, csA, cvb_jj, qbs_jj, prevcv):
                @blk.scalar
                def _(s, jj=jj, ns=ns, csA=csA, cvb_jj=cvb_jj, qbs_jj=qbs_jj,
                      prevcv=prevcv):
                    cvs0 = cvb_jj + 5
                    if ns[0] == 0:
                        p.wait(s, "cv", cvb_jj + 4)
                        p.wait(s, "qbs", qbs_jj)
                    for n in ns:
                        if n >= 2:
                            p.wait(s, "cv", cvs0 + 16 * (n - 1))
                        elif prevcv is not None:
                            p.wait(s, "cv", prevcv + 5 + 16 * (15 + n))
                        p.inc(s.dma_start(
                            out=bcb[n % 2][:, 0, 0:TP],
                            in_=bcd_d[n, jj * TP:(jj + 1) * TP].partition_broadcast(128)),
                            f"qb{n % 2}", 16)
                        p.inc(s.dma_start(
                            out=bcb[n % 2][:, 0, TP:2 * TP],
                            in_=bcd_d[NS + n, jj * TP:(jj + 1) * TP].partition_broadcast(128)),
                            f"qb{n % 2}", 16)
                        for t_i in range(DT_I):
                            if n > 0:
                                p.wait(s, "cs", csA + da_off(jj, n - 1) + t_i + 1)
                                p.wait(s, "cv", cvs0 + 16 * (n - 1) + 4 * (t_i // 4) + 2)
                            li = s.activation(out=da[:, t_i * TP:(t_i + 1) * TP],
                                              in_=sdel[jj % 2][:, t_i * TP:(t_i + 1) * TP],
                                              func=AF.Exp,
                                              scale=ca[:, t_i * NS + n:t_i * NS + n + 1])
                            p.inc(li, "cs")

            prev = None
            prev_qbs = 0
            prevcv_prev = None
            for j in range(NC2):
                ctb = p.c["ct"]
                csb = p.c["cs"]
                cvb = p.c["cv"]

                @blk.sync
                def _(sy, j=j, prev=prev):
                    if j == 0:
                        p.wait(sy, "qs", p.c["qs"])
                        p.wait(sy, "qt0", p.c["qt0"])
                        p.wait(sy, "qt1", p.c["qt1"])
                        p.wait(sy, "cv", p.c["cv"])
                        p.wait(sy, "cs", p.c["cs"])
                        p.wait(sy, "ct", p.c["ct"])
                    else:
                        (pq, pct, pcs, pcv) = prev
                        p.wait(sy, "cv", pcv + 5)
                        p.wait(sy, "ct", pct + 32)
                    for t_i in range(DT_I):
                        p.inc(sy.dma_start(out=xi2[:, t_i * TP:(t_i + 1) * TP],
                                           in_=xi2_d[:, t_i * T + j * TP: t_i * T + j * TP + TP]),
                              "qs", 16)
                qs_xi2 = p.c["qs"]


                qwb = (p.c["qw0"], p.c["qw1"])

                @blk.sync
                def _(sy, j=j, ctb=ctb):
                    for g in range(2):
                        c0 = 0 if g == 0 else 12 * 128
                        cw = 12 * 128 if g == 0 else (DI + 2 * NS - 12 * 128)
                        for k in range(DT_I):
                            p.wait(sy, "ct", ctb + 16 * g + k - 1)
                            p.inc(sy.dma_start(out=wxb[k % 2][:, 0:cw],
                                               in_=wxT_d[k * 128:(k + 1) * 128, c0:c0 + cw]),
                                  f"qw{k % 2}", 16)

                @blk.tensor
                def _(t, j=j, prev=prev, ctb=ctb, csb=csb, cvb=cvb, qwb=qwb,
                      qs_xi2=qs_xi2):
                    for g, grp in enumerate(GRPS):
                        if g == 0:
                            p.wait(t, "qs", qs_xi2)
                            if prev is not None:
                                p.wait(t, "cs", prev[2] + (228 if j >= 2 else 4))
                        else:
                            p.wait(t, "cs", csb + (193 if j >= 1 else 1))
                        for k in range(DT_I):
                            p.wait(t, f"qw{k % 2}",
                                   qwb[k % 2] + 16 * (8 * g + k // 2 + 1))
                            li = None
                            for bi, o in enumerate(grp):
                                w = 32 if o == NO - 1 else 128
                                lc = bi * 128
                                bank = pb[bi // 2]
                                half = (bi % 2) * TP
                                dst = (bank[:, half:half + TP] if w == 128
                                       else bank[:32, half:half + TP])
                                st_ = (k == 0 and bi % 2 == 0)
                                sp_ = (k == DT_I - 1 and (bi % 2 == 1
                                                          or bi == len(grp) - 1))
                                li = t.matmul(dst, wxb[k % 2][:, lc:lc + w],
                                              xi2[:, k * TP:(k + 1) * TP],
                                              start=st_, stop=sp_)
                            p.inc(li, "ct")

                @blk.sync
                def _(sy, j=j, csb=csb):
                    p.wait(sy, "cs", csb + (230 if j >= 1 else 6))
                    p.inc(sy.dma_start(out=bcd_d[:, j * TP:(j + 1) * TP], in_=bc[:, 0:TP]),
                          "qbs", 16)
                qbs_j = p.c["qbs"]
                @blk.sync
                def _(sy, j=j, prev=prev):
                    if j >= 1:
                        p.wait(sy, "cv", prev[3] + 262)
                    else:
                        p.wait(sy, "qt0", p.c["qt0"])
                        p.wait(sy, "qt1", p.c["qt1"])
                    for t_i in range(DT_I):
                        p.inc(sy.dma_start(out=srs[:, t_i * TP:(t_i + 1) * TP],
                                           in_=sres_d[:, t_i * T + j * TP: t_i * T + j * TP + TP]),
                              "qr", 16)
                qt0_sres = p.c["qr"]

                if prev is not None:
                    @blk.sync
                    def _(sy, j=j, prev=prev):
                        p.wait(sy, "cv", prev[3] + 262)
                        jp = j - 1
                        for t_i in range(DT_I):
                            p.inc(sy.dma_start(
                                out=v_d[:, t_i * T + jp * TP: t_i * T + jp * TP + TP],
                                in_=sy_[:, t_i * TP:(t_i + 1) * TP]), "qv", 16)

                daB = csb
                if prev is not None:
                    emit_dA_part(j - 1, list(range(0, 12)), daB, prev[3],
                                 prev_qbs, prevcv_prev)

                def emit_sp(g, j=j, prev=prev, ctb=ctb, csb=csb, cvb=cvb,
                            pqbs=prev_qbs):
                    @blk.scalar
                    def _(s, g=g):
                        grp = GRPS[g]
                        if g == 0:
                            p.wait(s, "cs", csb)
                            if prev is not None:
                                p.wait(s, "cv", prev[3] + 3)

                        p.wait(s, "ct", ctb + 16 * (g + 1))
                        dts = [(bi + 12 * g, o) for bi, o in enumerate(grp) if o < DI // 128]
                        li = None
                        for sl, o in dts:
                            lb = sl - 12 * g
                            bank = pb[lb // 2]
                            half = (lb % 2) * TP
                            li = s.activation(out=sp12[:, sl * TP:(sl + 1) * TP],
                                              in_=bank[:, half:half + TP],
                                              func=AF.Abs, bias=cbxd[:, o:o + 1])
                        for sl, o in dts:
                            lb = sl - 12 * g
                            bank = pb[lb // 2]
                            half = (lb % 2) * TP
                            li = s.activation(out=sdel[j % 2][:, o * TP:(o + 1) * TP],
                                              in_=bank[:, half:half + TP],
                                              func=AF.Relu, bias=cbxd[:, o:o + 1])
                        p.inc(li, "cs")
                        p.edge(s, "cs")
                        li = None
                        for sl, o in dts:
                            li = s.activation(out=sp12[:, sl * TP:(sl + 1) * TP],
                                              in_=sp12[:, sl * TP:(sl + 1) * TP],
                                              func=AF.Exp, scale=-1.0)
                        p.inc(li, "cs")
                        p.edge(s, "cs")
                        li = None
                        for sl, o in dts:
                            li = s.activation(out=sp12[:, sl * TP:(sl + 1) * TP],
                                              in_=sp12[:, sl * TP:(sl + 1) * TP],
                                              func=AF.Ln, bias=1.0)
                        if g == 1:
                            p.wait(s, "qbs", pqbs)
                            li = s.activation(out=bc[:, 0:TP], in_=pb[2][:32, 0:TP],
                                              func=AF.Identity, bias=cbxbc[:, 0:1])
                        p.inc(li, "cs")

                emit_sp(0)
                if prev is not None:
                    emit_dA_part(j - 1, [12, 13], daB, prev[3], prev_qbs,
                                 prevcv_prev)
                emit_sp(1)
                if prev is not None:
                    emit_dA_part(j - 1, [14, 15], daB, prev[3], prev_qbs,
                                 prevcv_prev)
                qb_j = (p.c["qb0"], p.c["qb1"])

                @blk.vector
                def _(v, j=j, prev=prev, ctb=ctb, csb=csb, cvb=cvb, qb_j=qb_j,
                      qt0_sres=qt0_sres, qt1_vst=p.c["qv"]):
                    for g, grp in enumerate(GRPS):
                        spb = csb + (192 if j >= 1 else 0)
                        p.wait(v, "cs", spb + (3 if g == 0 else
                                               (38 if j >= 1 else 6)))
                        dts = [(bi + 12 * g, o) for bi, o in enumerate(grp) if o < DI // 128]
                        li = None
                        for sl, o in dts:
                            li = v.tensor_tensor(out=sdel[j % 2][:, o * TP:(o + 1) * TP],
                                                 in0=sdel[j % 2][:, o * TP:(o + 1) * TP],
                                                 in1=sp12[:, sl * TP:(sl + 1) * TP],
                                                 op=ALU.add)
                        p.inc(li, "cv")
                        p.edge(v, "cv")
                        li = None
                        for sl, o in dts:
                            li = v.tensor_tensor(out=sdtx[j % 2][:, o * TP:(o + 1) * TP],
                                                 in0=sdel[j % 2][:, o * TP:(o + 1) * TP],
                                                 in1=xi2[:, o * TP:(o + 1) * TP],
                                                 op=ALU.mult)
                        p.inc(li, "cv")
                    p.wait(v, "qv", qt1_vst)
                    p.edge(v, "cv")
                    li = None
                    for t_i in range(DT_I):
                        li = v.tensor_scalar(out=sy_[:, t_i * TP:(t_i + 1) * TP],
                                             in0=xi2[:, t_i * TP:(t_i + 1) * TP],
                                             scalar1=cdv[:, t_i:t_i + 1],
                                             scalar2=None, op0=ALU.mult)
                    p.inc(li, "cv")
                    daB2 = csb + (6 if j == 0 else 262)
                    for n in range(NS):
                        p.wait(v, f"qb{n % 2}", qb_j[n % 2] + 32 * (n // 2 + 1))
                        bap_b = bcb[n % 2][:, 0:1, 0:TP].broadcast_to([128, 4, TP])
                        bap_c = bcb[n % 2][:, 0:1, TP:2 * TP].broadcast_to([128, 4, TP])
                        for tp_ in range(0, DT_I, 4):
                            qi = tp_ // 4
                            p.wait(v, "cs", daB2 + da_off(j, n) + tp_ + 4)
                            li = v.tensor_tensor(out=dbx[qi % 2][:],
                                                 in0=sdtx[j % 2][:, tp_ * TP:(tp_ + 4) * TP],
                                                 in1=bap_b, op=ALU.mult)
                            p.inc(li, "cv")
                            p.edge(v, "cv")
                            li = None
                            for t_i in range(tp_, tp_ + 4):
                                li = v.tensor_tensor_scan(
                                    out=h_all[:, (t_i % 8) * TP:(t_i % 8 + 1) * TP],
                                    data0=da[:, t_i * TP:(t_i + 1) * TP],
                                    data1=dbx[qi % 2][:, (t_i - tp_) * TP:(t_i - tp_ + 1) * TP],
                                    initial=hstate[:, t_i * NS + n:t_i * NS + n + 1],
                                    op0=ALU.mult, op1=ALU.add)
                            p.inc(li, "cv")
                            p.edge(v, "cv")
                            li = v.tensor_tensor(out=zt[qi % 2][:],
                                                 in0=h_all[:, (tp_ % 8) * TP:(tp_ % 8 + 4) * TP],
                                                 in1=bap_c, op=ALU.mult)
                            p.inc(li, "cv")
                            p.edge(v, "cv")
                            li = v.tensor_tensor(out=sy_[:, tp_ * TP:(tp_ + 4) * TP],
                                                 in0=sy_[:, tp_ * TP:(tp_ + 4) * TP],
                                                 in1=zt[qi % 2][:], op=ALU.add)
                            if tp_ % 8 == 4:
                                base = tp_ - 4
                                li = v.tensor_copy(
                                    out=hstate[:, base * NS + n:(tp_ + 4) * NS:NS],
                                    in_=h_all[:, TP - 1::TP])
                            p.inc(li, "cv")
                            p.edge(v, "cv")
                    p.wait(v, "qr", qt0_sres)
                    p.edge(v, "cv")
                    li = None
                    for t_i in range(DT_I):
                        li = v.tensor_tensor(out=sy_[:, t_i * TP:(t_i + 1) * TP],
                                             in0=sy_[:, t_i * TP:(t_i + 1) * TP],
                                             in1=srs[:, t_i * TP:(t_i + 1) * TP], op=ALU.mult)
                    p.inc(li, "cv")
                assert p.c["cv"] == cvb + 262, (p.c["cv"], cvb)


                prev_qbs = qbs_j
                prevcv_prev = prev[3] if prev is not None else None
                prev = (qs_xi2, ctb, csb, cvb)
                if j == NC2 - 1:
                    @blk.sync
                    def _(sy, j=j, cvb=cvb):
                        p.wait(sy, "cv", cvb + 262)
                        for t_i in range(DT_I):
                            p.inc(sy.dma_start(
                                out=v_d[:, t_i * T + j * TP: t_i * T + j * TP + TP],
                                in_=sy_[:, t_i * TP:(t_i + 1) * TP]), "qv", 16)
                    emit_dA_part(j, list(range(NS)), csb + (6 if j == 0 else 262),
                                 cvb, qbs_j, prevcv_prev)


        with contextlib.ExitStack() as es3:
            sb3 = lambda name, shape: es3.enter_context(nc.sbuf_tensor(name, shape, F32))
            wob = [sb3(f"wo{i}", [128, DM]) for i in range(2)]
            bf3 = lambda name, shape: es3.enter_context(nc.sbuf_tensor(name, shape, BF16))
            wobh = [bf3(f"woh{i}", [128, DM]) for i in range(2)]
            wobl = [bf3(f"wol{i}", [128, DM]) for i in range(2)]
            wt = [sb3(f"wt{k}", [128, DM]) for k in range(DT_M)]
            wtb = [bf3(f"wtb{k}", [128, DM]) for k in range(DT_M)]
            vv = sb3("p3_v", [128, DT_I * TC])
            vvh = bf3("p3_vh", [128, DT_I * TC])
            vvl = bf3("p3_vl", [128, DT_I * TC])
            x3 = sb3("p3_x", [128, NTT * DM])
            xnf3h = bf3("p3_xnfh", [128, DT_M * TC])
            xnf3l = bf3("p3_xnfl", [128, DT_M * TC])
            ss = sb3("p3_s", [128, DT_M * TC])
            sg = sb3("p3_g", [128, DT_M * TC])
            thr = sb3("p3_thr", [128, 2])
            red8 = sb3("p3_red", [128, DT_M + 2])

            @blk.sync
            def _(sy):
                p.wait(sy, "qs", p.c["qs"])
                p.wait(sy, "qt0", p.c["qt0"])
                p.wait(sy, "qt1", p.c["qt1"])
                p.wait(sy, "qv", p.c["qv"])
                p.wait(sy, "qr", p.c["qr"])
                p.wait(sy, "qb0", p.c["qb0"])
                p.wait(sy, "qb1", p.c["qb1"])
                p.wait(sy, "cv", p.c["cv"])
                p.wait(sy, "cs", p.c["cs"])
                p.wait(sy, "ct", p.c["ct"])
                for k in range(DT_M):
                    p.inc(sy.dma_start(out=wt[k][:], in_=wfT_d[k * 128:(k + 1) * 128, :]),
                          "qs", 16)
            qs_wf = p.c["qs"]

            @blk.vector
            def _(v):
                p.wait(v, "qs", qs_wf)
                li = None
                for k in range(DT_M):
                    li = v.tensor_reduce(out=red8[:, k:k + 1], in_=wt[k][:], axis=AX.X,
                                         op=ALU.add, apply_absolute_value=True)
                p.inc(li, "cv")
                p.edge(v, "cv")
                li = v.tensor_reduce(out=red8[:, DT_M:DT_M + 1], in_=red8[:, 0:DT_M], axis=AX.X,
                                     op=ALU.add)
                p.inc(li, "cv")
            cv_red = p.c["cv"]

            @blk.tensor
            def _(t):
                p.wait(t, "cv", cv_red)
                li = t.matmul(pb[0][:1, 0:1], red8[:, DT_M:DT_M + 1], ones_col[:],
                              start=True, stop=True)
                p.inc(li, "ct")
            ct_tot = p.c["ct"]

            @blk.scalar
            def _(s):
                p.wait(s, "ct", ct_tot)
                li = s.activation(out=thr[:1, 0:1], in_=pb[0][:1, 0:1], func=AF.Copy,
                                  scale=0.7 / float(DM * DM))
                p.inc(li, "cs")
            cs_thr = p.c["cs"]

            @blk.tensor
            def _(t):
                p.wait(t, "cs", cs_thr)
                li = t.matmul(pb[1][:, 0:1], ones_row[:1, :], thr[:1, 0:1], start=True, stop=True)
                p.inc(li, "ct")
            ct_thr = p.c["ct"]

            @blk.vector
            def _(v):
                p.wait(v, "ct", ct_thr)
                li = v.tensor_copy(out=thr[:, 0:1], in_=pb[1][:, 0:1])
                p.inc(li, "cv")
                p.edge(v, "cv")
                li = v.tensor_scalar(out=thr[:, 1:2], in0=thr[:, 0:1], scalar1=-1.0,
                                     scalar2=None, op0=ALU.mult)
                p.inc(li, "cv")
                p.edge(v, "cv")
                for kh in range(2):
                    ks = range(kh * 4, kh * 4 + 4)
                    li = None
                    for k in ks:
                        li = v.tensor_scalar(out=sg[:, (k % 4) * DM:(k % 4 + 1) * DM],
                                             in0=wt[k][:], scalar1=thr[:, 0:1],
                                             scalar2=None, op0=ALU.is_gt)
                    for k in ks:
                        li = v.tensor_scalar(out=ss[:, (k % 4) * DM:(k % 4 + 1) * DM],
                                             in0=wt[k][:], scalar1=thr[:, 1:2],
                                             scalar2=None, op0=ALU.is_lt)
                    p.inc(li, "cv")
                    p.edge(v, "cv")
                    li = None
                    for k in ks:
                        li = v.tensor_tensor(out=wt[k][:],
                                             in0=sg[:, (k % 4) * DM:(k % 4 + 1) * DM],
                                             in1=ss[:, (k % 4) * DM:(k % 4 + 1) * DM],
                                             op=ALU.subtract)
                    p.inc(li, "cv")
                    p.edge(v, "cv")
                    li = None
                    for k in ks:
                        li = v.tensor_copy(out=wtb[k][:], in_=wt[k][:])
                    p.inc(li, "cv")
                    p.edge(v, "cv")

            for j in range(NCH):
                cvb = p.c["cv"]
                ctb = p.c["ct"]
                csb = p.c["cs"]

                @blk.sync
                def _(sy, j=j):
                    p.wait(sy, "qs", p.c["qs"])
                    if j >= 1:
                        p.wait(sy, "cv", p.c["cv"])
                        p.wait(sy, "ct", p.c["ct"])
                    for t_i in range(DT_I):
                        p.inc(sy.dma_start(out=vv[:, t_i * TC:(t_i + 1) * TC],
                                           in_=v_d[:, t_i * T + j * TC: t_i * T + j * TC + TC]),
                              "qs", 16)
                    for tt in range(NTT):
                        p.inc(sy.dma_start(out=x3[:, tt * DM:(tt + 1) * DM],
                                           in_=x_d[j * TC + tt * 128: j * TC + (tt + 1) * 128, :]),
                              "qs", 16)
                qs_v = p.c["qs"]

                @blk.vector
                def _(v, j=j, qs_v=qs_v):
                    p.wait(v, "qs", qs_v)
                    li = None
                    for tt in range(NTT):
                        g = j * NTT + tt
                        li = v.tensor_scalar(out=x3[:, tt * DM:(tt + 1) * DM],
                                             in0=x3[:, tt * DM:(tt + 1) * DM],
                                             scalar1=mv_all[:, 2 * g:2 * g + 1],
                                             scalar2=rstd_all[:, g:g + 1],
                                             op0=ALU.subtract, op1=ALU.mult)
                    p.inc(li, "cv")

                qw_base3 = (p.c["qw0"], p.c["qw1"])
                czb = p.c["cz"]

                @blk.sync
                def _(sy, j=j, czb=czb):
                    for k in range(DT_I):
                        if k >= 2:
                            p.wait(sy, "cz", czb + 4 + 2 * (k - 2))
                        p.inc(sy.dma_start(out=wob[k % 2][:],
                                           in_=woT_d[k * 128:(k + 1) * 128, :]),
                              f"qw{k % 2}", 16)

                @blk.vector
                def _(v, j=j, qs_v=qs_v, qw_base3=qw_base3):
                    p.wait(v, "qs", qs_v)
                    li = None
                    for q_ in range(4):
                        li = v.tensor_copy(out=vvh[:, q_ * 4 * TC:(q_ + 1) * 4 * TC],
                                           in_=vv[:, q_ * 4 * TC:(q_ + 1) * 4 * TC])
                    p.inc(li, "cz")
                    p.edge(v, "cz")
                    li = None
                    for q_ in range(4):
                        li = v.tensor_tensor(out=vvl[:, q_ * 4 * TC:(q_ + 1) * 4 * TC],
                                             in0=vv[:, q_ * 4 * TC:(q_ + 1) * 4 * TC],
                                             in1=vvh[:, q_ * 4 * TC:(q_ + 1) * 4 * TC],
                                             op=ALU.subtract)
                    p.inc(li, "cz")
                    for k in range(2):
                        p.wait(v, f"qw{k % 2}", qw_base3[k % 2] + 16)
                        p.inc(v.tensor_copy(out=wobh[k % 2][:], in_=wob[k % 2][:]), "cz")
                        p.edge(v, "cz")
                        p.inc(v.tensor_tensor(out=wobl[k % 2][:], in0=wob[k % 2][:],
                                              in1=wobh[k % 2][:], op=ALU.subtract), "cz")

                @blk.tensor
                def _(t, j=j, cvb=cvb, csb=csb, czb=czb):
                    p.wait(t, "cv", cvb + 1)
                    for grp in range(DT_M):
                        if grp >= 2:
                            p.wait(t, "cv", cvb + 2 * grp - 1)
                        bank = pb[grp % 2]
                        li = None
                        for i in range(4):
                            idx = grp * 4 + i
                            tt, c = idx // DT_M, idx % DT_M
                            li = t.transpose(bank[:, i * 128:(i + 1) * 128],
                                             x3[:, tt * DM + c * 128: tt * DM + (c + 1) * 128],
                                             iden[:])
                        p.inc(li, "ct")
                    p.wait(t, "cv", cvb + 17)
                    for k in range(DT_I):
                        p.wait(t, "cz", czb + 4 + 2 * k)
                        li = None
                        for o in range(DT_M):
                            for wm, xm, tm in ((wobh, vvh, 0), (wobh, vvl, 1),
                                               (wobl, vvh, 2)):
                                li = t.matmul(pb[o][:, 0:TC],
                                              wm[k % 2][:, o * 128:(o + 1) * 128],
                                              xm[:, k * TC:(k + 1) * TC],
                                              start=(k == 0 and tm == 0),
                                              stop=(k == DT_I - 1 and tm == 2))
                        p.inc(li, "ct")
                    p.wait(t, "cs", csb + 1)
                    li = None
                    for o in range(DT_M):
                        for k in range(DT_M):
                            for xm, tm in ((xnf3h, 0), (xnf3l, 1)):
                                li = t.matmul(pb[o][:, 0:TC], wtb[k][:, o * 128:(o + 1) * 128],
                                              xm[:, k * TC:(k + 1) * TC],
                                              start=(k == 0 and tm == 0),
                                              stop=(k == DT_M - 1 and tm == 1))
                    p.inc(li, "ct")
                    p.wait(t, "cv", cvb + 18)
                    for grp in range(DT_M):
                        if grp >= 2:
                            p.wait(t, "cv", cvb + 18 + (grp - 1))
                        bank = pb[grp % 2]
                        li = None
                        for i in range(4):
                            idx = grp * 4 + i
                            c, tt = idx // NTT, idx % NTT
                            li = t.transpose(bank[:, i * 128:(i + 1) * 128],
                                             ss[:, c * TC + tt * 128: c * TC + (tt + 1) * 128],
                                             iden[:])
                        p.inc(li, "ct")

                @blk.vector
                def _(v, j=j, ctb=ctb):
                    for grp in range(DT_M):
                        p.wait(v, "ct", ctb + grp + 1)
                        bank = pb[grp % 2]
                        sls = []
                        for i in range(4):
                            idx = grp * 4 + i
                            tt, c = idx // DT_M, idx % DT_M
                            sls.append((slice(c * TC + tt * 128, c * TC + (tt + 1) * 128), i))
                        li = None
                        for sl, i in sls:
                            li = v.tensor_copy(out=xnf3h[:, sl],
                                               in_=bank[:, i * 128:(i + 1) * 128])
                        p.inc(li, "cv")
                        p.edge(v, "cv")
                        li = None
                        for sl, i in sls:
                            li = v.tensor_tensor(out=xnf3l[:, sl],
                                                 in0=bank[:, i * 128:(i + 1) * 128],
                                                 in1=xnf3h[:, sl], op=ALU.subtract)
                        p.inc(li, "cv")

                @blk.vector
                def _(v, j=j, ctb=ctb, czb=czb, qw_base3=qw_base3):
                    for k in range(2, DT_I):
                        p.wait(v, f"qw{k % 2}", qw_base3[k % 2] + 16 * (k // 2 + 1))
                        p.wait(v, "ct", ctb + 8 + (k - 2) + 1)
                        p.inc(v.tensor_copy(out=wobh[k % 2][:], in_=wob[k % 2][:]), "cz")
                        p.edge(v, "cz")
                        p.inc(v.tensor_tensor(out=wobl[k % 2][:], in0=wob[k % 2][:],
                                              in1=wobh[k % 2][:], op=ALU.subtract), "cz")

                @blk.scalar
                def _(s, j=j, ctb=ctb):
                    p.wait(s, "ct", ctb + 8 + 16)
                    li = None
                    for o in range(DT_M):
                        li = s.activation(out=ss[:, o * TC:(o + 1) * TC], in_=pb[o][:, 0:TC],
                                          func=AF.Identity, bias=cbout[:, o:o + 1])
                    p.inc(li, "cs")
                    p.wait(s, "ct", ctb + 8 + 16 + 1)
                    li = None
                    for o in range(DT_M):
                        li = s.activation(out=sg[:, o * TC:(o + 1) * TC], in_=pb[o][:, 0:TC],
                                          func=AF.Silu, bias=cbffn[:, o:o + 1], scale=alph[:, 0:1])
                    p.inc(li, "cs")

                @blk.vector
                def _(v, j=j, csb=csb):
                    p.wait(v, "cs", csb + 2)
                    li = None
                    for o in range(DT_M):
                        li = v.tensor_tensor(out=ss[:, o * TC:(o + 1) * TC],
                                             in0=ss[:, o * TC:(o + 1) * TC],
                                             in1=sg[:, o * TC:(o + 1) * TC], op=ALU.mult)
                    p.inc(li, "cv")

                @blk.sync
                def _(sy, j=j, ctb=ctb):
                    p.wait(sy, "ct", ctb + 8)
                    for tt in range(NTT):
                        p.inc(sy.dma_start(out=x3[:, tt * DM:(tt + 1) * DM],
                                           in_=x_d[j * TC + tt * 128: j * TC + (tt + 1) * 128, :]),
                              "qs", 16)
                qs_x3 = p.c["qs"]

                @blk.vector
                def _(v, j=j, ctb=ctb, qs_x3=qs_x3):
                    p.wait(v, "qs", qs_x3)
                    for grp in range(DT_M):
                        p.wait(v, "ct", ctb + 8 + 16 + 1 + grp + 1)
                        bank = pb[grp % 2]
                        li = None
                        for i in range(4):
                            idx = grp * 4 + i
                            c, tt = idx // NTT, idx % NTT
                            li = v.tensor_tensor(out=x3[:, tt * DM + c * 128: tt * DM + (c + 1) * 128],
                                                 in0=x3[:, tt * DM + c * 128: tt * DM + (c + 1) * 128],
                                                 in1=bank[:, i * 128:(i + 1) * 128], op=ALU.add)
                        p.inc(li, "cv")

                @blk.sync
                def _(sy, j=j, cvb=cvb):
                    p.wait(sy, "cv", cvb + 26)
                    for tt in range(NTT):
                        p.inc(sy.dma_start(out=out_d[j * TC + tt * 128: j * TC + (tt + 1) * 128, :],
                                           in_=x3[:, tt * DM:(tt + 1) * DM]), "qs", 16)

        @blk.gpsimd
        def _(g):
            p.wait(g, "qs", p.c["qs"])
            p.wait(g, "qg", p.c["qg"])
            p.wait(g, "qt0", p.c["qt0"])
            p.wait(g, "qt1", p.c["qt1"])
            p.wait(g, "qb0", p.c["qb0"])
            p.wait(g, "qb1", p.c["qb1"])
            p.wait(g, "qbs", p.c["qbs"])
            p.wait(g, "qr", p.c["qr"])
            p.wait(g, "qv", p.c["qv"])
            p.wait(g, "qh0", p.c["qh0"])
            p.wait(g, "qh1", p.c["qh1"])

    return nc


def kernel(L0_attn_x, L0_attn_W_in, L0_attn_b_in, L0_attn_W_conv, L0_attn_b_conv,
           L0_attn_W_x, L0_attn_b_x, L0_attn_W_out, L0_attn_b_out, L0_attn_A_log,
           L0_attn_D, L0_ffn_weight, L0_ffn_alpha, L0_ffn_bias):
    f32 = np.float32
    c_ = np.ascontiguousarray
    x = c_(np.asarray(L0_attn_x, f32))

    winT = c_(np.asarray(L0_attn_W_in, f32).T)
    wxT = c_(np.asarray(L0_attn_W_x, f32).T)
    woT = c_(np.asarray(L0_attn_W_out, f32).T)
    wfT = c_(np.asarray(L0_ffn_weight, f32).T)
    b_in = np.asarray(L0_attn_b_in, f32)
    b_x = np.asarray(L0_attn_b_x, f32)

    def col128(vec, nt):
        return c_(vec.reshape(nt, 128).T)

    cb_in_xi = col128(b_in[:DI], DT_I)
    cb_in_res = col128(b_in[DI:], DT_I)
    cw_conv = c_(np.asarray(L0_attn_W_conv, f32)[:, 0, :].reshape(DT_I, 128, 4)
                 .transpose(1, 0, 2).reshape(128, DT_I * 4))
    cb_conv = col128(np.asarray(L0_attn_b_conv, f32), DT_I)
    cb_x_d = col128(b_x[:DI], DT_I)
    cb_x_bc = c_(b_x[DI:].reshape(32, 1))
    cb_out = col128(np.asarray(L0_attn_b_out, f32), DT_M)
    c_d = col128(np.asarray(L0_attn_D, f32), DT_I)
    c_a_log = c_(np.asarray(L0_attn_A_log, f32).reshape(DT_I, 128, NS)
                 .transpose(1, 0, 2).reshape(128, DT_I * NS))
    c_alpha = c_(np.broadcast_to(np.asarray(L0_ffn_alpha, f32).reshape(1, 1), (128, 1)))
    cb_ffn = col128(np.asarray(L0_ffn_bias, f32), DT_M)
    iden = np.eye(128, dtype=f32)
    zeros3 = np.zeros((3, DM), f32)
    zmask = np.zeros((128, 1), f32)

    shared = dict(w_in_t=winT, w_x_t=wxT, w_out_t=woT, w_ffn_t=wfT,
                  cb_in_xi=cb_in_xi, cb_in_res=cb_in_res, cw_conv=cw_conv,
                  cb_conv=cb_conv, cb_x_d=cb_x_d, cb_x_bc=cb_x_bc, cb_out=cb_out,
                  c_d=c_d, c_a_log=c_a_log, c_alpha=c_alpha, cb_ffn=cb_ffn, iden=iden)

    in_maps = []
    for c in range(8):
        seq = c % 4
        m = dict(shared)
        m["x"] = c_(x[seq])
        m["x_halo3"] = zeros3
        m["halo_mask"] = zmask
        in_maps.append(m)

    nc = build_nc()
    res = run_bass_kernel_spmd(nc, in_maps, core_ids=list(range(8)))
    out = np.empty((4, 4096, DM), f32)
    for seq in range(4):
        out[seq] = res.results[seq]["out"]
    return out


# version 23
# speedup vs baseline: 1.2099x; 1.0094x over previous
import contextlib
import numpy as np
import concourse.bass as bass
import concourse.mybir as mybir
from concourse.bass_utils import run_bass_kernel_spmd

F32 = mybir.dt.float32
BF16 = mybir.dt.bfloat16
AF = mybir.ActivationFunctionType
ALU = mybir.AluOpType
AX = mybir.AxisListType

DM = 1024
DI = 2048
NS = 16
DT_M = DM // 128
DT_I = DI // 128
EPS = 1e-5

import os as _os
HALF = False
T = int(_os.environ.get("MK_T", "0")) or (2048 if HALF else 4096)
TC = 512
NCH = T // TC
NTT = TC // 128
XS = TC + 3


class P:
    def __init__(self, nc, es):
        self.nc = nc
        self.s = {}
        self.c = {}
        for name in ("qs", "qg", "ct", "cv", "cs", "qw0", "qw1", "qt0", "qt1", "qb0", "qb1", "qbs", "qr", "qv", "qh0", "qh1", "cz"):
            self.s[name] = es.enter_context(nc.semaphore(name))
            self.c[name] = 0

    def inc(self, instr, name, k=1):
        instr.then_inc(self.s[name], k)
        self.c[name] += k
        return self.c[name]

    def wait(self, eng, name, val=None):
        if val is None:
            val = self.c[name]
        if val > 0:
            eng.wait_ge(self.s[name], val)

    def edge(self, eng, name):
        self.wait(eng, name, self.c[name])


def build_nc():
    nc = bass.Bass("TRN2", target_bir_lowering=False)

    x_d = nc.dram_tensor("x", [T, DM], F32, kind="ExternalInput").ap()
    xh_d = nc.dram_tensor("x_halo3", [3, DM], F32, kind="ExternalInput").ap()
    hm_d = nc.dram_tensor("halo_mask", [128, 1], F32, kind="ExternalInput").ap()
    winT_d = nc.dram_tensor("w_in_t", [DM, 2 * DI], F32, kind="ExternalInput").ap()
    wxT_d = nc.dram_tensor("w_x_t", [DI, DI + 2 * NS], F32, kind="ExternalInput").ap()
    woT_d = nc.dram_tensor("w_out_t", [DI, DM], F32, kind="ExternalInput").ap()
    wfT_d = nc.dram_tensor("w_ffn_t", [DM, DM], F32, kind="ExternalInput").ap()
    cbixi_d = nc.dram_tensor("cb_in_xi", [128, DT_I], F32, kind="ExternalInput").ap()
    cbires_d = nc.dram_tensor("cb_in_res", [128, DT_I], F32, kind="ExternalInput").ap()
    cwconv_d = nc.dram_tensor("cw_conv", [128, DT_I * 4], F32, kind="ExternalInput").ap()
    cbconv_d = nc.dram_tensor("cb_conv", [128, DT_I], F32, kind="ExternalInput").ap()
    cbxd_d = nc.dram_tensor("cb_x_d", [128, DT_I], F32, kind="ExternalInput").ap()
    cbxbc_d = nc.dram_tensor("cb_x_bc", [32, 1], F32, kind="ExternalInput").ap()
    cbout_d = nc.dram_tensor("cb_out", [128, DT_M], F32, kind="ExternalInput").ap()
    cd_d = nc.dram_tensor("c_d", [128, DT_I], F32, kind="ExternalInput").ap()
    calog_d = nc.dram_tensor("c_a_log", [128, DT_I * NS], F32, kind="ExternalInput").ap()
    alpha_d = nc.dram_tensor("c_alpha", [128, 1], F32, kind="ExternalInput").ap()
    cbffn_d = nc.dram_tensor("cb_ffn", [128, DT_M], F32, kind="ExternalInput").ap()
    iden_d = nc.dram_tensor("iden", [128, 128], F32, kind="ExternalInput").ap()
    out_d = nc.dram_tensor("out", [T, DM], F32, kind="ExternalOutput").ap()
    xi2_d = nc.dram_tensor("xi2_s", [128, DT_I * T], F32, kind="Internal").ap()
    sres_d = nc.dram_tensor("sres_s", [128, DT_I * T], F32, kind="Internal").ap()
    v_d = nc.dram_tensor("v_s", [128, DT_I * T], F32, kind="Internal").ap()
    bcd_d = nc.dram_tensor("bc_s", [32, T], F32, kind="Internal").ap()

    es = contextlib.ExitStack()
    with es:
        p = P(nc, es)
        sb = lambda name, shape: es.enter_context(nc.sbuf_tensor(name, shape, F32))

        cbixi = sb("cbixi", [128, DT_I])
        cbires = sb("cbires", [128, DT_I])
        cwconv = sb("cwconv", [128, DT_I * 4])
        cbconv = sb("cbconv", [128, DT_I])
        cbxd = sb("cbxd", [128, DT_I])
        cbxbc = sb("cbxbc", [32, 1])
        cbout = sb("cbout", [128, DT_M])
        cdv = sb("cdv", [128, DT_I])
        ca = sb("ca", [128, DT_I * NS])
        alph = sb("alph", [128, 1])
        cbffn = sb("cbffn", [128, DT_M])
        iden = sb("iden_sb", [128, 128])
        hmask = sb("hmask", [128, 1])
        epsb = sb("epsb", [128, 1])
        ones_row = sb("ones_row", [1, 128])
        ones_col = sb("ones_col", [128, 1])
        mv_all = sb("mv_all", [128, 2 * (T // 128)])
        rstd_all = sb("rstd_all", [128, T // 128])
        hstate = sb("hstate", [128, DT_I * NS])
        xihalo = sb("xihalo", [128, DT_I * 3])

        pb = [es.enter_context(nc.psum_tensor(f"pb{i}", [128, 512], F32)) for i in range(8)]

        blk = es.enter_context(nc.Block())

        @blk.gpsimd
        def _(g):
            for dst, src in ((cbixi, cbixi_d), (cbires, cbires_d), (cwconv, cwconv_d),
                             (cbconv, cbconv_d), (cbxd, cbxd_d), (cbxbc, cbxbc_d),
                             (cbout, cbout_d), (cdv, cd_d), (ca, calog_d),
                             (alph, alpha_d), (cbffn, cbffn_d), (iden, iden_d),
                             (hmask, hm_d)):
                p.inc(g.dma_start(out=dst[:], in_=src[:]), "qg", 16)

        @blk.vector
        def _(v):
            p.wait(v, "qg")
            v.memset(ones_row[:], 1.0)
            v.memset(epsb[:], EPS)
            v.memset(ones_col[:], 1.0)
            v.memset(hstate[:], 0.0)
            p.inc(v.memset(xihalo[:], 0.0), "cv")

        @blk.scalar
        def _(s):
            p.wait(s, "qg")
            p.wait(s, "cv")
            li = None
            for t in range(DT_I):
                li = s.activation(out=ca[:, t * NS:(t + 1) * NS],
                                  in_=ca[:, t * NS:(t + 1) * NS], func=AF.Exp)
            p.inc(li, "cs")

        @blk.vector
        def _(v):
            p.wait(v, "cs")
            li = None
            for t in range(DT_I):
                li = v.tensor_scalar(out=ca[:, t * NS:(t + 1) * NS],
                                     in0=ca[:, t * NS:(t + 1) * NS],
                                     scalar1=-1.0, scalar2=None, op0=ALU.mult)
            p.inc(li, "cv")

        with contextlib.ExitStack() as es1:
            sb1 = lambda name, shape: es1.enter_context(nc.sbuf_tensor(name, shape, F32))
            winh = [es1.enter_context(nc.sbuf_tensor(f"winh{k}", [128, 2 * DI], BF16))
                    for k in range(DT_M)]
            winl = [es1.enter_context(nc.sbuf_tensor(f"winl{k}", [128, 2 * DI], BF16))
                    for k in range(DT_M)]
            x1 = sb1("p1_x", [128, NTT * DM])
            bns = sb1("p1_bns", [128, 64])
            xnfh = es1.enter_context(nc.sbuf_tensor("p1_xnfh", [128, DT_M * TC], BF16))
            xnfl = es1.enter_context(nc.sbuf_tensor("p1_xnfl", [128, DT_M * TC], BF16))
            xi = sb1("p1_xi", [128, DT_I * XS])
            cvs = [sb1(f"p1_cv{i}", [128, TC]) for i in range(2)]
            stg = [sb1(f"p1_stg{i}", [128, TC]) for i in range(2)]

            @blk.sync
            def _(sy):
                p.wait(sy, "cv", p.c["cv"])
                p.wait(sy, "cs", p.c["cs"])
                p.inc(sy.dma_start(out=x1[:3, 0:DM], in_=xh_d[:]), "qs", 16)
            qs_halo = p.c["qs"]

            cv_wb = p.c["cv"]

            @blk.sync
            def _(sy):
                p.wait(sy, "qs", p.c["qs"])
                for k in range(DT_M):
                    if k >= 2:
                        p.wait(sy, "cv", cv_wb + 2 * (k - 1))
                    p.inc(sy.dma_start(out=xi[:, (k % 2) * 4096:(k % 2) * 4096 + 4096],
                                       in_=winT_d[k * 128:(k + 1) * 128, :]),
                          f"qh{k % 2}", 16)

            @blk.vector
            def _(v):
                for k in range(DT_M):
                    tmp = xi[:, (k % 2) * 4096:(k % 2) * 4096 + 4096]
                    p.wait(v, f"qh{k % 2}", 16 * (k // 2 + 1))
                    li = v.tensor_copy(out=winh[k][:], in_=tmp)
                    p.inc(li, "cv")
                    p.edge(v, "cv")
                    li = v.tensor_tensor(out=winl[k][:], in0=tmp, in1=winh[k][:],
                                         op=ALU.subtract)
                    p.inc(li, "cv")
            qs_win = p.c["qs"]

            @blk.vector
            def _(v):
                p.wait(v, "qs", qs_halo)
                p.wait(v, "cs")
                v.bn_stats(out=bns[:3, 0:6], in_=x1[:3, 0:512])
                li = v.bn_stats(out=bns[:3, 6:12], in_=x1[:3, 512:1024])
                p.inc(li, "cv")
                p.edge(v, "cv")
                li = v.bn_aggr(out=bns[:3, 12:14], in_=bns[:3, 0:12])
                p.inc(li, "cv")
            cv_halo_stats = p.c["cv"]

            @blk.scalar
            def _(s):
                p.wait(s, "cv", cv_halo_stats)
                li = s.activation(out=bns[:3, 14:15], in_=bns[:3, 13:14], func=AF.Ln,
                                  bias=epsb[:3, 0:1])
                p.inc(li, "cs")
                p.edge(s, "cs")
                li = s.activation(out=bns[:3, 14:15], in_=bns[:3, 14:15], func=AF.Exp, scale=-0.5)
                p.inc(li, "cs")
            cs_halo_rstd = p.c["cs"]

            @blk.vector
            def _(v):
                p.wait(v, "cs", cs_halo_rstd)
                li = v.tensor_scalar(out=x1[:3, 0:DM], in0=x1[:3, 0:DM],
                                     scalar1=bns[:3, 12:13], scalar2=bns[:3, 14:15],
                                     op0=ALU.subtract, op1=ALU.mult)
                p.inc(li, "cv")
            cv_halo_xn = p.c["cv"]

            @blk.tensor
            def _(t):
                p.wait(t, "cv", cv_halo_xn)
                p.wait(t, "qs", qs_win)
                li = None
                for c in range(DT_M):
                    li = t.transpose(pb[0][:, c * 3:c * 3 + 3], x1[:3, c * 128:(c + 1) * 128],
                                     iden[:3, 0:3])
                p.inc(li, "ct")
            ct_halo_tr = p.c["ct"]

            @blk.vector
            def _(v):
                p.wait(v, "ct", ct_halo_tr)
                p.inc(v.tensor_copy(out=xnfh[:, 0:DT_M * 3], in_=pb[0][:, 0:DT_M * 3]),
                      "cv")
                p.edge(v, "cv")
                p.inc(v.tensor_tensor(out=xnfl[:, 0:DT_M * 3], in0=pb[0][:, 0:DT_M * 3],
                                      in1=xnfh[:, 0:DT_M * 3], op=ALU.subtract), "cv")
            cv_halo_evac = p.c["cv"]

            cs_hb = p.c["cs"]

            @blk.tensor
            def _(t):
                p.wait(t, "cv", cv_halo_evac)
                for o in range(DT_I):
                    if o >= 2:
                        p.wait(t, "cs", cs_hb + o - 1)
                    dst = pb[1] if o % 2 == 0 else pb[2]
                    li = None
                    for k in range(DT_M):
                        for wm, xm, tm in ((winh, xnfh, 0), (winh, xnfl, 1),
                                           (winl, xnfh, 2)):
                            li = t.matmul(dst[:, 0:3], wm[k][:, o * 128:(o + 1) * 128],
                                          xm[:, k * 3:k * 3 + 3],
                                          start=(k == 0 and tm == 0),
                                          stop=(k == DT_M - 1 and tm == 2))
                    p.inc(li, "ct")
            ct_halo_mm = p.c["ct"]

            @blk.scalar
            def _(s):
                for o in range(DT_I):
                    src = pb[1] if o % 2 == 0 else pb[2]
                    p.wait(s, "ct", ct_halo_mm - DT_I + o + 1)
                    li = s.activation(out=xihalo[:, o * 3:(o + 1) * 3], in_=src[:, 0:3],
                                      func=AF.Identity, bias=cbixi[:, o:o + 1])
                    p.inc(li, "cs")
            cs_halo = p.c["cs"]

            @blk.vector
            def _(v):
                p.wait(v, "cs", cs_halo)
                li = v.tensor_scalar(out=xihalo[:], in0=xihalo[:], scalar1=hmask[:, 0:1],
                                     scalar2=None, op0=ALU.mult)
                p.inc(li, "cv")
                p.edge(v, "cv")
                li = None
                for t in range(DT_I):
                    li = v.tensor_copy(out=xi[:, t * XS:t * XS + 3],
                                       in_=xihalo[:, t * 3:(t + 1) * 3])
                p.inc(li, "cv")

            ct_prev_tr_end = p.c["ct"]
            for j in range(NCH):
                qs_x = p.c["qs"] + 16 * NTT
                cvb = p.c["cv"]
                ctb = p.c["ct"]
                csb = p.c["cs"]
                qt_b = {0: p.c["qt0"], 1: p.c["qt1"]}

                @blk.sync
                def _(sy, j=j):
                    p.wait(sy, "ct", ct_prev_tr_end)
                    for tt in range(NTT):
                        p.inc(sy.dma_start(out=x1[:, tt * DM:(tt + 1) * DM],
                                           in_=x_d[j * TC + tt * 128: j * TC + (tt + 1) * 128, :]),
                              "qs", 16)

                @blk.vector
                def _(v, j=j, csb=csb):
                    p.wait(v, "qs", qs_x)
                    li = None
                    for tt in range(NTT):
                        v.bn_stats(out=bns[:, tt * 12:tt * 12 + 6],
                                   in_=x1[:, tt * DM:tt * DM + 512])
                        li = v.bn_stats(out=bns[:, tt * 12 + 6:tt * 12 + 12],
                                        in_=x1[:, tt * DM + 512:tt * DM + 1024])
                    p.inc(li, "cv")
                    p.edge(v, "cv")
                    li = None
                    for tt in range(NTT):
                        g = j * NTT + tt
                        li = v.bn_aggr(out=mv_all[:, 2 * g:2 * g + 2],
                                       in_=bns[:, tt * 12:tt * 12 + 12])
                    p.inc(li, "cv")
                    p.wait(v, "cs", csb + 2)
                    li = None
                    for tt in range(NTT):
                        g = j * NTT + tt
                        li = v.tensor_scalar(out=x1[:, tt * DM:(tt + 1) * DM],
                                             in0=x1[:, tt * DM:(tt + 1) * DM],
                                             scalar1=mv_all[:, 2 * g:2 * g + 1],
                                             scalar2=rstd_all[:, g:g + 1],
                                             op0=ALU.subtract, op1=ALU.mult)
                    p.inc(li, "cv")

                @blk.scalar
                def _(s, j=j, cvb=cvb):
                    p.wait(s, "cv", cvb + 2)
                    li = None
                    for tt in range(NTT):
                        g = j * NTT + tt
                        li = s.activation(out=rstd_all[:, g:g + 1],
                                          in_=mv_all[:, 2 * g + 1:2 * g + 2],
                                          func=AF.Ln, bias=epsb[:, 0:1])
                    p.inc(li, "cs")
                    p.edge(s, "cs")
                    li = None
                    for tt in range(NTT):
                        g = j * NTT + tt
                        li = s.activation(out=rstd_all[:, g:g + 1],
                                          in_=rstd_all[:, g:g + 1],
                                          func=AF.Exp, scale=-0.5)
                    p.inc(li, "cs")

                @blk.tensor
                def _(t, j=j, cvb=cvb, csb=csb):
                    p.wait(t, "cv", cvb + 3)
                    for grp in range(DT_M):
                        if grp >= 2:
                            p.wait(t, "cv", cvb + 5 + 2 * (grp - 2))
                        bank = pb[grp % 2]
                        li = None
                        for i in range(4):
                            idx = grp * 4 + i
                            tt, c = idx // DT_M, idx % DT_M
                            li = t.transpose(bank[:, i * 128:(i + 1) * 128],
                                             x1[:, tt * DM + c * 128: tt * DM + (c + 1) * 128],
                                             iden[:])
                        p.inc(li, "ct")
                    p.wait(t, "cv", cvb + 19)
                    for o in range(2 * DI // 128):
                        if o >= 3:
                            p.wait(t, "cs", csb + 2 + o - 2)
                        bank = pb[2 + o % 3]
                        li = None
                        for k in range(DT_M):
                            for wm, xm, tm in ((winh, xnfh, 0), (winh, xnfl, 1),
                                               (winl, xnfh, 2)):
                                li = t.matmul(bank[:], wm[k][:, o * 128:(o + 1) * 128],
                                              xm[:, k * TC:(k + 1) * TC],
                                              start=(k == 0 and tm == 0),
                                              stop=(k == DT_M - 1 and tm == 2))
                        p.inc(li, "ct")
                ct_prev_tr_end = ctb + 8

                @blk.vector
                def _(v, j=j, ctb=ctb):
                    for grp in range(DT_M):
                        p.wait(v, "ct", ctb + grp + 1)
                        bank = pb[grp % 2]
                        li = None
                        sls = []
                        for i in range(4):
                            idx = grp * 4 + i
                            tt, c = idx // DT_M, idx % DT_M
                            sls.append((slice(c * TC + tt * 128, c * TC + (tt + 1) * 128), i))
                        for sl, i in sls:
                            li = v.tensor_copy(out=xnfh[:, sl],
                                               in_=bank[:, i * 128:(i + 1) * 128])
                        p.inc(li, "cv")
                        p.edge(v, "cv")
                        li = None
                        for sl, i in sls:
                            li = v.tensor_tensor(out=xnfl[:, sl],
                                                 in0=bank[:, i * 128:(i + 1) * 128],
                                                 in1=xnfh[:, sl], op=ALU.subtract)
                        p.inc(li, "cv")

                qt_store_count = {0: 0, 1: 0}
                silu_qt_wait = {}
                res_qt_wait = {}
                for oo in range(DT_I):
                    s_ = oo % 2
                    res_qt_wait[oo] = qt_b[s_] + 16 * qt_store_count[s_]
                    qt_store_count[s_] += 1
                for t_i in range(DT_I):
                    s_ = t_i % 2
                    silu_qt_wait[t_i] = qt_b[s_] + 16 * qt_store_count[s_]
                    qt_store_count[s_] += 1

                @blk.scalar
                def _(s, j=j, ctb=ctb, cvb=cvb):
                    for o in range(2 * DI // 128):
                        p.wait(s, "ct", ctb + 8 + o + 1)
                        bank = pb[2 + o % 3]
                        if o < DI // 128:
                            li = s.activation(out=xi[:, o * XS + 3: o * XS + 3 + TC], in_=bank[:],
                                              func=AF.Identity, bias=cbixi[:, o:o + 1])
                        else:
                            oo = o - DI // 128
                            p.wait(s, f"qt{oo % 2}", res_qt_wait[oo])
                            li = s.activation(out=stg[oo % 2][:], in_=bank[:],
                                              func=AF.Silu, bias=cbires[:, oo:oo + 1])
                        p.inc(li, "cs")
                    for t_i in range(DT_I):
                        cg = t_i // 2
                        p.wait(s, "cv", cvb + 19 + 5 * cg + 4)
                        p.wait(s, f"qt{t_i % 2}", silu_qt_wait[t_i])
                        li = s.activation(out=stg[t_i % 2][:], in_=cvs[t_i % 2][:],
                                          func=AF.Silu, bias=cbconv[:, t_i:t_i + 1])
                        p.inc(li, "cs")

                @blk.vector
                def _(v, j=j, csb=csb):
                    for cg in range(DT_I // 2):
                        ts_grp = (2 * cg, 2 * cg + 1)
                        p.wait(v, "cs", csb + 2 + 2 * cg + 2)
                        if cg >= 1:
                            p.wait(v, "cs", csb + 2 + 32 + 2 * cg)
                        li = None
                        for t_i in ts_grp:
                            base = t_i * XS
                            li = v.tensor_scalar(out=cvs[t_i % 2][:], in0=xi[:, base:base + TC],
                                                 scalar1=cwconv[:, t_i * 4:t_i * 4 + 1],
                                                 scalar2=None, op0=ALU.mult)
                        p.inc(li, "cv")
                        for k in (1, 2, 3):
                            p.edge(v, "cv")
                            li = None
                            for t_i in ts_grp:
                                base = t_i * XS
                                li = v.scalar_tensor_tensor(out=cvs[t_i % 2][:],
                                                            in0=xi[:, base + k:base + k + TC],
                                                            scalar=cwconv[:, t_i * 4 + k:t_i * 4 + k + 1],
                                                            in1=cvs[t_i % 2][:],
                                                            op0=ALU.mult, op1=ALU.add)
                            p.inc(li, "cv")
                        p.edge(v, "cv")
                        li = None
                        for t_i in ts_grp:
                            base = t_i * XS
                            li = v.tensor_copy(out=xi[:, base:base + 3],
                                               in_=xi[:, base + TC:base + TC + 3])
                        p.inc(li, "cv")

                @blk.sync
                def _(sy, j=j, csb=csb):
                    for oo in range(DT_I):
                        p.wait(sy, "cs", csb + 2 + 16 + oo + 1)
                        p.inc(sy.dma_start(out=sres_d[:, oo * T + j * TC: oo * T + j * TC + TC],
                                           in_=stg[oo % 2][:]), f"qt{oo % 2}", 16)
                    for t_i in range(DT_I):
                        p.wait(sy, "cs", csb + 2 + 32 + t_i + 1)
                        p.inc(sy.dma_start(out=xi2_d[:, t_i * T + j * TC: t_i * T + j * TC + TC],
                                           in_=stg[t_i % 2][:]), f"qt{t_i % 2}", 16)

        TP = 256
        NC2 = T // TP
        with contextlib.ExitStack() as es2:
            sb2 = lambda name, shape: es2.enter_context(nc.sbuf_tensor(name, shape, F32))
            wxb = [sb2(f"wx{i}", [128, 12 * 128]) for i in range(2)]
            bcb = [sb2(f"p2_bcb{i}", [128, 1, 2 * TP]) for i in range(2)]
            xi2 = sb2("p2_xi2", [128, DT_I * TP])
            sdel = [sb2(f"p2_del{i}", [128, DT_I * TP]) for i in range(2)]
            sdtx = [sb2(f"p2_dtx{i}", [128, DT_I * TP]) for i in range(2)]
            sy_ = sb2("p2_y", [128, DT_I * TP])
            srs = sb2("p2_srs", [128, DT_I * TP])
            h_all = sb2("p2_h", [128, 8 * TP])
            da = sb2("p2_da", [128, DT_I * TP])
            sp12 = sb2("p2_sp", [128, 17 * TP])
            dbx = [sb2(f"p2_dbx{i}", [128, 4 * TP]) for i in range(2)]
            zt = [sb2(f"p2_z{i}", [128, 4 * TP]) for i in range(2)]
            bc = sb2("p2_bc", [32, TP])

            NO = (DI + 2 * NS + 127) // 128
            GRPS = [list(range(0, 12)), list(range(12, 17))]

            def da_off(jj, n):
                if jj == NC2 - 1:
                    return 16 * n
                return 16 * n + (0 if n < 12 else 3 if n < 14 else 6)

            def emit_dA_part(jj, ns, csA, cvb_jj, qbs_jj, prevcv):
                @blk.scalar
                def _(s, jj=jj, ns=ns, csA=csA, cvb_jj=cvb_jj, qbs_jj=qbs_jj,
                      prevcv=prevcv):
                    cvs0 = cvb_jj + 5
                    if ns[0] == 0:
                        p.wait(s, "cv", cvb_jj + 4)
                        p.wait(s, "qbs", qbs_jj)
                    for n in ns:
                        if n >= 2:
                            p.wait(s, "cv", cvs0 + 16 * (n - 1))
                        elif prevcv is not None:
                            p.wait(s, "cv", prevcv + 5 + 16 * (15 + n))
                        p.inc(s.dma_start(
                            out=bcb[n % 2][:, 0, 0:TP],
                            in_=bcd_d[n, jj * TP:(jj + 1) * TP].partition_broadcast(128)),
                            f"qb{n % 2}", 16)
                        p.inc(s.dma_start(
                            out=bcb[n % 2][:, 0, TP:2 * TP],
                            in_=bcd_d[NS + n, jj * TP:(jj + 1) * TP].partition_broadcast(128)),
                            f"qb{n % 2}", 16)
                        for t_i in range(DT_I):
                            if n > 0:
                                p.wait(s, "cs", csA + da_off(jj, n - 1) + t_i + 1)
                                p.wait(s, "cv", cvs0 + 16 * (n - 1) + 4 * (t_i // 4) + 2)
                            li = s.activation(out=da[:, t_i * TP:(t_i + 1) * TP],
                                              in_=sdel[jj % 2][:, t_i * TP:(t_i + 1) * TP],
                                              func=AF.Exp,
                                              scale=ca[:, t_i * NS + n:t_i * NS + n + 1])
                            p.inc(li, "cs")

            prev = None
            prev_qbs = 0
            prevcv_prev = None
            for j in range(NC2):
                ctb = p.c["ct"]
                csb = p.c["cs"]
                cvb = p.c["cv"]

                @blk.sync
                def _(sy, j=j, prev=prev):
                    if j == 0:
                        p.wait(sy, "qs", p.c["qs"])
                        p.wait(sy, "qt0", p.c["qt0"])
                        p.wait(sy, "qt1", p.c["qt1"])
                        p.wait(sy, "cv", p.c["cv"])
                        p.wait(sy, "cs", p.c["cs"])
                        p.wait(sy, "ct", p.c["ct"])
                    else:
                        (pq, pct, pcs, pcv) = prev
                        p.wait(sy, "cv", pcv + 5)
                        p.wait(sy, "ct", pct + 32)
                    for t_i in range(DT_I):
                        p.inc(sy.dma_start(out=xi2[:, t_i * TP:(t_i + 1) * TP],
                                           in_=xi2_d[:, t_i * T + j * TP: t_i * T + j * TP + TP]),
                              "qs", 16)
                qs_xi2 = p.c["qs"]


                qwb = (p.c["qw0"], p.c["qw1"])

                @blk.sync
                def _(sy, j=j, ctb=ctb):
                    for g in range(2):
                        c0 = 0 if g == 0 else 12 * 128
                        cw = 12 * 128 if g == 0 else (DI + 2 * NS - 12 * 128)
                        for k in range(DT_I):
                            p.wait(sy, "ct", ctb + 16 * g + k - 1)
                            p.inc(sy.dma_start(out=wxb[k % 2][:, 0:cw],
                                               in_=wxT_d[k * 128:(k + 1) * 128, c0:c0 + cw]),
                                  f"qw{k % 2}", 16)

                @blk.tensor
                def _(t, j=j, prev=prev, ctb=ctb, csb=csb, cvb=cvb, qwb=qwb,
                      qs_xi2=qs_xi2):
                    for g, grp in enumerate(GRPS):
                        if g == 0:
                            p.wait(t, "qs", qs_xi2)
                            if prev is not None:
                                p.wait(t, "cs", prev[2] + (228 if j >= 2 else 4))
                        else:
                            p.wait(t, "cs", csb + (193 if j >= 1 else 1))
                        for k in range(DT_I):
                            p.wait(t, f"qw{k % 2}",
                                   qwb[k % 2] + 16 * (8 * g + k // 2 + 1))
                            li = None
                            for bi, o in enumerate(grp):
                                w = 32 if o == NO - 1 else 128
                                lc = bi * 128
                                bank = pb[bi // 2]
                                half = (bi % 2) * TP
                                dst = (bank[:, half:half + TP] if w == 128
                                       else bank[:32, half:half + TP])
                                st_ = (k == 0 and bi % 2 == 0)
                                sp_ = (k == DT_I - 1 and (bi % 2 == 1
                                                          or bi == len(grp) - 1))
                                li = t.matmul(dst, wxb[k % 2][:, lc:lc + w],
                                              xi2[:, k * TP:(k + 1) * TP],
                                              start=st_, stop=sp_)
                            p.inc(li, "ct")

                @blk.sync
                def _(sy, j=j, csb=csb):
                    p.wait(sy, "cs", csb + (230 if j >= 1 else 6))
                    p.inc(sy.dma_start(out=bcd_d[:, j * TP:(j + 1) * TP], in_=bc[:, 0:TP]),
                          "qbs", 16)
                qbs_j = p.c["qbs"]
                @blk.sync
                def _(sy, j=j, prev=prev):
                    if j >= 1:
                        p.wait(sy, "cv", prev[3] + 262)
                    else:
                        p.wait(sy, "qt0", p.c["qt0"])
                        p.wait(sy, "qt1", p.c["qt1"])
                    for t_i in range(DT_I):
                        p.inc(sy.dma_start(out=srs[:, t_i * TP:(t_i + 1) * TP],
                                           in_=sres_d[:, t_i * T + j * TP: t_i * T + j * TP + TP]),
                              "qr", 16)
                qt0_sres = p.c["qr"]

                if prev is not None:
                    @blk.sync
                    def _(sy, j=j, prev=prev):
                        p.wait(sy, "cv", prev[3] + 262)
                        jp = j - 1
                        for t_i in range(DT_I):
                            p.inc(sy.dma_start(
                                out=v_d[:, t_i * T + jp * TP: t_i * T + jp * TP + TP],
                                in_=sy_[:, t_i * TP:(t_i + 1) * TP]), "qv", 16)

                daB = csb
                if prev is not None:
                    emit_dA_part(j - 1, list(range(0, 12)), daB, prev[3],
                                 prev_qbs, prevcv_prev)

                def emit_sp(g, j=j, prev=prev, ctb=ctb, csb=csb, cvb=cvb,
                            pqbs=prev_qbs):
                    @blk.scalar
                    def _(s, g=g):
                        grp = GRPS[g]
                        if g == 0:
                            p.wait(s, "cs", csb)
                            if prev is not None:
                                p.wait(s, "cv", prev[3] + 3)

                        p.wait(s, "ct", ctb + 16 * (g + 1))
                        dts = [(bi + 12 * g, o) for bi, o in enumerate(grp) if o < DI // 128]
                        li = None
                        for sl, o in dts:
                            lb = sl - 12 * g
                            bank = pb[lb // 2]
                            half = (lb % 2) * TP
                            li = s.activation(out=sp12[:, sl * TP:(sl + 1) * TP],
                                              in_=bank[:, half:half + TP],
                                              func=AF.Abs, bias=cbxd[:, o:o + 1])
                        for sl, o in dts:
                            lb = sl - 12 * g
                            bank = pb[lb // 2]
                            half = (lb % 2) * TP
                            li = s.activation(out=sdel[j % 2][:, o * TP:(o + 1) * TP],
                                              in_=bank[:, half:half + TP],
                                              func=AF.Relu, bias=cbxd[:, o:o + 1])
                        p.inc(li, "cs")
                        p.edge(s, "cs")
                        li = None
                        for sl, o in dts:
                            li = s.activation(out=sp12[:, sl * TP:(sl + 1) * TP],
                                              in_=sp12[:, sl * TP:(sl + 1) * TP],
                                              func=AF.Exp, scale=-1.0)
                        p.inc(li, "cs")
                        p.edge(s, "cs")
                        li = None
                        for sl, o in dts:
                            li = s.activation(out=sp12[:, sl * TP:(sl + 1) * TP],
                                              in_=sp12[:, sl * TP:(sl + 1) * TP],
                                              func=AF.Ln, bias=1.0)
                        if g == 1:
                            p.wait(s, "qbs", pqbs)
                            li = s.activation(out=bc[:, 0:TP], in_=pb[2][:32, 0:TP],
                                              func=AF.Identity, bias=cbxbc[:, 0:1])
                        p.inc(li, "cs")

                emit_sp(0)
                if prev is not None:
                    emit_dA_part(j - 1, [12, 13], daB, prev[3], prev_qbs,
                                 prevcv_prev)
                emit_sp(1)
                if prev is not None:
                    emit_dA_part(j - 1, [14, 15], daB, prev[3], prev_qbs,
                                 prevcv_prev)
                qb_j = (p.c["qb0"], p.c["qb1"])

                @blk.vector
                def _(v, j=j, prev=prev, ctb=ctb, csb=csb, cvb=cvb, qb_j=qb_j,
                      qt0_sres=qt0_sres, qt1_vst=p.c["qv"]):
                    for g, grp in enumerate(GRPS):
                        spb = csb + (192 if j >= 1 else 0)
                        p.wait(v, "cs", spb + (3 if g == 0 else
                                               (38 if j >= 1 else 6)))
                        dts = [(bi + 12 * g, o) for bi, o in enumerate(grp) if o < DI // 128]
                        li = None
                        for sl, o in dts:
                            li = v.tensor_tensor(out=sdel[j % 2][:, o * TP:(o + 1) * TP],
                                                 in0=sdel[j % 2][:, o * TP:(o + 1) * TP],
                                                 in1=sp12[:, sl * TP:(sl + 1) * TP],
                                                 op=ALU.add)
                        p.inc(li, "cv")
                        p.edge(v, "cv")
                        li = None
                        for sl, o in dts:
                            li = v.tensor_tensor(out=sdtx[j % 2][:, o * TP:(o + 1) * TP],
                                                 in0=sdel[j % 2][:, o * TP:(o + 1) * TP],
                                                 in1=xi2[:, o * TP:(o + 1) * TP],
                                                 op=ALU.mult)
                        p.inc(li, "cv")
                    p.wait(v, "qv", qt1_vst)
                    p.edge(v, "cv")
                    li = None
                    for t_i in range(DT_I):
                        li = v.tensor_scalar(out=sy_[:, t_i * TP:(t_i + 1) * TP],
                                             in0=xi2[:, t_i * TP:(t_i + 1) * TP],
                                             scalar1=cdv[:, t_i:t_i + 1],
                                             scalar2=None, op0=ALU.mult)
                    p.inc(li, "cv")
                    daB2 = csb + (6 if j == 0 else 262)
                    for n in range(NS):
                        p.wait(v, f"qb{n % 2}", qb_j[n % 2] + 32 * (n // 2 + 1))
                        bap_b = bcb[n % 2][:, 0:1, 0:TP].broadcast_to([128, 4, TP])
                        bap_c = bcb[n % 2][:, 0:1, TP:2 * TP].broadcast_to([128, 4, TP])
                        for tp_ in range(0, DT_I, 4):
                            qi = tp_ // 4
                            p.wait(v, "cs", daB2 + da_off(j, n) + tp_ + 4)
                            li = v.tensor_tensor(out=dbx[qi % 2][:],
                                                 in0=sdtx[j % 2][:, tp_ * TP:(tp_ + 4) * TP],
                                                 in1=bap_b, op=ALU.mult)
                            p.inc(li, "cv")
                            p.edge(v, "cv")
                            li = None
                            for t_i in range(tp_, tp_ + 4):
                                li = v.tensor_tensor_scan(
                                    out=h_all[:, (t_i % 8) * TP:(t_i % 8 + 1) * TP],
                                    data0=da[:, t_i * TP:(t_i + 1) * TP],
                                    data1=dbx[qi % 2][:, (t_i - tp_) * TP:(t_i - tp_ + 1) * TP],
                                    initial=hstate[:, t_i * NS + n:t_i * NS + n + 1],
                                    op0=ALU.mult, op1=ALU.add)
                            p.inc(li, "cv")
                            p.edge(v, "cv")
                            li = v.tensor_tensor(out=zt[qi % 2][:],
                                                 in0=h_all[:, (tp_ % 8) * TP:(tp_ % 8 + 4) * TP],
                                                 in1=bap_c, op=ALU.mult)
                            p.inc(li, "cv")
                            p.edge(v, "cv")
                            li = v.tensor_tensor(out=sy_[:, tp_ * TP:(tp_ + 4) * TP],
                                                 in0=sy_[:, tp_ * TP:(tp_ + 4) * TP],
                                                 in1=zt[qi % 2][:], op=ALU.add)
                            if tp_ % 8 == 4:
                                base = tp_ - 4
                                li = v.tensor_copy(
                                    out=hstate[:, base * NS + n:(tp_ + 4) * NS:NS],
                                    in_=h_all[:, TP - 1::TP])
                            p.inc(li, "cv")
                            p.edge(v, "cv")
                    p.wait(v, "qr", qt0_sres)
                    p.edge(v, "cv")
                    li = None
                    for t_i in range(DT_I):
                        li = v.tensor_tensor(out=sy_[:, t_i * TP:(t_i + 1) * TP],
                                             in0=sy_[:, t_i * TP:(t_i + 1) * TP],
                                             in1=srs[:, t_i * TP:(t_i + 1) * TP], op=ALU.mult)
                    p.inc(li, "cv")
                assert p.c["cv"] == cvb + 262, (p.c["cv"], cvb)


                prev_qbs = qbs_j
                prevcv_prev = prev[3] if prev is not None else None
                prev = (qs_xi2, ctb, csb, cvb)
                if j == NC2 - 1:
                    @blk.sync
                    def _(sy, j=j, cvb=cvb):
                        p.wait(sy, "cv", cvb + 262)
                        for t_i in range(DT_I):
                            p.inc(sy.dma_start(
                                out=v_d[:, t_i * T + j * TP: t_i * T + j * TP + TP],
                                in_=sy_[:, t_i * TP:(t_i + 1) * TP]), "qv", 16)
                    emit_dA_part(j, list(range(NS)), csb + (6 if j == 0 else 262),
                                 cvb, qbs_j, prevcv_prev)


        with contextlib.ExitStack() as es3:
            sb3 = lambda name, shape: es3.enter_context(nc.sbuf_tensor(name, shape, F32))
            wob = [sb3(f"wo{i}", [128, DM]) for i in range(2)]
            bf3 = lambda name, shape: es3.enter_context(nc.sbuf_tensor(name, shape, BF16))
            wobh = [bf3(f"woh{i}", [128, DM]) for i in range(2)]
            wobl = [bf3(f"wol{i}", [128, DM]) for i in range(2)]
            wt = [sb3(f"wt{k}", [128, DM]) for k in range(DT_M)]
            wtb = [bf3(f"wtb{k}", [128, DM]) for k in range(DT_M)]
            vv = sb3("p3_v", [128, DT_I * TC])
            vvh = bf3("p3_vh", [128, DT_I * TC])
            vvl = bf3("p3_vl", [128, DT_I * TC])
            x3 = sb3("p3_x", [128, NTT * DM])
            xnf3h = bf3("p3_xnfh", [128, DT_M * TC])
            xnf3l = bf3("p3_xnfl", [128, DT_M * TC])
            ss = sb3("p3_s", [128, DT_M * TC])
            sg = sb3("p3_g", [128, DT_M * TC])
            thr = sb3("p3_thr", [128, 2])
            red8 = sb3("p3_red", [128, DT_M + 2])

            @blk.sync
            def _(sy):
                p.wait(sy, "qs", p.c["qs"])
                p.wait(sy, "qt0", p.c["qt0"])
                p.wait(sy, "qt1", p.c["qt1"])
                p.wait(sy, "qv", p.c["qv"])
                p.wait(sy, "qr", p.c["qr"])
                p.wait(sy, "qb0", p.c["qb0"])
                p.wait(sy, "qb1", p.c["qb1"])
                p.wait(sy, "cv", p.c["cv"])
                p.wait(sy, "cs", p.c["cs"])
                p.wait(sy, "ct", p.c["ct"])
                for k in range(DT_M):
                    p.inc(sy.dma_start(out=wt[k][:], in_=wfT_d[k * 128:(k + 1) * 128, :]),
                          "qs", 16)
            qs_wf = p.c["qs"]

            @blk.vector
            def _(v):
                p.wait(v, "qs", qs_wf)
                li = None
                for k in range(DT_M):
                    li = v.tensor_reduce(out=red8[:, k:k + 1], in_=wt[k][:], axis=AX.X,
                                         op=ALU.add, apply_absolute_value=True)
                p.inc(li, "cv")
                p.edge(v, "cv")
                li = v.tensor_reduce(out=red8[:, DT_M:DT_M + 1], in_=red8[:, 0:DT_M], axis=AX.X,
                                     op=ALU.add)
                p.inc(li, "cv")
            cv_red = p.c["cv"]

            @blk.tensor
            def _(t):
                p.wait(t, "cv", cv_red)
                li = t.matmul(pb[0][:1, 0:1], red8[:, DT_M:DT_M + 1], ones_col[:],
                              start=True, stop=True)
                p.inc(li, "ct")
            ct_tot = p.c["ct"]

            @blk.scalar
            def _(s):
                p.wait(s, "ct", ct_tot)
                li = s.activation(out=thr[:1, 0:1], in_=pb[0][:1, 0:1], func=AF.Copy,
                                  scale=0.7 / float(DM * DM))
                p.inc(li, "cs")
            cs_thr = p.c["cs"]

            @blk.tensor
            def _(t):
                p.wait(t, "cs", cs_thr)
                li = t.matmul(pb[1][:, 0:1], ones_row[:1, :], thr[:1, 0:1], start=True, stop=True)
                p.inc(li, "ct")
            ct_thr = p.c["ct"]

            @blk.vector
            def _(v):
                p.wait(v, "ct", ct_thr)
                li = v.tensor_copy(out=thr[:, 0:1], in_=pb[1][:, 0:1])
                p.inc(li, "cv")
                p.edge(v, "cv")
                li = v.tensor_scalar(out=thr[:, 1:2], in0=thr[:, 0:1], scalar1=-1.0,
                                     scalar2=None, op0=ALU.mult)
                p.inc(li, "cv")
                p.edge(v, "cv")
                for kh in range(2):
                    ks = range(kh * 4, kh * 4 + 4)
                    li = None
                    for k in ks:
                        li = v.tensor_scalar(out=sg[:, (k % 4) * DM:(k % 4 + 1) * DM],
                                             in0=wt[k][:], scalar1=thr[:, 0:1],
                                             scalar2=None, op0=ALU.is_gt)
                    for k in ks:
                        li = v.tensor_scalar(out=ss[:, (k % 4) * DM:(k % 4 + 1) * DM],
                                             in0=wt[k][:], scalar1=thr[:, 1:2],
                                             scalar2=None, op0=ALU.is_lt)
                    p.inc(li, "cv")
                    p.edge(v, "cv")
                    li = None
                    for k in ks:
                        li = v.tensor_tensor(out=wt[k][:],
                                             in0=sg[:, (k % 4) * DM:(k % 4 + 1) * DM],
                                             in1=ss[:, (k % 4) * DM:(k % 4 + 1) * DM],
                                             op=ALU.subtract)
                    p.inc(li, "cv")
                    p.edge(v, "cv")
                    li = None
                    for k in ks:
                        li = v.tensor_copy(out=wtb[k][:], in_=wt[k][:])
                    p.inc(li, "cv")
                    p.edge(v, "cv")

            for j in range(NCH):
                cvb = p.c["cv"]
                ctb = p.c["ct"]
                csb = p.c["cs"]

                @blk.sync
                def _(sy, j=j):
                    p.wait(sy, "qs", p.c["qs"])
                    if j >= 1:
                        p.wait(sy, "cv", p.c["cv"])
                        p.wait(sy, "ct", p.c["ct"])
                    for t_i in range(DT_I):
                        p.inc(sy.dma_start(out=vv[:, t_i * TC:(t_i + 1) * TC],
                                           in_=v_d[:, t_i * T + j * TC: t_i * T + j * TC + TC]),
                              "qs", 16)
                    for tt in range(NTT):
                        p.inc(sy.dma_start(out=x3[:, tt * DM:(tt + 1) * DM],
                                           in_=x_d[j * TC + tt * 128: j * TC + (tt + 1) * 128, :]),
                              "qs", 16)
                qs_v = p.c["qs"]

                @blk.vector
                def _(v, j=j, qs_v=qs_v):
                    p.wait(v, "qs", qs_v)
                    li = None
                    for tt in range(NTT):
                        g = j * NTT + tt
                        li = v.tensor_scalar(out=x3[:, tt * DM:(tt + 1) * DM],
                                             in0=x3[:, tt * DM:(tt + 1) * DM],
                                             scalar1=mv_all[:, 2 * g:2 * g + 1],
                                             scalar2=rstd_all[:, g:g + 1],
                                             op0=ALU.subtract, op1=ALU.mult)
                    p.inc(li, "cv")

                qw_base3 = (p.c["qw0"], p.c["qw1"])
                czb = p.c["cz"]

                @blk.sync
                def _(sy, j=j, czb=czb):
                    for k in range(DT_I):
                        if k >= 2:
                            p.wait(sy, "cz", czb + 4 + 2 * (k - 2))
                        p.inc(sy.dma_start(out=wob[k % 2][:],
                                           in_=woT_d[k * 128:(k + 1) * 128, :]),
                              f"qw{k % 2}", 16)

                @blk.vector
                def _(v, j=j, qs_v=qs_v, qw_base3=qw_base3):
                    p.wait(v, "qs", qs_v)
                    li = None
                    for q_ in range(4):
                        li = v.tensor_copy(out=vvh[:, q_ * 4 * TC:(q_ + 1) * 4 * TC],
                                           in_=vv[:, q_ * 4 * TC:(q_ + 1) * 4 * TC])
                    p.inc(li, "cz")
                    p.edge(v, "cz")
                    li = None
                    for q_ in range(4):
                        li = v.tensor_tensor(out=vvl[:, q_ * 4 * TC:(q_ + 1) * 4 * TC],
                                             in0=vv[:, q_ * 4 * TC:(q_ + 1) * 4 * TC],
                                             in1=vvh[:, q_ * 4 * TC:(q_ + 1) * 4 * TC],
                                             op=ALU.subtract)
                    p.inc(li, "cz")
                    for k in range(2):
                        p.wait(v, f"qw{k % 2}", qw_base3[k % 2] + 16)
                        p.inc(v.tensor_copy(out=wobh[k % 2][:], in_=wob[k % 2][:]), "cz")
                        p.edge(v, "cz")
                        p.inc(v.tensor_tensor(out=wobl[k % 2][:], in0=wob[k % 2][:],
                                              in1=wobh[k % 2][:], op=ALU.subtract), "cz")

                @blk.tensor
                def _(t, j=j, cvb=cvb, csb=csb, czb=czb):
                    p.wait(t, "cv", cvb + 1)
                    for grp in range(DT_M):
                        if grp >= 2:
                            p.wait(t, "cv", cvb + 2 * grp - 1)
                        bank = pb[grp % 2]
                        li = None
                        for i in range(4):
                            idx = grp * 4 + i
                            tt, c = idx // DT_M, idx % DT_M
                            li = t.transpose(bank[:, i * 128:(i + 1) * 128],
                                             x3[:, tt * DM + c * 128: tt * DM + (c + 1) * 128],
                                             iden[:])
                        p.inc(li, "ct")
                    p.wait(t, "cv", cvb + 17)
                    for k in range(DT_I):
                        p.wait(t, "cz", czb + 4 + 2 * k)
                        li = None
                        for o in range(DT_M):
                            for wm, xm, tm in ((wobh, vvh, 0), (wobh, vvl, 1),
                                               (wobl, vvh, 2)):
                                li = t.matmul(pb[o][:, 0:TC],
                                              wm[k % 2][:, o * 128:(o + 1) * 128],
                                              xm[:, k * TC:(k + 1) * TC],
                                              start=(k == 0 and tm == 0),
                                              stop=(k == DT_I - 1 and tm == 2))
                        p.inc(li, "ct")
                    p.wait(t, "cs", csb + 1)
                    li = None
                    for o in range(DT_M):
                        for k in range(DT_M):
                            for xm, tm in ((xnf3h, 0), (xnf3l, 1)):
                                li = t.matmul(pb[o][:, 0:TC], wtb[k][:, o * 128:(o + 1) * 128],
                                              xm[:, k * TC:(k + 1) * TC],
                                              start=(k == 0 and tm == 0),
                                              stop=(k == DT_M - 1 and tm == 1))
                    p.inc(li, "ct")
                    p.wait(t, "cv", cvb + 18)
                    for grp in range(DT_M):
                        if grp >= 2:
                            p.wait(t, "cv", cvb + 18 + (grp - 1))
                        bank = pb[grp % 2]
                        li = None
                        for i in range(4):
                            idx = grp * 4 + i
                            c, tt = idx // NTT, idx % NTT
                            li = t.transpose(bank[:, i * 128:(i + 1) * 128],
                                             ss[:, c * TC + tt * 128: c * TC + (tt + 1) * 128],
                                             iden[:])
                        p.inc(li, "ct")

                @blk.vector
                def _(v, j=j, ctb=ctb):
                    for grp in range(DT_M):
                        p.wait(v, "ct", ctb + grp + 1)
                        bank = pb[grp % 2]
                        sls = []
                        for i in range(4):
                            idx = grp * 4 + i
                            tt, c = idx // DT_M, idx % DT_M
                            sls.append((slice(c * TC + tt * 128, c * TC + (tt + 1) * 128), i))
                        li = None
                        for sl, i in sls:
                            li = v.tensor_copy(out=xnf3h[:, sl],
                                               in_=bank[:, i * 128:(i + 1) * 128])
                        p.inc(li, "cv")
                        p.edge(v, "cv")
                        li = None
                        for sl, i in sls:
                            li = v.tensor_tensor(out=xnf3l[:, sl],
                                                 in0=bank[:, i * 128:(i + 1) * 128],
                                                 in1=xnf3h[:, sl], op=ALU.subtract)
                        p.inc(li, "cv")

                @blk.vector
                def _(v, j=j, ctb=ctb, czb=czb, qw_base3=qw_base3):
                    for k in range(2, DT_I):
                        p.wait(v, f"qw{k % 2}", qw_base3[k % 2] + 16 * (k // 2 + 1))
                        p.wait(v, "ct", ctb + 8 + (k - 2) + 1)
                        p.inc(v.tensor_copy(out=wobh[k % 2][:], in_=wob[k % 2][:]), "cz")
                        p.edge(v, "cz")
                        p.inc(v.tensor_tensor(out=wobl[k % 2][:], in0=wob[k % 2][:],
                                              in1=wobh[k % 2][:], op=ALU.subtract), "cz")

                @blk.scalar
                def _(s, j=j, ctb=ctb):
                    p.wait(s, "ct", ctb + 8 + 16)
                    li = None
                    for o in range(DT_M):
                        li = s.activation(out=ss[:, o * TC:(o + 1) * TC], in_=pb[o][:, 0:TC],
                                          func=AF.Identity, bias=cbout[:, o:o + 1])
                    p.inc(li, "cs")
                    p.wait(s, "ct", ctb + 8 + 16 + 1)
                    li = None
                    for o in range(DT_M):
                        li = s.activation(out=sg[:, o * TC:(o + 1) * TC], in_=pb[o][:, 0:TC],
                                          func=AF.Silu, bias=cbffn[:, o:o + 1], scale=alph[:, 0:1])
                    p.inc(li, "cs")

                @blk.vector
                def _(v, j=j, csb=csb):
                    p.wait(v, "cs", csb + 2)
                    li = None
                    for o in range(DT_M):
                        li = v.tensor_tensor(out=ss[:, o * TC:(o + 1) * TC],
                                             in0=ss[:, o * TC:(o + 1) * TC],
                                             in1=sg[:, o * TC:(o + 1) * TC], op=ALU.mult)
                    p.inc(li, "cv")

                @blk.sync
                def _(sy, j=j, ctb=ctb):
                    p.wait(sy, "ct", ctb + 8)
                    for tt in range(NTT):
                        p.inc(sy.dma_start(out=x3[:, tt * DM:(tt + 1) * DM],
                                           in_=x_d[j * TC + tt * 128: j * TC + (tt + 1) * 128, :]),
                              "qs", 16)
                qs_x3 = p.c["qs"]

                @blk.vector
                def _(v, j=j, ctb=ctb, qs_x3=qs_x3):
                    p.wait(v, "qs", qs_x3)
                    for grp in range(DT_M):
                        p.wait(v, "ct", ctb + 8 + 16 + 1 + grp + 1)
                        bank = pb[grp % 2]
                        li = None
                        for i in range(4):
                            idx = grp * 4 + i
                            c, tt = idx // NTT, idx % NTT
                            li = v.tensor_tensor(out=x3[:, tt * DM + c * 128: tt * DM + (c + 1) * 128],
                                                 in0=x3[:, tt * DM + c * 128: tt * DM + (c + 1) * 128],
                                                 in1=bank[:, i * 128:(i + 1) * 128], op=ALU.add)
                        p.inc(li, "cv")

                @blk.sync
                def _(sy, j=j, cvb=cvb):
                    p.wait(sy, "cv", cvb + 26)
                    for tt in range(NTT):
                        p.inc(sy.dma_start(out=out_d[j * TC + tt * 128: j * TC + (tt + 1) * 128, :],
                                           in_=x3[:, tt * DM:(tt + 1) * DM]), "qs", 16)

        @blk.gpsimd
        def _(g):
            p.wait(g, "qs", p.c["qs"])
            p.wait(g, "qg", p.c["qg"])
            p.wait(g, "qt0", p.c["qt0"])
            p.wait(g, "qt1", p.c["qt1"])
            p.wait(g, "qb0", p.c["qb0"])
            p.wait(g, "qb1", p.c["qb1"])
            p.wait(g, "qbs", p.c["qbs"])
            p.wait(g, "qr", p.c["qr"])
            p.wait(g, "qv", p.c["qv"])
            p.wait(g, "qh0", p.c["qh0"])
            p.wait(g, "qh1", p.c["qh1"])

    return nc


def kernel(L0_attn_x, L0_attn_W_in, L0_attn_b_in, L0_attn_W_conv, L0_attn_b_conv,
           L0_attn_W_x, L0_attn_b_x, L0_attn_W_out, L0_attn_b_out, L0_attn_A_log,
           L0_attn_D, L0_ffn_weight, L0_ffn_alpha, L0_ffn_bias):
    f32 = np.float32
    c_ = np.ascontiguousarray
    x = c_(np.asarray(L0_attn_x, f32))

    winT = c_(np.asarray(L0_attn_W_in, f32).T)
    wxT = c_(np.asarray(L0_attn_W_x, f32).T)
    woT = c_(np.asarray(L0_attn_W_out, f32).T)
    wfT = c_(np.asarray(L0_ffn_weight, f32).T)
    b_in = np.asarray(L0_attn_b_in, f32)
    b_x = np.asarray(L0_attn_b_x, f32)

    def col128(vec, nt):
        return c_(vec.reshape(nt, 128).T)

    cb_in_xi = col128(b_in[:DI], DT_I)
    cb_in_res = col128(b_in[DI:], DT_I)
    cw_conv = c_(np.asarray(L0_attn_W_conv, f32)[:, 0, :].reshape(DT_I, 128, 4)
                 .transpose(1, 0, 2).reshape(128, DT_I * 4))
    cb_conv = col128(np.asarray(L0_attn_b_conv, f32), DT_I)
    cb_x_d = col128(b_x[:DI], DT_I)
    cb_x_bc = c_(b_x[DI:].reshape(32, 1))
    cb_out = col128(np.asarray(L0_attn_b_out, f32), DT_M)
    c_d = col128(np.asarray(L0_attn_D, f32), DT_I)
    c_a_log = c_(np.asarray(L0_attn_A_log, f32).reshape(DT_I, 128, NS)
                 .transpose(1, 0, 2).reshape(128, DT_I * NS))
    c_alpha = c_(np.broadcast_to(np.asarray(L0_ffn_alpha, f32).reshape(1, 1), (128, 1)))
    cb_ffn = col128(np.asarray(L0_ffn_bias, f32), DT_M)
    iden = np.eye(128, dtype=f32)
    zeros3 = np.zeros((3, DM), f32)
    zmask = np.zeros((128, 1), f32)

    shared = dict(w_in_t=winT, w_x_t=wxT, w_out_t=woT, w_ffn_t=wfT,
                  cb_in_xi=cb_in_xi, cb_in_res=cb_in_res, cw_conv=cw_conv,
                  cb_conv=cb_conv, cb_x_d=cb_x_d, cb_x_bc=cb_x_bc, cb_out=cb_out,
                  c_d=c_d, c_a_log=c_a_log, c_alpha=c_alpha, cb_ffn=cb_ffn, iden=iden)

    in_maps = []
    for c in range(8):
        seq = c % 4
        m = dict(shared)
        m["x"] = c_(x[seq])
        m["x_halo3"] = zeros3
        m["halo_mask"] = zmask
        in_maps.append(m)

    nc = build_nc()
    res = run_bass_kernel_spmd(nc, in_maps, core_ids=list(range(8)))
    out = np.empty((4, 4096, DM), f32)
    for seq in range(4):
        out[seq] = res.results[seq]["out"]
    return out


# version 24
# speedup vs baseline: 1.2109x; 1.0008x over previous
import contextlib
import numpy as np
import concourse.bass as bass
import concourse.mybir as mybir
from concourse.bass_utils import run_bass_kernel_spmd

F32 = mybir.dt.float32
BF16 = mybir.dt.bfloat16
AF = mybir.ActivationFunctionType
ALU = mybir.AluOpType
AX = mybir.AxisListType

DM = 1024
DI = 2048
NS = 16
DT_M = DM // 128
DT_I = DI // 128
EPS = 1e-5

import os as _os
HALF = False
T = int(_os.environ.get("MK_T", "0")) or (2048 if HALF else 4096)
TC = 512
NCH = T // TC
NTT = TC // 128
XS = TC + 3


class P:
    def __init__(self, nc, es):
        self.nc = nc
        self.s = {}
        self.c = {}
        for name in ("qs", "qg", "ct", "cv", "cs", "qw0", "qw1", "qt0", "qt1", "qt2", "qt3", "qb0", "qb1", "qbs", "qr", "qv", "qv2", "qh0", "qh1", "cz"):
            self.s[name] = es.enter_context(nc.semaphore(name))
            self.c[name] = 0

    def inc(self, instr, name, k=1):
        instr.then_inc(self.s[name], k)
        self.c[name] += k
        return self.c[name]

    def wait(self, eng, name, val=None):
        if val is None:
            val = self.c[name]
        if val > 0:
            eng.wait_ge(self.s[name], val)

    def edge(self, eng, name):
        self.wait(eng, name, self.c[name])


def build_nc():
    nc = bass.Bass("TRN2", target_bir_lowering=False)

    x_d = nc.dram_tensor("x", [T, DM], F32, kind="ExternalInput").ap()
    xh_d = nc.dram_tensor("x_halo3", [3, DM], F32, kind="ExternalInput").ap()
    hm_d = nc.dram_tensor("halo_mask", [128, 1], F32, kind="ExternalInput").ap()
    winT_d = nc.dram_tensor("w_in_t", [DM, 2 * DI], F32, kind="ExternalInput").ap()
    wxT_d = nc.dram_tensor("w_x_t", [DI, DI + 2 * NS], F32, kind="ExternalInput").ap()
    woT_d = nc.dram_tensor("w_out_t", [DI, DM], F32, kind="ExternalInput").ap()
    wfT_d = nc.dram_tensor("w_ffn_t", [DM, DM], F32, kind="ExternalInput").ap()
    cbixi_d = nc.dram_tensor("cb_in_xi", [128, DT_I], F32, kind="ExternalInput").ap()
    cbires_d = nc.dram_tensor("cb_in_res", [128, DT_I], F32, kind="ExternalInput").ap()
    cwconv_d = nc.dram_tensor("cw_conv", [128, DT_I * 4], F32, kind="ExternalInput").ap()
    cbconv_d = nc.dram_tensor("cb_conv", [128, DT_I], F32, kind="ExternalInput").ap()
    cbxd_d = nc.dram_tensor("cb_x_d", [128, DT_I], F32, kind="ExternalInput").ap()
    cbxbc_d = nc.dram_tensor("cb_x_bc", [32, 1], F32, kind="ExternalInput").ap()
    cbout_d = nc.dram_tensor("cb_out", [128, DT_M], F32, kind="ExternalInput").ap()
    cd_d = nc.dram_tensor("c_d", [128, DT_I], F32, kind="ExternalInput").ap()
    calog_d = nc.dram_tensor("c_a_log", [128, DT_I * NS], F32, kind="ExternalInput").ap()
    alpha_d = nc.dram_tensor("c_alpha", [128, 1], F32, kind="ExternalInput").ap()
    cbffn_d = nc.dram_tensor("cb_ffn", [128, DT_M], F32, kind="ExternalInput").ap()
    iden_d = nc.dram_tensor("iden", [128, 128], F32, kind="ExternalInput").ap()
    out_d = nc.dram_tensor("out", [T, DM], F32, kind="ExternalOutput").ap()
    xi2_d = nc.dram_tensor("xi2_s", [128, DT_I * T], F32, kind="Internal").ap()
    sres_d = nc.dram_tensor("sres_s", [128, DT_I * T], F32, kind="Internal").ap()
    v_d = nc.dram_tensor("v_s", [128, DT_I * T], F32, kind="Internal").ap()
    bcd_d = nc.dram_tensor("bc_s", [32, T], F32, kind="Internal").ap()

    es = contextlib.ExitStack()
    with es:
        p = P(nc, es)
        sb = lambda name, shape: es.enter_context(nc.sbuf_tensor(name, shape, F32))

        cbixi = sb("cbixi", [128, DT_I])
        cbires = sb("cbires", [128, DT_I])
        cwconv = sb("cwconv", [128, DT_I * 4])
        cbconv = sb("cbconv", [128, DT_I])
        cbxd = sb("cbxd", [128, DT_I])
        cbxbc = sb("cbxbc", [32, 1])
        cbout = sb("cbout", [128, DT_M])
        cdv = sb("cdv", [128, DT_I])
        ca = sb("ca", [128, DT_I * NS])
        alph = sb("alph", [128, 1])
        cbffn = sb("cbffn", [128, DT_M])
        iden = sb("iden_sb", [128, 128])
        hmask = sb("hmask", [128, 1])
        epsb = sb("epsb", [128, 1])
        ones_row = sb("ones_row", [1, 128])
        ones_col = sb("ones_col", [128, 1])
        mv_all = sb("mv_all", [128, 2 * (T // 128)])
        rstd_all = sb("rstd_all", [128, T // 128])
        hstate = sb("hstate", [128, DT_I * NS])
        xihalo = sb("xihalo", [128, DT_I * 3])

        pb = [es.enter_context(nc.psum_tensor(f"pb{i}", [128, 512], F32)) for i in range(8)]

        blk = es.enter_context(nc.Block())

        @blk.gpsimd
        def _(g):
            for dst, src in ((cbixi, cbixi_d), (cbires, cbires_d), (cwconv, cwconv_d),
                             (cbconv, cbconv_d), (cbxd, cbxd_d), (cbxbc, cbxbc_d),
                             (cbout, cbout_d), (cdv, cd_d), (ca, calog_d),
                             (alph, alpha_d), (cbffn, cbffn_d), (iden, iden_d),
                             (hmask, hm_d)):
                p.inc(g.dma_start(out=dst[:], in_=src[:]), "qg", 16)

        @blk.vector
        def _(v):
            p.wait(v, "qg")
            v.memset(ones_row[:], 1.0)
            v.memset(epsb[:], EPS)
            v.memset(ones_col[:], 1.0)
            v.memset(hstate[:], 0.0)
            p.inc(v.memset(xihalo[:], 0.0), "cv")

        @blk.scalar
        def _(s):
            p.wait(s, "qg")
            p.wait(s, "cv")
            li = None
            for t in range(DT_I):
                li = s.activation(out=ca[:, t * NS:(t + 1) * NS],
                                  in_=ca[:, t * NS:(t + 1) * NS], func=AF.Exp)
            p.inc(li, "cs")

        @blk.vector
        def _(v):
            p.wait(v, "cs")
            li = None
            for t in range(DT_I):
                li = v.tensor_scalar(out=ca[:, t * NS:(t + 1) * NS],
                                     in0=ca[:, t * NS:(t + 1) * NS],
                                     scalar1=-1.0, scalar2=None, op0=ALU.mult)
            p.inc(li, "cv")

        with contextlib.ExitStack() as es1:
            sb1 = lambda name, shape: es1.enter_context(nc.sbuf_tensor(name, shape, F32))
            winh = [es1.enter_context(nc.sbuf_tensor(f"winh{k}", [128, 2 * DI], BF16))
                    for k in range(DT_M)]
            winl = [es1.enter_context(nc.sbuf_tensor(f"winl{k}", [128, 2 * DI], BF16))
                    for k in range(DT_M)]
            x1 = sb1("p1_x", [128, NTT * DM])
            bns = sb1("p1_bns", [128, 64])
            xnfh = es1.enter_context(nc.sbuf_tensor("p1_xnfh", [128, DT_M * TC], BF16))
            xnfl = es1.enter_context(nc.sbuf_tensor("p1_xnfl", [128, DT_M * TC], BF16))
            xi = sb1("p1_xi", [128, DT_I * XS])
            stg = [sb1(f"p1_stg{i}", [128, TC]) for i in range(4)]

            @blk.sync
            def _(sy):
                p.wait(sy, "cv", p.c["cv"])
                p.wait(sy, "cs", p.c["cs"])
                p.inc(sy.dma_start(out=x1[:3, 0:DM], in_=xh_d[:]), "qs", 16)
            qs_halo = p.c["qs"]

            cv_wb = p.c["cv"]

            @blk.sync
            def _(sy):
                p.wait(sy, "qs", p.c["qs"])
                for k in range(DT_M):
                    if k >= 2:
                        p.wait(sy, "cv", cv_wb + 2 * (k - 1))
                    p.inc(sy.dma_start(out=xi[:, (k % 2) * 4096:(k % 2) * 4096 + 4096],
                                       in_=winT_d[k * 128:(k + 1) * 128, :]),
                          f"qh{k % 2}", 16)

            @blk.vector
            def _(v):
                for k in range(DT_M):
                    tmp = xi[:, (k % 2) * 4096:(k % 2) * 4096 + 4096]
                    p.wait(v, f"qh{k % 2}", 16 * (k // 2 + 1))
                    li = v.tensor_copy(out=winh[k][:], in_=tmp)
                    p.inc(li, "cv")
                    p.edge(v, "cv")
                    li = v.tensor_tensor(out=winl[k][:], in0=tmp, in1=winh[k][:],
                                         op=ALU.subtract)
                    p.inc(li, "cv")
            qs_win = p.c["qs"]

            @blk.vector
            def _(v):
                p.wait(v, "qs", qs_halo)
                p.wait(v, "cs")
                v.bn_stats(out=bns[:3, 0:6], in_=x1[:3, 0:512])
                li = v.bn_stats(out=bns[:3, 6:12], in_=x1[:3, 512:1024])
                p.inc(li, "cv")
                p.edge(v, "cv")
                li = v.bn_aggr(out=bns[:3, 12:14], in_=bns[:3, 0:12])
                p.inc(li, "cv")
            cv_halo_stats = p.c["cv"]

            @blk.scalar
            def _(s):
                p.wait(s, "cv", cv_halo_stats)
                li = s.activation(out=bns[:3, 14:15], in_=bns[:3, 13:14], func=AF.Ln,
                                  bias=epsb[:3, 0:1])
                p.inc(li, "cs")
                p.edge(s, "cs")
                li = s.activation(out=bns[:3, 14:15], in_=bns[:3, 14:15], func=AF.Exp, scale=-0.5)
                p.inc(li, "cs")
            cs_halo_rstd = p.c["cs"]

            @blk.vector
            def _(v):
                p.wait(v, "cs", cs_halo_rstd)
                li = v.tensor_scalar(out=x1[:3, 0:DM], in0=x1[:3, 0:DM],
                                     scalar1=bns[:3, 12:13], scalar2=bns[:3, 14:15],
                                     op0=ALU.subtract, op1=ALU.mult)
                p.inc(li, "cv")
            cv_halo_xn = p.c["cv"]

            @blk.tensor
            def _(t):
                p.wait(t, "cv", cv_halo_xn)
                p.wait(t, "qs", qs_win)
                li = None
                for c in range(DT_M):
                    li = t.transpose(pb[0][:, c * 3:c * 3 + 3], x1[:3, c * 128:(c + 1) * 128],
                                     iden[:3, 0:3])
                p.inc(li, "ct")
            ct_halo_tr = p.c["ct"]

            @blk.vector
            def _(v):
                p.wait(v, "ct", ct_halo_tr)
                p.inc(v.tensor_copy(out=xnfh[:, 0:DT_M * 3], in_=pb[0][:, 0:DT_M * 3]),
                      "cv")
                p.edge(v, "cv")
                p.inc(v.tensor_tensor(out=xnfl[:, 0:DT_M * 3], in0=pb[0][:, 0:DT_M * 3],
                                      in1=xnfh[:, 0:DT_M * 3], op=ALU.subtract), "cv")
            cv_halo_evac = p.c["cv"]

            cs_hb = p.c["cs"]

            @blk.tensor
            def _(t):
                p.wait(t, "cv", cv_halo_evac)
                for o in range(DT_I):
                    if o >= 2:
                        p.wait(t, "cs", cs_hb + o - 1)
                    dst = pb[1] if o % 2 == 0 else pb[2]
                    li = None
                    for k in range(DT_M):
                        for wm, xm, tm in ((winh, xnfh, 0), (winh, xnfl, 1),
                                           (winl, xnfh, 2)):
                            li = t.matmul(dst[:, 0:3], wm[k][:, o * 128:(o + 1) * 128],
                                          xm[:, k * 3:k * 3 + 3],
                                          start=(k == 0 and tm == 0),
                                          stop=(k == DT_M - 1 and tm == 2))
                    p.inc(li, "ct")
            ct_halo_mm = p.c["ct"]

            @blk.scalar
            def _(s):
                for o in range(DT_I):
                    src = pb[1] if o % 2 == 0 else pb[2]
                    p.wait(s, "ct", ct_halo_mm - DT_I + o + 1)
                    li = s.activation(out=xihalo[:, o * 3:(o + 1) * 3], in_=src[:, 0:3],
                                      func=AF.Identity, bias=cbixi[:, o:o + 1])
                    p.inc(li, "cs")
            cs_halo = p.c["cs"]

            @blk.vector
            def _(v):
                p.wait(v, "cs", cs_halo)
                li = v.tensor_scalar(out=xihalo[:], in0=xihalo[:], scalar1=hmask[:, 0:1],
                                     scalar2=None, op0=ALU.mult)
                p.inc(li, "cv")
                p.edge(v, "cv")
                li = None
                for t in range(DT_I):
                    li = v.tensor_copy(out=xi[:, t * XS:t * XS + 3],
                                       in_=xihalo[:, t * 3:(t + 1) * 3])
                p.inc(li, "cv")

            ct_prev_tr_end = p.c["ct"]
            for j in range(NCH):
                qs_x = p.c["qs"] + 16 * NTT
                cvb = p.c["cv"]
                ctb = p.c["ct"]
                csb = p.c["cs"]
                qt_b = {i: p.c[f"qt{i}"] for i in range(4)}

                @blk.sync
                def _(sy, j=j):
                    p.wait(sy, "ct", ct_prev_tr_end)
                    for tt in range(NTT):
                        p.inc(sy.dma_start(out=x1[:, tt * DM:(tt + 1) * DM],
                                           in_=x_d[j * TC + tt * 128: j * TC + (tt + 1) * 128, :]),
                              "qs", 16)

                @blk.vector
                def _(v, j=j, csb=csb):
                    p.wait(v, "qs", qs_x)
                    li = None
                    for tt in range(NTT):
                        v.bn_stats(out=bns[:, tt * 12:tt * 12 + 6],
                                   in_=x1[:, tt * DM:tt * DM + 512])
                        li = v.bn_stats(out=bns[:, tt * 12 + 6:tt * 12 + 12],
                                        in_=x1[:, tt * DM + 512:tt * DM + 1024])
                    p.inc(li, "cv")
                    p.edge(v, "cv")
                    li = None
                    for tt in range(NTT):
                        g = j * NTT + tt
                        li = v.bn_aggr(out=mv_all[:, 2 * g:2 * g + 2],
                                       in_=bns[:, tt * 12:tt * 12 + 12])
                    p.inc(li, "cv")
                    p.wait(v, "cs", csb + 2)
                    li = None
                    for tt in range(NTT):
                        g = j * NTT + tt
                        li = v.tensor_scalar(out=x1[:, tt * DM:(tt + 1) * DM],
                                             in0=x1[:, tt * DM:(tt + 1) * DM],
                                             scalar1=mv_all[:, 2 * g:2 * g + 1],
                                             scalar2=rstd_all[:, g:g + 1],
                                             op0=ALU.subtract, op1=ALU.mult)
                    p.inc(li, "cv")

                @blk.scalar
                def _(s, j=j, cvb=cvb):
                    p.wait(s, "cv", cvb + 2)
                    li = None
                    for tt in range(NTT):
                        g = j * NTT + tt
                        li = s.activation(out=rstd_all[:, g:g + 1],
                                          in_=mv_all[:, 2 * g + 1:2 * g + 2],
                                          func=AF.Ln, bias=epsb[:, 0:1])
                    p.inc(li, "cs")
                    p.edge(s, "cs")
                    li = None
                    for tt in range(NTT):
                        g = j * NTT + tt
                        li = s.activation(out=rstd_all[:, g:g + 1],
                                          in_=rstd_all[:, g:g + 1],
                                          func=AF.Exp, scale=-0.5)
                    p.inc(li, "cs")

                @blk.tensor
                def _(t, j=j, cvb=cvb, csb=csb):
                    p.wait(t, "cv", cvb + 3)
                    for grp in range(DT_M):
                        if grp >= 2:
                            p.wait(t, "cv", cvb + 5 + 2 * (grp - 2))
                        bank = pb[grp % 2]
                        li = None
                        for i in range(4):
                            idx = grp * 4 + i
                            tt, c = idx // DT_M, idx % DT_M
                            li = t.transpose(bank[:, i * 128:(i + 1) * 128],
                                             x1[:, tt * DM + c * 128: tt * DM + (c + 1) * 128],
                                             iden[:])
                        p.inc(li, "ct")
                    p.wait(t, "cv", cvb + 19)
                    for o in range(2 * DI // 128):
                        if o >= 3:
                            p.wait(t, "cs", csb + 2 + o - 2)
                        bank = pb[2 + o % 3]
                        li = None
                        for k in range(DT_M):
                            for wm, xm, tm in ((winh, xnfh, 0), (winh, xnfl, 1),
                                               (winl, xnfh, 2)):
                                li = t.matmul(bank[:], wm[k][:, o * 128:(o + 1) * 128],
                                              xm[:, k * TC:(k + 1) * TC],
                                              start=(k == 0 and tm == 0),
                                              stop=(k == DT_M - 1 and tm == 2))
                        p.inc(li, "ct")
                ct_prev_tr_end = ctb + 8

                @blk.vector
                def _(v, j=j, ctb=ctb):
                    for grp in range(DT_M):
                        p.wait(v, "ct", ctb + grp + 1)
                        bank = pb[grp % 2]
                        li = None
                        sls = []
                        for i in range(4):
                            idx = grp * 4 + i
                            tt, c = idx // DT_M, idx % DT_M
                            sls.append((slice(c * TC + tt * 128, c * TC + (tt + 1) * 128), i))
                        for sl, i in sls:
                            li = v.tensor_copy(out=xnfh[:, sl],
                                               in_=bank[:, i * 128:(i + 1) * 128])
                        p.inc(li, "cv")
                        p.edge(v, "cv")
                        li = None
                        for sl, i in sls:
                            li = v.tensor_tensor(out=xnfl[:, sl],
                                                 in0=bank[:, i * 128:(i + 1) * 128],
                                                 in1=xnfh[:, sl], op=ALU.subtract)
                        p.inc(li, "cv")

                res_qt_wait = {}
                for oo in range(DT_I):
                    s_ = oo % 4
                    res_qt_wait[oo] = qt_b[s_] + 16 * (oo // 4)


                @blk.scalar
                def _(s, j=j, ctb=ctb, cvb=cvb):
                    for o in range(2 * DI // 128):
                        p.wait(s, "ct", ctb + 8 + o + 1)
                        bank = pb[2 + o % 3]
                        if o < DI // 128:
                            li = s.activation(out=xi[:, o * XS + 3: o * XS + 3 + TC], in_=bank[:],
                                              func=AF.Identity, bias=cbixi[:, o:o + 1])
                        else:
                            oo = o - DI // 128
                            p.wait(s, f"qt{oo % 4}", res_qt_wait[oo])
                            li = s.activation(out=stg[oo % 4][:], in_=bank[:],
                                              func=AF.Silu, bias=cbires[:, oo:oo + 1])
                        p.inc(li, "cs")
                    for t_i in range(DT_I):
                        cg = t_i // 2
                        p.wait(s, "cv", cvb + 19 + 5 * cg + 4)
                        li = s.activation(out=stg[t_i % 4][:], in_=stg[t_i % 4][:],
                                          func=AF.Silu, bias=cbconv[:, t_i:t_i + 1])
                        p.inc(li, "cs")

                @blk.vector
                def _(v, j=j, csb=csb, qt_b=qt_b):
                    for cg in range(DT_I // 2):
                        ts_grp = (2 * cg, 2 * cg + 1)
                        p.wait(v, "cs", csb + 2 + 2 * cg + 2)
                        for t_i in ts_grp:
                            p.wait(v, f"qt{t_i % 4}",
                                   qt_b[t_i % 4] + 16 * (t_i // 4 + 4))
                        li = None
                        for t_i in ts_grp:
                            base = t_i * XS
                            li = v.tensor_scalar(out=stg[t_i % 4][:], in0=xi[:, base:base + TC],
                                                 scalar1=cwconv[:, t_i * 4:t_i * 4 + 1],
                                                 scalar2=None, op0=ALU.mult)
                        p.inc(li, "cv")
                        for k in (1, 2, 3):
                            p.edge(v, "cv")
                            li = None
                            for t_i in ts_grp:
                                base = t_i * XS
                                li = v.scalar_tensor_tensor(out=stg[t_i % 4][:],
                                                            in0=xi[:, base + k:base + k + TC],
                                                            scalar=cwconv[:, t_i * 4 + k:t_i * 4 + k + 1],
                                                            in1=stg[t_i % 4][:],
                                                            op0=ALU.mult, op1=ALU.add)
                            p.inc(li, "cv")
                        p.edge(v, "cv")
                        li = None
                        for t_i in ts_grp:
                            base = t_i * XS
                            li = v.tensor_copy(out=xi[:, base:base + 3],
                                               in_=xi[:, base + TC:base + TC + 3])
                        p.inc(li, "cv")

                @blk.sync
                def _(sy, j=j, csb=csb):
                    for oo in range(DT_I):
                        p.wait(sy, "cs", csb + 2 + 16 + oo + 1)
                        p.inc(sy.dma_start(out=sres_d[:, oo * T + j * TC: oo * T + j * TC + TC],
                                           in_=stg[oo % 4][:]), f"qt{oo % 4}", 16)
                    for t_i in range(DT_I):
                        p.wait(sy, "cs", csb + 2 + 32 + t_i + 1)
                        p.inc(sy.dma_start(out=xi2_d[:, t_i * T + j * TC: t_i * T + j * TC + TC],
                                           in_=stg[t_i % 4][:]), f"qt{t_i % 4}", 16)

        TP = 256
        NC2 = T // TP
        with contextlib.ExitStack() as es2:
            sb2 = lambda name, shape: es2.enter_context(nc.sbuf_tensor(name, shape, F32))
            wxb = [sb2(f"wx{i}", [128, 12 * 128]) for i in range(2)]
            bcb = [sb2(f"p2_bcb{i}", [128, 1, 2 * TP]) for i in range(2)]
            xi2 = sb2("p2_xi2", [128, DT_I * TP])
            sdel = [sb2(f"p2_del{i}", [128, DT_I * TP]) for i in range(2)]
            sdtx = [sb2(f"p2_dtx{i}", [128, DT_I * TP]) for i in range(2)]
            sy_ = sb2("p2_y", [128, DT_I * TP])
            srs = sb2("p2_srs", [128, DT_I * TP])
            h_all = sb2("p2_h", [128, 8 * TP])
            da = sb2("p2_da", [128, DT_I * TP])
            sp12 = sb2("p2_sp", [128, 17 * TP])
            dbx = [sb2(f"p2_dbx{i}", [128, 4 * TP]) for i in range(2)]
            zt = [sb2(f"p2_z{i}", [128, 4 * TP]) for i in range(2)]
            bc = sb2("p2_bc", [32, TP])

            NO = (DI + 2 * NS + 127) // 128
            GRPS = [list(range(0, 12)), list(range(12, 17))]

            def da_off(jj, n):
                if jj == NC2 - 1:
                    return 16 * n
                return 16 * n + (0 if n < 12 else 3 if n < 14 else 6)

            def emit_dA_part(jj, ns, csA, cvb_jj, qbs_jj, prevcv):
                @blk.scalar
                def _(s, jj=jj, ns=ns, csA=csA, cvb_jj=cvb_jj, qbs_jj=qbs_jj,
                      prevcv=prevcv):
                    cvs0 = cvb_jj + 5
                    if ns[0] == 0:
                        p.wait(s, "cv", cvb_jj + 4)
                        p.wait(s, "qbs", qbs_jj)
                    for n in ns:
                        if n >= 2:
                            p.wait(s, "cv", cvs0 + 16 * (n - 1))
                        elif prevcv is not None:
                            p.wait(s, "cv", prevcv + 5 + 16 * (15 + n))
                        p.inc(s.dma_start(
                            out=bcb[n % 2][:, 0, 0:TP],
                            in_=bcd_d[n, jj * TP:(jj + 1) * TP].partition_broadcast(128)),
                            f"qb{n % 2}", 16)
                        p.inc(s.dma_start(
                            out=bcb[n % 2][:, 0, TP:2 * TP],
                            in_=bcd_d[NS + n, jj * TP:(jj + 1) * TP].partition_broadcast(128)),
                            f"qb{n % 2}", 16)
                        for t_i in range(DT_I):
                            if n > 0:
                                p.wait(s, "cs", csA + da_off(jj, n - 1) + t_i + 1)
                                p.wait(s, "cv", cvs0 + 16 * (n - 1) + 4 * (t_i // 4) + 2)
                            li = s.activation(out=da[:, t_i * TP:(t_i + 1) * TP],
                                              in_=sdel[jj % 2][:, t_i * TP:(t_i + 1) * TP],
                                              func=AF.Exp,
                                              scale=ca[:, t_i * NS + n:t_i * NS + n + 1])
                            p.inc(li, "cs")

            prev = None
            prev_qbs = 0
            prevcv_prev = None
            for j in range(NC2):
                ctb = p.c["ct"]
                csb = p.c["cs"]
                cvb = p.c["cv"]

                @blk.sync
                def _(sy, j=j, prev=prev):
                    if j == 0:
                        p.wait(sy, "qs", p.c["qs"])
                        p.wait(sy, "qt0", p.c["qt0"])
                        p.wait(sy, "qt1", p.c["qt1"])
                        p.wait(sy, "qt2", p.c["qt2"])
                        p.wait(sy, "qt3", p.c["qt3"])
                        p.wait(sy, "cv", p.c["cv"])
                        p.wait(sy, "cs", p.c["cs"])
                        p.wait(sy, "ct", p.c["ct"])
                    else:
                        (pq, pct, pcs, pcv) = prev
                        p.wait(sy, "cv", pcv + 5)
                        p.wait(sy, "ct", pct + 32)
                    for t_i in range(DT_I):
                        p.inc(sy.dma_start(out=xi2[:, t_i * TP:(t_i + 1) * TP],
                                           in_=xi2_d[:, t_i * T + j * TP: t_i * T + j * TP + TP]),
                              "qs", 16)
                qs_xi2 = p.c["qs"]


                qwb = (p.c["qw0"], p.c["qw1"])

                @blk.sync
                def _(sy, j=j, ctb=ctb):
                    for g in range(2):
                        c0 = 0 if g == 0 else 12 * 128
                        cw = 12 * 128 if g == 0 else (DI + 2 * NS - 12 * 128)
                        for k in range(DT_I):
                            p.wait(sy, "ct", ctb + 16 * g + k - 1)
                            p.inc(sy.dma_start(out=wxb[k % 2][:, 0:cw],
                                               in_=wxT_d[k * 128:(k + 1) * 128, c0:c0 + cw]),
                                  f"qw{k % 2}", 16)

                @blk.tensor
                def _(t, j=j, prev=prev, ctb=ctb, csb=csb, cvb=cvb, qwb=qwb,
                      qs_xi2=qs_xi2):
                    for g, grp in enumerate(GRPS):
                        if g == 0:
                            p.wait(t, "qs", qs_xi2)
                            if prev is not None:
                                p.wait(t, "cs", prev[2] + (228 if j >= 2 else 4))
                        else:
                            p.wait(t, "cs", csb + (193 if j >= 1 else 1))
                        for k in range(DT_I):
                            p.wait(t, f"qw{k % 2}",
                                   qwb[k % 2] + 16 * (8 * g + k // 2 + 1))
                            li = None
                            for bi, o in enumerate(grp):
                                w = 32 if o == NO - 1 else 128
                                lc = bi * 128
                                bank = pb[bi // 2]
                                half = (bi % 2) * TP
                                dst = (bank[:, half:half + TP] if w == 128
                                       else bank[:32, half:half + TP])
                                st_ = (k == 0 and bi % 2 == 0)
                                sp_ = (k == DT_I - 1 and (bi % 2 == 1
                                                          or bi == len(grp) - 1))
                                li = t.matmul(dst, wxb[k % 2][:, lc:lc + w],
                                              xi2[:, k * TP:(k + 1) * TP],
                                              start=st_, stop=sp_)
                            p.inc(li, "ct")

                @blk.sync
                def _(sy, j=j, csb=csb):
                    p.wait(sy, "cs", csb + (230 if j >= 1 else 6))
                    p.inc(sy.dma_start(out=bcd_d[:, j * TP:(j + 1) * TP], in_=bc[:, 0:TP]),
                          "qbs", 16)
                qbs_j = p.c["qbs"]
                @blk.sync
                def _(sy, j=j, prev=prev):
                    if j >= 1:
                        p.wait(sy, "cv", prev[3] + 262)
                    else:
                        p.wait(sy, "qt0", p.c["qt0"])
                        p.wait(sy, "qt1", p.c["qt1"])
                    for t_i in range(DT_I):
                        p.inc(sy.dma_start(out=srs[:, t_i * TP:(t_i + 1) * TP],
                                           in_=sres_d[:, t_i * T + j * TP: t_i * T + j * TP + TP]),
                              "qr", 16)
                qt0_sres = p.c["qr"]

                if prev is not None:
                    @blk.sync
                    def _(sy, j=j, prev=prev):
                        p.wait(sy, "cv", prev[3] + 262)
                        jp = j - 1
                        for t_i in range(DT_I):
                            p.inc(sy.dma_start(
                                out=v_d[:, t_i * T + jp * TP: t_i * T + jp * TP + TP],
                                in_=sy_[:, t_i * TP:(t_i + 1) * TP]),
                                  "qv" if t_i < 8 else "qv2", 16)

                daB = csb
                if prev is not None:
                    emit_dA_part(j - 1, list(range(0, 12)), daB, prev[3],
                                 prev_qbs, prevcv_prev)

                def emit_sp(g, j=j, prev=prev, ctb=ctb, csb=csb, cvb=cvb,
                            pqbs=prev_qbs):
                    @blk.scalar
                    def _(s, g=g):
                        grp = GRPS[g]
                        if g == 0:
                            p.wait(s, "cs", csb)
                            if prev is not None:
                                p.wait(s, "cv", prev[3] + 3)

                        p.wait(s, "ct", ctb + 16 * (g + 1))
                        dts = [(bi + 12 * g, o) for bi, o in enumerate(grp) if o < DI // 128]
                        li = None
                        for sl, o in dts:
                            lb = sl - 12 * g
                            bank = pb[lb // 2]
                            half = (lb % 2) * TP
                            li = s.activation(out=sp12[:, sl * TP:(sl + 1) * TP],
                                              in_=bank[:, half:half + TP],
                                              func=AF.Abs, bias=cbxd[:, o:o + 1])
                        for sl, o in dts:
                            lb = sl - 12 * g
                            bank = pb[lb // 2]
                            half = (lb % 2) * TP
                            li = s.activation(out=sdel[j % 2][:, o * TP:(o + 1) * TP],
                                              in_=bank[:, half:half + TP],
                                              func=AF.Relu, bias=cbxd[:, o:o + 1])
                        p.inc(li, "cs")
                        p.edge(s, "cs")
                        li = None
                        for sl, o in dts:
                            li = s.activation(out=sp12[:, sl * TP:(sl + 1) * TP],
                                              in_=sp12[:, sl * TP:(sl + 1) * TP],
                                              func=AF.Exp, scale=-1.0)
                        p.inc(li, "cs")
                        p.edge(s, "cs")
                        li = None
                        for sl, o in dts:
                            li = s.activation(out=sp12[:, sl * TP:(sl + 1) * TP],
                                              in_=sp12[:, sl * TP:(sl + 1) * TP],
                                              func=AF.Ln, bias=1.0)
                        if g == 1:
                            p.wait(s, "qbs", pqbs)
                            li = s.activation(out=bc[:, 0:TP], in_=pb[2][:32, 0:TP],
                                              func=AF.Identity, bias=cbxbc[:, 0:1])
                        p.inc(li, "cs")

                emit_sp(0)
                if prev is not None:
                    emit_dA_part(j - 1, [12, 13], daB, prev[3], prev_qbs,
                                 prevcv_prev)
                emit_sp(1)
                if prev is not None:
                    emit_dA_part(j - 1, [14, 15], daB, prev[3], prev_qbs,
                                 prevcv_prev)
                qb_j = (p.c["qb0"], p.c["qb1"])

                @blk.vector
                def _(v, j=j, prev=prev, ctb=ctb, csb=csb, cvb=cvb, qb_j=qb_j,
                      qt0_sres=qt0_sres, qt1_vst=p.c["qv"], qt2_vst=p.c["qv2"]):
                    for g, grp in enumerate(GRPS):
                        spb = csb + (192 if j >= 1 else 0)
                        p.wait(v, "cs", spb + (3 if g == 0 else
                                               (38 if j >= 1 else 6)))
                        dts = [(bi + 12 * g, o) for bi, o in enumerate(grp) if o < DI // 128]
                        li = None
                        for sl, o in dts:
                            li = v.tensor_tensor(out=sdel[j % 2][:, o * TP:(o + 1) * TP],
                                                 in0=sdel[j % 2][:, o * TP:(o + 1) * TP],
                                                 in1=sp12[:, sl * TP:(sl + 1) * TP],
                                                 op=ALU.add)
                        p.inc(li, "cv")
                        p.edge(v, "cv")
                        li = None
                        for sl, o in dts:
                            li = v.tensor_tensor(out=sdtx[j % 2][:, o * TP:(o + 1) * TP],
                                                 in0=sdel[j % 2][:, o * TP:(o + 1) * TP],
                                                 in1=xi2[:, o * TP:(o + 1) * TP],
                                                 op=ALU.mult)
                        p.inc(li, "cv")
                    p.wait(v, "qv", qt1_vst)
                    p.edge(v, "cv")
                    li = None
                    for t_i in range(DT_I):
                        if t_i == 8:
                            p.wait(v, "qv2", qt2_vst)
                        li = v.tensor_scalar(out=sy_[:, t_i * TP:(t_i + 1) * TP],
                                             in0=xi2[:, t_i * TP:(t_i + 1) * TP],
                                             scalar1=cdv[:, t_i:t_i + 1],
                                             scalar2=None, op0=ALU.mult)
                    p.inc(li, "cv")
                    daB2 = csb + (6 if j == 0 else 262)
                    for n in range(NS):
                        p.wait(v, f"qb{n % 2}", qb_j[n % 2] + 32 * (n // 2 + 1))
                        bap_b = bcb[n % 2][:, 0:1, 0:TP].broadcast_to([128, 4, TP])
                        bap_c = bcb[n % 2][:, 0:1, TP:2 * TP].broadcast_to([128, 4, TP])
                        for tp_ in range(0, DT_I, 4):
                            qi = tp_ // 4
                            p.wait(v, "cs", daB2 + da_off(j, n) + tp_ + 4)
                            li = v.tensor_tensor(out=dbx[qi % 2][:],
                                                 in0=sdtx[j % 2][:, tp_ * TP:(tp_ + 4) * TP],
                                                 in1=bap_b, op=ALU.mult)
                            p.inc(li, "cv")
                            p.edge(v, "cv")
                            li = None
                            for t_i in range(tp_, tp_ + 4):
                                li = v.tensor_tensor_scan(
                                    out=h_all[:, (t_i % 8) * TP:(t_i % 8 + 1) * TP],
                                    data0=da[:, t_i * TP:(t_i + 1) * TP],
                                    data1=dbx[qi % 2][:, (t_i - tp_) * TP:(t_i - tp_ + 1) * TP],
                                    initial=hstate[:, t_i * NS + n:t_i * NS + n + 1],
                                    op0=ALU.mult, op1=ALU.add)
                            p.inc(li, "cv")
                            p.edge(v, "cv")
                            li = v.tensor_tensor(out=zt[qi % 2][:],
                                                 in0=h_all[:, (tp_ % 8) * TP:(tp_ % 8 + 4) * TP],
                                                 in1=bap_c, op=ALU.mult)
                            p.inc(li, "cv")
                            p.edge(v, "cv")
                            li = v.tensor_tensor(out=sy_[:, tp_ * TP:(tp_ + 4) * TP],
                                                 in0=sy_[:, tp_ * TP:(tp_ + 4) * TP],
                                                 in1=zt[qi % 2][:], op=ALU.add)
                            if tp_ % 8 == 4:
                                base = tp_ - 4
                                li = v.tensor_copy(
                                    out=hstate[:, base * NS + n:(tp_ + 4) * NS:NS],
                                    in_=h_all[:, TP - 1::TP])
                            p.inc(li, "cv")
                            p.edge(v, "cv")
                    p.wait(v, "qr", qt0_sres)
                    p.edge(v, "cv")
                    li = None
                    for t_i in range(DT_I):
                        li = v.tensor_tensor(out=sy_[:, t_i * TP:(t_i + 1) * TP],
                                             in0=sy_[:, t_i * TP:(t_i + 1) * TP],
                                             in1=srs[:, t_i * TP:(t_i + 1) * TP], op=ALU.mult)
                    p.inc(li, "cv")
                assert p.c["cv"] == cvb + 262, (p.c["cv"], cvb)


                prev_qbs = qbs_j
                prevcv_prev = prev[3] if prev is not None else None
                prev = (qs_xi2, ctb, csb, cvb)
                if j == NC2 - 1:
                    @blk.sync
                    def _(sy, j=j, cvb=cvb):
                        p.wait(sy, "cv", cvb + 262)
                        for t_i in range(DT_I):
                            p.inc(sy.dma_start(
                                out=v_d[:, t_i * T + j * TP: t_i * T + j * TP + TP],
                                in_=sy_[:, t_i * TP:(t_i + 1) * TP]),
                                  "qv" if t_i < 8 else "qv2", 16)
                    emit_dA_part(j, list(range(NS)), csb + (6 if j == 0 else 262),
                                 cvb, qbs_j, prevcv_prev)


        with contextlib.ExitStack() as es3:
            sb3 = lambda name, shape: es3.enter_context(nc.sbuf_tensor(name, shape, F32))
            wob = [sb3(f"wo{i}", [128, DM]) for i in range(2)]
            bf3 = lambda name, shape: es3.enter_context(nc.sbuf_tensor(name, shape, BF16))
            wobh = [bf3(f"woh{i}", [128, DM]) for i in range(2)]
            wobl = [bf3(f"wol{i}", [128, DM]) for i in range(2)]
            wt = [sb3(f"wt{k}", [128, DM]) for k in range(DT_M)]
            wtb = [bf3(f"wtb{k}", [128, DM]) for k in range(DT_M)]
            vv = sb3("p3_v", [128, DT_I * TC])
            vvh = bf3("p3_vh", [128, DT_I * TC])
            vvl = bf3("p3_vl", [128, DT_I * TC])
            x3 = sb3("p3_x", [128, NTT * DM])
            xnf3h = bf3("p3_xnfh", [128, DT_M * TC])
            xnf3l = bf3("p3_xnfl", [128, DT_M * TC])
            ss = sb3("p3_s", [128, DT_M * TC])
            sg = sb3("p3_g", [128, DT_M * TC])
            thr = sb3("p3_thr", [128, 2])
            red8 = sb3("p3_red", [128, DT_M + 2])

            @blk.sync
            def _(sy):
                p.wait(sy, "qs", p.c["qs"])
                p.wait(sy, "qt0", p.c["qt0"])
                p.wait(sy, "qt1", p.c["qt1"])
                p.wait(sy, "qt2", p.c["qt2"])
                p.wait(sy, "qt3", p.c["qt3"])
                p.wait(sy, "qv", p.c["qv"])
                p.wait(sy, "qv2", p.c["qv2"])
                p.wait(sy, "qr", p.c["qr"])
                p.wait(sy, "qb0", p.c["qb0"])
                p.wait(sy, "qb1", p.c["qb1"])
                p.wait(sy, "cv", p.c["cv"])
                p.wait(sy, "cs", p.c["cs"])
                p.wait(sy, "ct", p.c["ct"])
                for k in range(DT_M):
                    p.inc(sy.dma_start(out=wt[k][:], in_=wfT_d[k * 128:(k + 1) * 128, :]),
                          "qs", 16)
            qs_wf = p.c["qs"]

            @blk.vector
            def _(v):
                p.wait(v, "qs", qs_wf)
                li = None
                for k in range(DT_M):
                    li = v.tensor_reduce(out=red8[:, k:k + 1], in_=wt[k][:], axis=AX.X,
                                         op=ALU.add, apply_absolute_value=True)
                p.inc(li, "cv")
                p.edge(v, "cv")
                li = v.tensor_reduce(out=red8[:, DT_M:DT_M + 1], in_=red8[:, 0:DT_M], axis=AX.X,
                                     op=ALU.add)
                p.inc(li, "cv")
            cv_red = p.c["cv"]

            @blk.tensor
            def _(t):
                p.wait(t, "cv", cv_red)
                li = t.matmul(pb[0][:1, 0:1], red8[:, DT_M:DT_M + 1], ones_col[:],
                              start=True, stop=True)
                p.inc(li, "ct")
            ct_tot = p.c["ct"]

            @blk.scalar
            def _(s):
                p.wait(s, "ct", ct_tot)
                li = s.activation(out=thr[:1, 0:1], in_=pb[0][:1, 0:1], func=AF.Copy,
                                  scale=0.7 / float(DM * DM))
                p.inc(li, "cs")
            cs_thr = p.c["cs"]

            @blk.tensor
            def _(t):
                p.wait(t, "cs", cs_thr)
                li = t.matmul(pb[1][:, 0:1], ones_row[:1, :], thr[:1, 0:1], start=True, stop=True)
                p.inc(li, "ct")
            ct_thr = p.c["ct"]

            @blk.vector
            def _(v):
                p.wait(v, "ct", ct_thr)
                li = v.tensor_copy(out=thr[:, 0:1], in_=pb[1][:, 0:1])
                p.inc(li, "cv")
                p.edge(v, "cv")
                li = v.tensor_scalar(out=thr[:, 1:2], in0=thr[:, 0:1], scalar1=-1.0,
                                     scalar2=None, op0=ALU.mult)
                p.inc(li, "cv")
                p.edge(v, "cv")
                for kh in range(2):
                    ks = range(kh * 4, kh * 4 + 4)
                    li = None
                    for k in ks:
                        li = v.tensor_scalar(out=sg[:, (k % 4) * DM:(k % 4 + 1) * DM],
                                             in0=wt[k][:], scalar1=thr[:, 0:1],
                                             scalar2=None, op0=ALU.is_gt)
                    for k in ks:
                        li = v.tensor_scalar(out=ss[:, (k % 4) * DM:(k % 4 + 1) * DM],
                                             in0=wt[k][:], scalar1=thr[:, 1:2],
                                             scalar2=None, op0=ALU.is_lt)
                    p.inc(li, "cv")
                    p.edge(v, "cv")
                    li = None
                    for k in ks:
                        li = v.tensor_tensor(out=wt[k][:],
                                             in0=sg[:, (k % 4) * DM:(k % 4 + 1) * DM],
                                             in1=ss[:, (k % 4) * DM:(k % 4 + 1) * DM],
                                             op=ALU.subtract)
                    p.inc(li, "cv")
                    p.edge(v, "cv")
                    li = None
                    for k in ks:
                        li = v.tensor_copy(out=wtb[k][:], in_=wt[k][:])
                    p.inc(li, "cv")
                    p.edge(v, "cv")

            for j in range(NCH):
                cvb = p.c["cv"]
                ctb = p.c["ct"]
                csb = p.c["cs"]

                @blk.sync
                def _(sy, j=j):
                    p.wait(sy, "qs", p.c["qs"])
                    if j >= 1:
                        p.wait(sy, "cv", p.c["cv"])
                        p.wait(sy, "ct", p.c["ct"])
                    for t_i in range(DT_I):
                        p.inc(sy.dma_start(out=vv[:, t_i * TC:(t_i + 1) * TC],
                                           in_=v_d[:, t_i * T + j * TC: t_i * T + j * TC + TC]),
                              "qs", 16)
                    for tt in range(NTT):
                        p.inc(sy.dma_start(out=x3[:, tt * DM:(tt + 1) * DM],
                                           in_=x_d[j * TC + tt * 128: j * TC + (tt + 1) * 128, :]),
                              "qs", 16)
                qs_v = p.c["qs"]

                @blk.vector
                def _(v, j=j, qs_v=qs_v):
                    p.wait(v, "qs", qs_v)
                    li = None
                    for tt in range(NTT):
                        g = j * NTT + tt
                        li = v.tensor_scalar(out=x3[:, tt * DM:(tt + 1) * DM],
                                             in0=x3[:, tt * DM:(tt + 1) * DM],
                                             scalar1=mv_all[:, 2 * g:2 * g + 1],
                                             scalar2=rstd_all[:, g:g + 1],
                                             op0=ALU.subtract, op1=ALU.mult)
                    p.inc(li, "cv")

                qw_base3 = (p.c["qw0"], p.c["qw1"])
                czb = p.c["cz"]

                @blk.sync
                def _(sy, j=j, czb=czb):
                    for k in range(DT_I):
                        if k >= 2:
                            p.wait(sy, "cz", czb + 4 + 2 * (k - 2))
                        p.inc(sy.dma_start(out=wob[k % 2][:],
                                           in_=woT_d[k * 128:(k + 1) * 128, :]),
                              f"qw{k % 2}", 16)

                @blk.vector
                def _(v, j=j, qs_v=qs_v, qw_base3=qw_base3):
                    p.wait(v, "qs", qs_v)
                    li = None
                    for q_ in range(4):
                        li = v.tensor_copy(out=vvh[:, q_ * 4 * TC:(q_ + 1) * 4 * TC],
                                           in_=vv[:, q_ * 4 * TC:(q_ + 1) * 4 * TC])
                    p.inc(li, "cz")
                    p.edge(v, "cz")
                    li = None
                    for q_ in range(4):
                        li = v.tensor_tensor(out=vvl[:, q_ * 4 * TC:(q_ + 1) * 4 * TC],
                                             in0=vv[:, q_ * 4 * TC:(q_ + 1) * 4 * TC],
                                             in1=vvh[:, q_ * 4 * TC:(q_ + 1) * 4 * TC],
                                             op=ALU.subtract)
                    p.inc(li, "cz")
                    for k in range(2):
                        p.wait(v, f"qw{k % 2}", qw_base3[k % 2] + 16)
                        p.inc(v.tensor_copy(out=wobh[k % 2][:], in_=wob[k % 2][:]), "cz")
                        p.edge(v, "cz")
                        p.inc(v.tensor_tensor(out=wobl[k % 2][:], in0=wob[k % 2][:],
                                              in1=wobh[k % 2][:], op=ALU.subtract), "cz")

                @blk.tensor
                def _(t, j=j, cvb=cvb, csb=csb, czb=czb):
                    p.wait(t, "cv", cvb + 1)
                    for grp in range(DT_M):
                        if grp >= 2:
                            p.wait(t, "cv", cvb + 2 * grp - 1)
                        bank = pb[grp % 2]
                        li = None
                        for i in range(4):
                            idx = grp * 4 + i
                            tt, c = idx // DT_M, idx % DT_M
                            li = t.transpose(bank[:, i * 128:(i + 1) * 128],
                                             x3[:, tt * DM + c * 128: tt * DM + (c + 1) * 128],
                                             iden[:])
                        p.inc(li, "ct")
                    p.wait(t, "cv", cvb + 17)
                    for k in range(DT_I):
                        p.wait(t, "cz", czb + 4 + 2 * k)
                        li = None
                        for o in range(DT_M):
                            for wm, xm, tm in ((wobh, vvh, 0), (wobh, vvl, 1),
                                               (wobl, vvh, 2)):
                                li = t.matmul(pb[o][:, 0:TC],
                                              wm[k % 2][:, o * 128:(o + 1) * 128],
                                              xm[:, k * TC:(k + 1) * TC],
                                              start=(k == 0 and tm == 0),
                                              stop=(k == DT_I - 1 and tm == 2))
                        p.inc(li, "ct")
                    p.wait(t, "cs", csb + 1)
                    li = None
                    for o in range(DT_M):
                        for k in range(DT_M):
                            for xm, tm in ((xnf3h, 0), (xnf3l, 1)):
                                li = t.matmul(pb[o][:, 0:TC], wtb[k][:, o * 128:(o + 1) * 128],
                                              xm[:, k * TC:(k + 1) * TC],
                                              start=(k == 0 and tm == 0),
                                              stop=(k == DT_M - 1 and tm == 1))
                    p.inc(li, "ct")
                    p.wait(t, "cv", cvb + 18)
                    for grp in range(DT_M):
                        if grp >= 2:
                            p.wait(t, "cv", cvb + 18 + (grp - 1))
                        bank = pb[grp % 2]
                        li = None
                        for i in range(4):
                            idx = grp * 4 + i
                            c, tt = idx // NTT, idx % NTT
                            li = t.transpose(bank[:, i * 128:(i + 1) * 128],
                                             ss[:, c * TC + tt * 128: c * TC + (tt + 1) * 128],
                                             iden[:])
                        p.inc(li, "ct")

                @blk.vector
                def _(v, j=j, ctb=ctb):
                    for grp in range(DT_M):
                        p.wait(v, "ct", ctb + grp + 1)
                        bank = pb[grp % 2]
                        sls = []
                        for i in range(4):
                            idx = grp * 4 + i
                            tt, c = idx // DT_M, idx % DT_M
                            sls.append((slice(c * TC + tt * 128, c * TC + (tt + 1) * 128), i))
                        li = None
                        for sl, i in sls:
                            li = v.tensor_copy(out=xnf3h[:, sl],
                                               in_=bank[:, i * 128:(i + 1) * 128])
                        p.inc(li, "cv")
                        p.edge(v, "cv")
                        li = None
                        for sl, i in sls:
                            li = v.tensor_tensor(out=xnf3l[:, sl],
                                                 in0=bank[:, i * 128:(i + 1) * 128],
                                                 in1=xnf3h[:, sl], op=ALU.subtract)
                        p.inc(li, "cv")

                @blk.vector
                def _(v, j=j, ctb=ctb, czb=czb, qw_base3=qw_base3):
                    for k in range(2, DT_I):
                        p.wait(v, f"qw{k % 2}", qw_base3[k % 2] + 16 * (k // 2 + 1))
                        p.wait(v, "ct", ctb + 8 + (k - 2) + 1)
                        p.inc(v.tensor_copy(out=wobh[k % 2][:], in_=wob[k % 2][:]), "cz")
                        p.edge(v, "cz")
                        p.inc(v.tensor_tensor(out=wobl[k % 2][:], in0=wob[k % 2][:],
                                              in1=wobh[k % 2][:], op=ALU.subtract), "cz")

                @blk.scalar
                def _(s, j=j, ctb=ctb):
                    p.wait(s, "ct", ctb + 8 + 16)
                    li = None
                    for o in range(DT_M):
                        li = s.activation(out=ss[:, o * TC:(o + 1) * TC], in_=pb[o][:, 0:TC],
                                          func=AF.Identity, bias=cbout[:, o:o + 1])
                    p.inc(li, "cs")
                    p.wait(s, "ct", ctb + 8 + 16 + 1)
                    li = None
                    for o in range(DT_M):
                        li = s.activation(out=sg[:, o * TC:(o + 1) * TC], in_=pb[o][:, 0:TC],
                                          func=AF.Silu, bias=cbffn[:, o:o + 1], scale=alph[:, 0:1])
                    p.inc(li, "cs")

                @blk.vector
                def _(v, j=j, csb=csb):
                    p.wait(v, "cs", csb + 2)
                    li = None
                    for o in range(DT_M):
                        li = v.tensor_tensor(out=ss[:, o * TC:(o + 1) * TC],
                                             in0=ss[:, o * TC:(o + 1) * TC],
                                             in1=sg[:, o * TC:(o + 1) * TC], op=ALU.mult)
                    p.inc(li, "cv")

                @blk.sync
                def _(sy, j=j, ctb=ctb):
                    p.wait(sy, "ct", ctb + 8)
                    for tt in range(NTT):
                        p.inc(sy.dma_start(out=x3[:, tt * DM:(tt + 1) * DM],
                                           in_=x_d[j * TC + tt * 128: j * TC + (tt + 1) * 128, :]),
                              "qs", 16)
                qs_x3 = p.c["qs"]

                @blk.vector
                def _(v, j=j, ctb=ctb, qs_x3=qs_x3):
                    p.wait(v, "qs", qs_x3)
                    for grp in range(DT_M):
                        p.wait(v, "ct", ctb + 8 + 16 + 1 + grp + 1)
                        bank = pb[grp % 2]
                        li = None
                        for i in range(4):
                            idx = grp * 4 + i
                            c, tt = idx // NTT, idx % NTT
                            li = v.tensor_tensor(out=x3[:, tt * DM + c * 128: tt * DM + (c + 1) * 128],
                                                 in0=x3[:, tt * DM + c * 128: tt * DM + (c + 1) * 128],
                                                 in1=bank[:, i * 128:(i + 1) * 128], op=ALU.add)
                        p.inc(li, "cv")

                @blk.sync
                def _(sy, j=j, cvb=cvb):
                    p.wait(sy, "cv", cvb + 26)
                    for tt in range(NTT):
                        p.inc(sy.dma_start(out=out_d[j * TC + tt * 128: j * TC + (tt + 1) * 128, :],
                                           in_=x3[:, tt * DM:(tt + 1) * DM]), "qs", 16)

        @blk.gpsimd
        def _(g):
            p.wait(g, "qs", p.c["qs"])
            p.wait(g, "qg", p.c["qg"])
            p.wait(g, "qt0", p.c["qt0"])
            p.wait(g, "qt1", p.c["qt1"])
            p.wait(g, "qt2", p.c["qt2"])
            p.wait(g, "qt3", p.c["qt3"])
            p.wait(g, "qb0", p.c["qb0"])
            p.wait(g, "qb1", p.c["qb1"])
            p.wait(g, "qbs", p.c["qbs"])
            p.wait(g, "qr", p.c["qr"])
            p.wait(g, "qv", p.c["qv"])
            p.wait(g, "qv2", p.c["qv2"])
            p.wait(g, "qh0", p.c["qh0"])
            p.wait(g, "qh1", p.c["qh1"])

    return nc


def kernel(L0_attn_x, L0_attn_W_in, L0_attn_b_in, L0_attn_W_conv, L0_attn_b_conv,
           L0_attn_W_x, L0_attn_b_x, L0_attn_W_out, L0_attn_b_out, L0_attn_A_log,
           L0_attn_D, L0_ffn_weight, L0_ffn_alpha, L0_ffn_bias):
    f32 = np.float32
    c_ = np.ascontiguousarray
    x = c_(np.asarray(L0_attn_x, f32))

    winT = c_(np.asarray(L0_attn_W_in, f32).T)
    wxT = c_(np.asarray(L0_attn_W_x, f32).T)
    woT = c_(np.asarray(L0_attn_W_out, f32).T)
    wfT = c_(np.asarray(L0_ffn_weight, f32).T)
    b_in = np.asarray(L0_attn_b_in, f32)
    b_x = np.asarray(L0_attn_b_x, f32)

    def col128(vec, nt):
        return c_(vec.reshape(nt, 128).T)

    cb_in_xi = col128(b_in[:DI], DT_I)
    cb_in_res = col128(b_in[DI:], DT_I)
    cw_conv = c_(np.asarray(L0_attn_W_conv, f32)[:, 0, :].reshape(DT_I, 128, 4)
                 .transpose(1, 0, 2).reshape(128, DT_I * 4))
    cb_conv = col128(np.asarray(L0_attn_b_conv, f32), DT_I)
    cb_x_d = col128(b_x[:DI], DT_I)
    cb_x_bc = c_(b_x[DI:].reshape(32, 1))
    cb_out = col128(np.asarray(L0_attn_b_out, f32), DT_M)
    c_d = col128(np.asarray(L0_attn_D, f32), DT_I)
    c_a_log = c_(np.asarray(L0_attn_A_log, f32).reshape(DT_I, 128, NS)
                 .transpose(1, 0, 2).reshape(128, DT_I * NS))
    c_alpha = c_(np.broadcast_to(np.asarray(L0_ffn_alpha, f32).reshape(1, 1), (128, 1)))
    cb_ffn = col128(np.asarray(L0_ffn_bias, f32), DT_M)
    iden = np.eye(128, dtype=f32)
    zeros3 = np.zeros((3, DM), f32)
    zmask = np.zeros((128, 1), f32)

    shared = dict(w_in_t=winT, w_x_t=wxT, w_out_t=woT, w_ffn_t=wfT,
                  cb_in_xi=cb_in_xi, cb_in_res=cb_in_res, cw_conv=cw_conv,
                  cb_conv=cb_conv, cb_x_d=cb_x_d, cb_x_bc=cb_x_bc, cb_out=cb_out,
                  c_d=c_d, c_a_log=c_a_log, c_alpha=c_alpha, cb_ffn=cb_ffn, iden=iden)

    in_maps = []
    for c in range(8):
        seq = c % 4
        m = dict(shared)
        m["x"] = c_(x[seq])
        m["x_halo3"] = zeros3
        m["halo_mask"] = zmask
        in_maps.append(m)

    nc = build_nc()
    res = run_bass_kernel_spmd(nc, in_maps, core_ids=list(range(8)))
    out = np.empty((4, 4096, DM), f32)
    for seq in range(4):
        out[seq] = res.results[seq]["out"]
    return out
